# version 28
# baseline (speedup 1.0000x reference)
import contextlib
import numpy as np
import ml_dtypes
import concourse.bass as bass
import concourse.mybir as mybir
from concourse.bass_utils import run_bass_kernel_spmd

F32 = mybir.dt.float32
BF16 = mybir.dt.bfloat16
I32 = mybir.dt.int32
AF = mybir.ActivationFunctionType
ALU = mybir.AluOpType
AX = mybir.AxisListType

T = 2048
HALO = 3
TH = T + HALO
TPAD = 2064
D = 1024
DI = 2048
NCH = 16
NDC = 8

ENGS = ("sync", "gpsimd", "scalar", "vector", "tensor")


def build_nc(debug=False):
    nc = bass.Bass("TRN2", target_bir_lowering=False)

    d_x = nc.dram_tensor("x", [TPAD, D], F32, kind="ExternalInput").ap()
    d_win = nc.dram_tensor("win", [32, 128, 1024], F32, kind="ExternalInput").ap()
    d_dtw = nc.dram_tensor("dtw", [16, 128, 2048], F32, kind="ExternalInput").ap()
    d_bw = nc.dram_tensor("bw", [16, 128, 2048], F32, kind="ExternalInput").ap()
    d_cw = nc.dram_tensor("cw", [16, 128, 2048], F32, kind="ExternalInput").ap()
    d_ow = nc.dram_tensor("ow", [128, 16384], F32, kind="ExternalInput").ap()
    d_fw = nc.dram_tensor("fw", [1024, 1024], F32, kind="ExternalInput").ap()
    d_inb = nc.dram_tensor("inb", [128, 32], F32, kind="ExternalInput").ap()
    d_cwv = nc.dram_tensor("cwv", [128, 64], F32, kind="ExternalInput").ap()
    d_cvb = nc.dram_tensor("cvb", [128, 16], F32, kind="ExternalInput").ap()
    d_dtb = nc.dram_tensor("dtb", [128, 16], F32, kind="ExternalInput").ap()
    d_alog = nc.dram_tensor("alog", [128, 16], F32, kind="ExternalInput").ap()
    d_dpar = nc.dram_tensor("dpar", [128, 16], F32, kind="ExternalInput").ap()
    d_outb = nc.dram_tensor("outb", [128, 1024], F32, kind="ExternalInput").ap()
    d_ffnb = nc.dram_tensor("ffnb", [128, 1024], F32, kind="ExternalInput").ap()
    d_eye = nc.dram_tensor("eye", [128, 128], BF16, kind="ExternalInput").ap()
    d_halo = nc.dram_tensor("halo", [128, 1], F32, kind="ExternalInput").ap()
    d_out = nc.dram_tensor("y_out", [T, D], F32, kind="ExternalOutput").ap()

    dk = "Internal"
    if debug:
        d_dbg = nc.dram_tensor("dbg0", [4, 128, 1024], F32, kind="ExternalOutput").ap()
        d_dbg2 = nc.dram_tensor("dbg1", [128, 2048], F32, kind="ExternalOutput").ap()
        d_dbg3 = nc.dram_tensor("dbg2", [128, 1024], F32, kind="ExternalOutput").ap()
    s_xbf = nc.dram_tensor("s_xbf", [TPAD, D], BF16, kind="Internal").ap()
    s_u32 = nc.dram_tensor("s_u32", [NCH, 128, T], F32, kind=dk).ap()
    s_g = nc.dram_tensor("s_g", [NCH, 128, T], F32, kind=dk).ap()
    s_ybf = nc.dram_tensor("s_ybf", [NCH, 128, T], BF16, kind=dk).ap()
    s_wd = nc.dram_tensor("s_wd", [128, 8192], BF16, kind=dk).ap()
    d_ccin = [nc.dram_tensor(f"ccin{c}", [128, 1], F32, kind="Internal").ap() for c in range(NCH)]
    d_ccout = [nc.dram_tensor(f"ccout{c}", [128, 1], F32, kind="Internal").ap() for c in range(NCH)]

    es = contextlib.ExitStack()

    sem = {e: es.enter_context(nc.semaphore(f"sm_{e}")) for e in ENGS}
    ccsem = [es.enter_context(nc.semaphore(f"cc{c}")) for c in range(NCH)]
    cnt = {e: 0 for e in ENGS}
    streams = {e: [] for e in ENGS}

    def emit(eng, fn):
        streams[eng].append(fn)

    def alloc(eng, amt=1):
        cnt[eng] += amt
        return (eng, cnt[eng])

    slot_sems = {}
    slot_gen = {}

    def slot_tok(name):
        return ("S", name, slot_gen.get(name, 0) * 16)

    def mkwait(toks):
        merged = {}
        for t_ in toks:
            if t_ is None:
                continue
            if t_[0] == "S":
                _, nm, v = t_
                key = ("S", nm)
            else:
                key = t_[0]
                v = t_[1]
            merged[key] = max(merged.get(key, 0), v)
        return tuple(merged.items())

    def do_wait(q, merged):
        for key, v in merged:
            if isinstance(key, tuple):
                q.wait_ge(slot_sems[key[1]], v)
            else:
                q.wait_ge(sem[key], v)

    def barrier_now():
        return [(e, cnt[e]) for e in ENGS] + [slot_tok(s) for s in slot_gen]

    t_consts = es.enter_context(nc.sbuf_tensor("cst", [128, 4], F32))
    t_inb = es.enter_context(nc.sbuf_tensor("sb_inb", [128, 32], F32))
    t_cwv = es.enter_context(nc.sbuf_tensor("sb_cwv", [128, 64], F32))
    t_cvb = es.enter_context(nc.sbuf_tensor("sb_cvb", [128, 16], F32))
    t_dtb = es.enter_context(nc.sbuf_tensor("sb_dtb", [128, 16], F32))
    t_negA = es.enter_context(nc.sbuf_tensor("negA", [128, 16], F32))
    t_negAe = es.enter_context(nc.sbuf_tensor("negAe", [128, 16], F32))
    t_dpar = es.enter_context(nc.sbuf_tensor("sb_dpar", [128, 16], F32))
    t_eye = es.enter_context(nc.sbuf_tensor("sb_eye", [128, 128], BF16))
    t_halo = es.enter_context(nc.sbuf_tensor("sb_halo", [128, 1], F32))
    t_evm = es.enter_context(nc.sbuf_tensor("evm", [128, 1], F32))

    nc.const_aps.register(F32, -1.0, t_consts[:, 0:1])
    nc.const_aps.register(F32, 1e-5, t_consts[:, 1:2])
    nc.const_aps.register(F32, -0.5, t_consts[:, 2:3])

    pbank = [es.enter_context(nc.psum_tensor(f"pb{i}", [128, 512], F32)) for i in range(6)]
    p_bf = es.enter_context(nc.psum_tensor("pbf", [128, 1024], BF16))
    bank_w = [None] * 6
    bank_r = [[] for _ in range(6)]

    def dma(eng, out_ap, in_ap, waits=(), slot=None, transpose=False):
        assert slot is not None
        w = mkwait(waits)
        if slot not in slot_sems:
            slot_sems[slot] = es.enter_context(nc.semaphore(f"ds_{slot}"))
            slot_gen[slot] = 0
        slot_gen[slot] += 1
        ssem = slot_sems[slot]
        tok = ("S", slot, slot_gen[slot] * 16)

        def body(q, w=w, out_ap=out_ap, in_ap=in_ap, ssem=ssem, transpose=transpose):
            do_wait(q, w)
            if transpose:
                q.dma_start_transpose(out_ap, in_ap).then_inc(ssem, 16)
            else:
                q.dma_start(out=out_ap, in_=in_ap).then_inc(ssem, 16)
        emit(eng, body)
        return tok

    def mm_group(lhsTs, rhss, bk, waits=(), psum_ap=None):
        w = mkwait(list(waits) + bank_r[bk])
        bank_r[bk] = []
        tok = alloc("tensor")
        out_ap = psum_ap if psum_ap is not None else pbank[bk][:, :]

        def body(t, w=w, lhsTs=tuple(lhsTs), rhss=tuple(rhss), out_ap=out_ap):
            do_wait(t, w)
            n = len(lhsTs)
            for i in range(n):
                ins = t.matmul(out_ap, lhsTs[i], rhss[i], start=(i == 0), stop=(i == n - 1))
            ins.then_inc(sem["tensor"], 1)
        emit("tensor", body)
        bank_w[bk] = tok
        return tok

    def op(eng, builder, waits=(), selfwait=False, amt=1):
        w = mkwait(waits)
        tok = alloc(eng, amt)
        val = cnt[eng]

        def body(q, w=w, builder=builder, val=val, amt=amt, selfwait=selfwait):
            do_wait(q, w)
            builder(q).then_inc(sem[eng], amt)
            if selfwait:
                q.wait_ge(sem[eng], val)
        emit(eng, body)
        return tok

    def pre_g(g):
        g.memset(t_consts[:, 0:1], -1.0)
        g.memset(t_consts[:, 1:2], 1e-5)
        g.memset(t_consts[:, 2:3], -0.5).then_inc(sem["gpsimd"], 1)
    emit("gpsimd", pre_g)
    tok_consts = alloc("gpsimd")

    tok_smalls = None
    for dst, src in ((t_inb, d_inb), (t_cwv, d_cwv), (t_cvb, d_cvb), (t_dtb, d_dtb),
                     (t_negA, d_alog), (t_dpar, d_dpar), (t_eye, d_eye), (t_halo, d_halo)):
        tok_smalls = dma("sync", dst[:, :], src, slot="smalls")

    tok_nA_s = op("scalar", lambda s: s.activation(t_negA[:, :], t_negA[:, :], AF.Exp),
                  waits=[tok_smalls], selfwait=True)
    op("vector", lambda v: v.tensor_scalar(t_negA[:, :], t_negA[:, :], -1.0, None, ALU.mult),
       waits=[tok_nA_s], selfwait=True)
    op("vector", lambda v: v.tensor_scalar(t_negAe[:, :], t_negA[:, :], 1e-4, None, ALU.mult),
       selfwait=True)
    tok_negA = op("vector", lambda v: v.tensor_scalar(t_evm[:, :], t_halo[:, :], -1.0, 1.0,
                                                      ALU.mult, ALU.add), selfwait=True)

    tok_xbf = dma("gpsimd", s_xbf, d_x, slot="xcast")
    abank = 0

    esA = contextlib.ExitStack()
    u_bf = es.enter_context(nc.sbuf_tensor("u_bf", [128, NCH, T], BF16))
    ysl01 = [es.enter_context(nc.sbuf_tensor("ysl01_0", [128, NCH, 128], BF16))]
    x_fm = esA.enter_context(nc.sbuf_tensor("x_fm", [128, NDC, TPAD], BF16))
    a_slab = [esA.enter_context(nc.sbuf_tensor(f"aslab{i}", [128, NDC, 128], BF16)) for i in range(2)]
    t_xbr = esA.enter_context(nc.sbuf_tensor("xbr", [128, TH], F32))
    t_acc = esA.enter_context(nc.sbuf_tensor("cacc", [128, T], F32))
    u32b = [esA.enter_context(nc.sbuf_tensor(f"u32b{i}", [128, T], F32)) for i in range(2)]
    gb = [esA.enter_context(nc.sbuf_tensor(f"gb{i}", [128, T], F32)) for i in range(2)]
    wq_bf = esA.enter_context(nc.sbuf_tensor("wq_bf", [128, NDC, 1024], BF16))
    wqf = [esA.enter_context(nc.sbuf_tensor(f"wqf{i}", [128, 1024], F32)) for i in range(2)]
    a_q1 = esA.enter_context(nc.sbuf_tensor("wq1", [128, 1024], F32))
    a_q2 = esA.enter_context(nc.sbuf_tensor("wq2", [128, 1024], F32))
    a_q3 = esA.enter_context(nc.sbuf_tensor("wq3", [128, 1024], F32))
    a_q4 = esA.enter_context(nc.sbuf_tensor("wq4", [128, 1024], F32))
    a_qi = esA.enter_context(nc.sbuf_tensor("wqi", [128, 1024], I32))
    a_qsc = esA.enter_context(nc.sbuf_tensor("wqsc", [128, 32], F32))
    a_qrs = esA.enter_context(nc.sbuf_tensor("wqrs", [128, 32], F32))
    wdst = esA.enter_context(nc.sbuf_tensor("wdst", [128, 1024], BF16))

    def make_quant(q1, q2, q3, q4, qi, qsc, qrs, eng="vector", qsp=None):
        if eng == "split":
            def emit_split(src_ap, out_bf_ap, waits):
                src3 = src_ap.rearrange("p (n m) -> p n m", m=32)
                rs_b = qrs[:, :].unsqueeze(2).broadcast_to([128, 32, 32])
                sc_b = qsc[:, :].unsqueeze(2).broadcast_to([128, 32, 32])
                op("vector", lambda v: v.reduce_max(qsc[:, :], src3, axis=AX.X,
                                                    apply_absolute_value=True),
                   waits=waits, selfwait=True)
                op("vector", lambda v: v.tensor_scalar(qsc[:, :], qsc[:, :], 1.0 / 7.5, 1e-30,
                                                       ALU.mult, ALU.max), selfwait=True)
                tv1 = op("vector", lambda v: v.reciprocal(qrs[:, :], qsc[:, :]), selfwait=True)
                op("gpsimd", lambda g: g.tensor_tensor(
                    q1[:, :].rearrange("p (n m) -> p n m", m=32), src3, rs_b, ALU.mult),
                   waits=[tv1], selfwait=True)
                tg1 = op("gpsimd", lambda g: g.tensor_mul(q2[:, :], q1[:, :], q1[:, :]),
                         selfwait=True)
                op("vector", lambda v: v.tensor_scalar(q3[:, :], q2[:, :], 4.0, None, ALU.is_ge),
                   waits=[tg1], selfwait=True)
                tv2 = op("vector", lambda v: v.tensor_scalar(q2[:, :], q2[:, :], 16.0, None,
                                                             ALU.is_ge), selfwait=True)
                op("gpsimd", lambda g: g.tensor_scalar(q4[:, :], q3[:, :], 0.125, 0.125,
                                                       ALU.mult, ALU.add),
                   waits=[tv2], selfwait=True)
                op("gpsimd", lambda g: g.tensor_scalar(qsp[:, :], q2[:, :], 0.25, None, ALU.mult),
                   selfwait=True)
                op("gpsimd", lambda g: g.tensor_add(q4[:, :], q4[:, :], qsp[:, :]), selfwait=True)
                op("gpsimd", lambda g: g.tensor_scalar(q3[:, :], q3[:, :], -4.0, 8.0, ALU.mult,
                                                       ALU.add), selfwait=True)
                op("gpsimd", lambda g: g.tensor_scalar(qsp[:, :], q2[:, :], -2.0, None, ALU.mult),
                   selfwait=True)
                op("gpsimd", lambda g: g.tensor_add(q3[:, :], q3[:, :], qsp[:, :]), selfwait=True)
                op("gpsimd", lambda g: g.tensor_mul(q1[:, :], q1[:, :], q3[:, :]), selfwait=True)
                op("gpsimd", lambda g: g.tensor_scalar(q1[:, :], q1[:, :], -1.0, 0.5, ALU.mult,
                                                       ALU.add), selfwait=True)
                op("gpsimd", lambda g: g.tensor_copy(qi[:, :], q1[:, :]), selfwait=True)
                tg2 = op("gpsimd", lambda g: g.tensor_copy(q2[:, :], qi[:, :]), selfwait=True)
                tv3 = op("vector", lambda v: v.tensor_tensor(q3[:, :], q2[:, :], q1[:, :],
                                                             ALU.is_gt),
                         waits=[tg2], selfwait=True)
                op("gpsimd", lambda g: g.tensor_sub(q2[:, :], q2[:, :], q3[:, :]),
                   waits=[tv3], selfwait=True)
                op("gpsimd", lambda g: g.tensor_mul(q2[:, :], q2[:, :], q4[:, :]), selfwait=True)
                op("gpsimd", lambda g: g.tensor_scalar(q2[:, :], q2[:, :], -1.0, None, ALU.mult),
                   selfwait=True)
                op("gpsimd", lambda g: g.tensor_tensor(
                    q2[:, :].rearrange("p (n m) -> p n m", m=32),
                    q2[:, :].rearrange("p (n m) -> p n m", m=32), sc_b, ALU.mult), selfwait=True)
                return op("gpsimd", lambda g: g.tensor_copy(out_bf_ap, q2[:, :]), selfwait=True)
            return emit_split

        def emit_quant(src_ap, out_bf_ap, waits, sink=None):
            holder = {}
            steps = []

            def op_q(eng_, builder, waits_=(), selfwait=True):
                steps.append(lambda: op(eng_, builder, waits=waits_, selfwait=selfwait))

            src3 = src_ap.rearrange("p (n m) -> p n m", m=32)
            rs_b = qrs[:, :].unsqueeze(2).broadcast_to([128, 32, 32])
            sc_b = qsc[:, :].unsqueeze(2).broadcast_to([128, 32, 32])
            op_q("vector", lambda v: v.reduce_max(qsc[:, :], src3, axis=AX.X,
                                                  apply_absolute_value=True),
                 waits_=waits)
            op_q("vector", lambda v: v.tensor_scalar(qsc[:, :], qsc[:, :], 1.0 / 7.5, 1e-30,
                                                     ALU.mult, ALU.max))
            op_q("vector", lambda v: v.reciprocal(qrs[:, :], qsc[:, :]))
            op_q(eng, lambda e: e.tensor_tensor(q1[:, :].rearrange("p (n m) -> p n m", m=32),
                                                src3, rs_b, ALU.mult))
            op_q(eng, lambda e: e.tensor_mul(q2[:, :], q1[:, :], q1[:, :]))
            op_q(eng, lambda e: e.tensor_scalar(q3[:, :], q2[:, :], 4.0, None, ALU.is_ge))
            op_q(eng, lambda e: e.tensor_scalar(q2[:, :], q2[:, :], 16.0, None, ALU.is_ge))
            op_q(eng, lambda e: e.tensor_scalar(q4[:, :], q3[:, :], 0.125, 0.125, ALU.mult,
                                                ALU.add))
            op_q(eng, lambda e: e.scalar_tensor_tensor(q4[:, :], q2[:, :], 0.25, q4[:, :],
                                                       ALU.mult, ALU.add))
            op_q(eng, lambda e: e.tensor_scalar(q3[:, :], q3[:, :], -4.0, 8.0, ALU.mult,
                                                ALU.add))
            op_q(eng, lambda e: e.scalar_tensor_tensor(q3[:, :], q2[:, :], -2.0, q3[:, :],
                                                       ALU.mult, ALU.add))
            op_q(eng, lambda e: e.tensor_mul(q1[:, :], q1[:, :], q3[:, :]))
            op_q(eng, lambda e: e.tensor_scalar(q1[:, :], q1[:, :], -1.0, 0.5, ALU.mult,
                                                ALU.add))
            op_q(eng, lambda e: e.tensor_copy(qi[:, :], q1[:, :]))
            op_q(eng, lambda e: e.tensor_copy(q2[:, :], qi[:, :]))
            op_q(eng, lambda e: e.tensor_tensor(q3[:, :], q2[:, :], q1[:, :], ALU.is_gt))
            op_q(eng, lambda e: e.tensor_sub(q2[:, :], q2[:, :], q3[:, :]))
            op_q(eng, lambda e: e.tensor_mul(q2[:, :], q2[:, :], q4[:, :]))
            op_q(eng, lambda e: e.scalar_tensor_tensor(
                q2[:, :].rearrange("p (n m) -> p n m", m=32),
                q2[:, :].rearrange("p (n m) -> p n m", m=32), -1.0, sc_b,
                ALU.mult, ALU.mult))

            def _final():
                holder["tok"] = op(eng, lambda e: e.tensor_copy(out_bf_ap, q2[:, :]),
                                   selfwait=True)
            steps.append(_final)
            if sink is None:
                for s_ in steps:
                    s_()
                return holder["tok"]
            sink.extend(steps)
            return holder
        return emit_quant

    quant_g = make_quant(a_q1, a_q2, a_q3, a_q4, a_qi, a_qsc, a_qrs)

    tok_xfm = None
    for c in range(NDC):
        tok_xfm = dma("scalar", x_fm[:, c, :], s_xbf[:, c * 128:(c + 1) * 128],
                      waits=[tok_xbf], slot="xfm", transpose=True)

    XB_TILES = [(0, 512), (512, 512), (1024, 512), (1536, 512), (2048, 3)]
    Z_TILES = [(3, 512), (515, 512), (1027, 512), (1539, 512)]

    a_slab_free = [[], []]
    u32_free = [[], []]
    g_free = [[], []]
    xbr_free = []
    acc_free = []
    ubf_last = None
    u32_wr = [None] * NCH
    g_wr = [None] * NCH
    wqf_free = [[], []]
    wq_holders = []
    wq_pending = []
    slab_tok_a = [None] * 32

    def emit_a_slab(c):
        sb = c % 2
        slab_tok_a[c] = dma("gpsimd",
                            a_slab[sb][:, :, :].rearrange("p s j -> p (s j)"),
                            d_win[c], waits=a_slab_free[sb] + [tok_consts],
                            slot=f"aslab{sb}")
        a_slab_free[sb] = []

    emit_a_slab(0)

    for c in range(32):
        sb = c % 2
        if c + 1 < 32:
            emit_a_slab(c + 1)
        tiles = XB_TILES if c < 16 else Z_TILES
        evac_toks = []
        for ti, (off, w) in enumerate(tiles):
            bk = abank % 6
            abank += 1
            mm_group([a_slab[sb][:, s_, :] for s_ in range(NDC)],
                     [x_fm[:, s_, off:off + w] for s_ in range(NDC)],
                     bk, waits=[slab_tok_a[c], tok_xfm],
                     psum_ap=pbank[bk][:, 0:w])
            if c < 16:
                etok = op("scalar", lambda s, bk=bk, off=off, w=w, c=c: s.activation(
                    t_xbr[:, off:off + w], pbank[bk][:, 0:w], AF.Identity,
                    bias=t_inb[:, c:c + 1]),
                    waits=[bank_w[bk]] + xbr_free)
            else:
                etok = op("scalar", lambda s, bk=bk, off=off, w=w, c=c, sb=sb: s.activation(
                    gb[sb][:, off - 3:off - 3 + w], pbank[bk][:, 0:w], AF.Sigmoid,
                    bias=t_inb[:, c:c + 1]),
                    waits=[bank_w[bk]] + g_free[sb])
            bank_r[bk].append(etok)
            evac_toks.append(etok)
        a_slab_free[sb].append(bank_w[(abank - 1) % 6])
        if c < 16:
            op("vector", lambda v: v.tensor_scalar(
                t_xbr[:, 0:HALO], t_xbr[:, 0:HALO], t_halo[:, 0:1], None, ALU.mult),
               waits=evac_toks + [tok_negA] + acc_free + u32_free[sb] + xbr_free, selfwait=True)
            xbr_free = []
            acc_free = []
            u32_free[sb] = []
            op("vector", lambda v, c=c: v.tensor_scalar(
                t_acc[:, :], t_xbr[:, 0:T], t_cwv[:, 4 * c:4 * c + 1], None, ALU.mult))
            for k in range(1, 4):
                op("vector", lambda v, c=c, k=k: v.scalar_tensor_tensor(
                    t_acc[:, :], t_xbr[:, k:k + T], t_cwv[:, 4 * c + k:4 * c + k + 1],
                    t_acc[:, :], ALU.mult, ALU.add),
                   selfwait=(k == 3))
            tok_conv = ("vector", cnt["vector"])
            xbr_free.append(tok_conv)
            tok_silu = op("scalar", lambda s, c=c, sb=sb: s.activation(
                u32b[sb][:, :], t_acc[:, :], AF.Silu, bias=t_cvb[:, c:c + 1]), waits=[tok_conv])
            acc_free.append(tok_silu)
            ubf_last = op("scalar", lambda s, c=c, sb=sb: s.copy(u_bf[:, c, :], u32b[sb][:, :]),
                          waits=[tok_silu])
            u32_free[sb].append(ubf_last)
            u32_wr[c] = dma("sync", s_u32[c], u32b[sb][:, :], waits=[ubf_last], slot=f"u32s{sb}")
            u32_free[sb].append(u32_wr[c])
        else:
            cz = c - 16
            g_wr[cz] = dma("sync", s_g[cz], gb[sb][:, :], waits=evac_toks, slot=f"gs{sb}")
            g_free[sb] = [g_wr[cz]]
            if cz % 2 == 0:
                i = cz // 2
                tok_wl = dma("sync", wqf[i % 2][:, :], d_fw[i * 128:(i + 1) * 128, :],
                             waits=wqf_free[i % 2], slot=f"wqf{i % 2}")
                wqf_free[i % 2] = []
                wq_holders.append({"tok": quant_g(wqf[i % 2][:, :], wq_bf[:, i, :],
                                                  [tok_wl])})
                wqf_free[i % 2] = [wq_holders[-1]["tok"]]

    while wq_pending:
        wq_pending.pop(0)()
    wq_toks = [h["tok"] for h in wq_holders]

    bf_rd = []
    wd_wr = None
    for j in range(NDC):
        w = mkwait(wq_toks + bf_rd + [tok_smalls])
        bf_rd = []
        tok_tp = alloc("tensor")

        def body(t, w=w, j=j):
            do_wait(t, w)
            for i in range(NDC):
                ins = t.transpose(p_bf[:, i * 128:(i + 1) * 128],
                                  wq_bf[:, i, j * 128:(j + 1) * 128], t_eye[:, :])
            ins.then_inc(sem["tensor"], 1)
        emit("tensor", body)
        tok_ev = op("scalar", lambda s: s.copy(wdst[:, :], p_bf[:, :]), waits=[tok_tp])
        bf_rd.append(tok_ev)
        wd_wr = dma("gpsimd", s_wd[:, j * 1024:(j + 1) * 1024], wdst[:, :], waits=[tok_ev], slot="wds")
        bf_rd.append(wd_wr)

    barrierA = barrier_now()
    esA.close()

    esC = contextlib.ExitStack()
    c_slab = {wn: [esC.enter_context(nc.sbuf_tensor(f"cs_{wn}{i}", [128, NCH, 128], BF16))
                   for i in range(2)] for wn in ("dt", "bw", "cw")}
    u32p = [esC.enter_context(nc.sbuf_tensor(f"u32p{i}", [128, T], F32)) for i in range(2)]
    gp = [esC.enter_context(nc.sbuf_tensor(f"gp{i}", [128, T], F32)) for i in range(2)]
    t_px = [esC.enter_context(nc.sbuf_tensor(f"px{i}", [128, T], F32)) for i in range(2)]
    t_dec = [esC.enter_context(nc.sbuf_tensor(f"dec{i}", [128, T], F32)) for i in range(2)]
    t_B = [esC.enter_context(nc.sbuf_tensor(f"tB{i}", [128, T], F32)) for i in range(2)]
    t_C = [esC.enter_context(nc.sbuf_tensor(f"tC{i}", [128, T], F32)) for i in range(2)]
    t_st = esC.enter_context(nc.sbuf_tensor("st", [128, T], F32))
    t_ybf = esC.enter_context(nc.sbuf_tensor("ybf", [128, T], BF16))
    t_scol = esC.enter_context(nc.sbuf_tensor("scol", [128, 1], F32))
    t_sini = esC.enter_context(nc.sbuf_tensor("sini", [128, 1], F32))

    dws = {"dt": d_dtw, "bw": d_bw, "cw": d_cw}
    cs_free = {wn: [[], []] for wn in ("dt", "bw", "cw")}
    u32p_free = [[], []]
    gp_free = [[], []]
    px_free = [[], []]
    decf = [[], []]
    Bf = [[], []]
    Cf = [[], []]
    st_free, ybf_free, sini_free, scol_free = [], [], [], []
    ybf_wr = [None] * NCH
    slab_tok_c = [None] * NCH
    cbank = 0
    saved = {}

    def emit_c_slabs(c, first=False):
        sb = c % 2
        toks = {}
        for wn in ("dt", "bw", "cw"):
            toks[wn] = dma("gpsimd", c_slab[wn][sb][:, :, :].rearrange("p s j -> p (s j)"),
                           dws[wn][c], waits=cs_free[wn][sb] + (barrierA if first else []),
                           slot=f"c{wn}{sb}")
            cs_free[wn][sb] = []
        slab_tok_c[c] = toks

    emit_c_slabs(0, first=True)
    ow_v = u_bf[:, 0:8, :].rearrange("p c j -> p (c j)")
    wd_v = u_bf[:, 8:12, :].rearrange("p c j -> p (c j)")
    tok_ow = None

    def emit_deferred(c):
        nonlocal st_free, ybf_free
        if c < 0:
            return
        sb = c % 2
        sv = saved[c]
        op("vector", lambda v: v.tensor_scalar(t_sini[:, :], t_sini[:, :], t_halo[:, 0:1],
                                               None, ALU.mult),
           waits=[sv["recv"]] + st_free, selfwait=True)
        st_free = []
        op("vector", lambda v, sb=sb: v.tensor_tensor_scan(
            t_st[:, :], t_dec[sb][:, :], t_B[sb][:, :], t_sini[:, 0:1], ALU.mult, ALU.add),
           selfwait=True)
        sini_free.append(("vector", cnt["vector"]))
        op("vector", lambda v, sb=sb: v.tensor_mul(t_st[:, :], t_C[sb][:, :], t_st[:, :]),
           waits=sv["C_evacs"])
        op("vector", lambda v, sb=sb, c=c: v.scalar_tensor_tensor(
            t_st[:, :], u32p[sb][:, :], t_dpar[:, c:c + 1], t_st[:, :], ALU.mult, ALU.add))
        tok_y = op("vector", lambda v, sb=sb: v.tensor_mul(t_st[:, :], t_st[:, :], gp[sb][:, :]),
                   waits=[sv["pf"]], selfwait=True)
        decf[sb].append(tok_y)
        Bf[sb].append(tok_y)
        Cf[sb].append(tok_y)
        u32p_free[sb].append(tok_y)
        gp_free[sb].append(tok_y)
        tok_ybf = op("scalar", lambda s: s.copy(t_ybf[:, :], t_st[:, :]),
                     waits=[tok_y] + ybf_free)
        st_free.append(tok_ybf)
        ybf_wr[c] = dma("sync", s_ybf[c], t_ybf[:, :], waits=[tok_ybf], slot="ybfs")
        ybf_free = [ybf_wr[c]]
        dma("sync", ysl01[0][:, c, :], s_ybf[c, :, 0:128], waits=[ybf_wr[c]], slot="y0f")

    for c in range(NCH):
        sb = c % 2
        if c + 1 < NCH:
            emit_c_slabs(c + 1)
        tok_pf_u = dma("sync", u32p[sb][:, :], s_u32[c],
                       waits=u32p_free[sb] + [u32_wr[c]] + (barrierA if c == 0 else []),
                       slot=f"u32p{sb}")
        u32p_free[sb] = []
        tok_pf = dma("sync", gp[sb][:, :], s_g[c], waits=gp_free[sb] + [g_wr[c]], slot=f"gp{sb}")
        gp_free[sb] = []

        dt_evacs = []
        for ti in range(4):
            bk = cbank % 6
            cbank += 1
            mm_group([c_slab["dt"][sb][:, s_, :] for s_ in range(NCH)],
                     [u_bf[:, s_, ti * 512:(ti + 1) * 512] for s_ in range(NCH)],
                     bk, waits=[slab_tok_c[c]["dt"], ubf_last])
            etok = op("scalar", lambda s, bk=bk, ti=ti, c=c, sb=sb: s.activation(
                t_px[sb][:, ti * 512:(ti + 1) * 512], pbank[bk][:, :], AF.Identity,
                bias=t_dtb[:, c:c + 1]), waits=[bank_w[bk]] + px_free[sb])
            bank_r[bk].append(etok)
            dt_evacs.append(etok)
            if ti == 3:
                px_free[sb] = []

        B_evacs = []
        C_evacs = []
        for ti in range(4):
            bk = cbank % 6
            cbank += 1
            mm_group([c_slab["bw"][sb][:, s_, :] for s_ in range(NCH)],
                     [u_bf[:, s_, ti * 512:(ti + 1) * 512] for s_ in range(NCH)],
                     bk, waits=[slab_tok_c[c]["bw"]])
            etok = op("scalar", lambda s, bk=bk, ti=ti, sb=sb: s.activation(
                t_B[sb][:, ti * 512:(ti + 1) * 512], pbank[bk][:, :], AF.Tanh),
                waits=[bank_w[bk]] + Bf[sb])
            bank_r[bk].append(etok)
            B_evacs.append(etok)
            if ti == 3:
                Bf[sb] = []
        for ti in range(4):
            bk = cbank % 6
            cbank += 1
            mm_group([c_slab["cw"][sb][:, s_, :] for s_ in range(NCH)],
                     [u_bf[:, s_, ti * 512:(ti + 1) * 512] for s_ in range(NCH)],
                     bk, waits=[slab_tok_c[c]["cw"]])
            etok = op("scalar", lambda s, bk=bk, ti=ti, sb=sb: s.activation(
                t_C[sb][:, ti * 512:(ti + 1) * 512], pbank[bk][:, :], AF.Tanh),
                waits=[bank_w[bk]] + Cf[sb])
            bank_r[bk].append(etok)
            C_evacs.append(etok)
            if ti == 3:
                Cf[sb] = []
        last_mm = ("tensor", cnt["tensor"])
        for wn in ("dt", "bw", "cw"):
            cs_free[wn][sb] = [last_mm]
        if c == NCH - 1:
            tok_ow = dma("gpsimd", ow_v, d_ow, waits=[last_mm], slot="ow")

        op("scalar", lambda s, sb=sb: s.activation(t_px[sb][:, :], t_px[sb][:, :], AF.Exp),
           waits=dt_evacs + [tok_consts], selfwait=True)
        tok_sp = op("scalar", lambda s, sb=sb: s.activation(t_px[sb][:, :], t_px[sb][:, :],
                                                            AF.Ln, bias=1.0), selfwait=True)
        tok_dec = op("scalar", lambda s, c=c, sb=sb: s.activation(
            t_dec[sb][:, :], t_px[sb][:, :], AF.Exp, bias=t_negAe[:, c:c + 1],
            scale=t_negA[:, c:c + 1]), waits=[tok_sp, tok_negA] + decf[sb])
        decf[sb] = []

        op("vector", lambda v, sb=sb: v.tensor_mul(t_B[sb][:, :], t_B[sb][:, :], u32p[sb][:, :]),
           waits=[tok_dec, tok_pf_u] + B_evacs)
        op("vector", lambda v, sb=sb: v.tensor_scalar(t_px[sb][:, :], t_dec[sb][:, :], -1.0, 1.0,
                                                      ALU.mult, ALU.add))
        op("vector", lambda v, sb=sb: v.tensor_mul(t_B[sb][:, :], t_px[sb][:, :], t_B[sb][:, :]),
           selfwait=True)
        px_free[sb].append(("vector", cnt["vector"]))
        op("vector", lambda v, sb=sb: v.tensor_tensor_scan(
            t_st[:, :], t_dec[sb][:, :], t_B[sb][:, :], 0.0, ALU.mult, ALU.add),
           waits=st_free, selfwait=True)
        st_free = []
        tok_scol = op("vector", lambda v: v.tensor_scalar(
            t_scol[:, :], t_st[:, T - 1:T], t_evm[:, 0:1], None, ALU.mult),
            waits=scol_free, selfwait=True)
        st_free.append(tok_scol)

        tok_ccin = dma("gpsimd", d_ccin[c], t_scol[:, :], waits=[tok_scol], slot="ccin")
        scol_free = [tok_ccin]

        emit_deferred(c - 1)

        def cc_body(g, c=c, w=mkwait([tok_ccin])):
            do_wait(g, w)
            g.collective_compute(
                "AllReduce", ALU.add,
                replica_groups=[[0, 1], [2, 3], [4, 5], [6, 7]],
                ins=[d_ccin[c]], outs=[d_ccout[c]],
            ).then_inc(ccsem[c])
        emit("gpsimd", cc_body)

        w_recv = mkwait(sini_free)
        sini_free = []
        if "recv" not in slot_sems:
            slot_sems["recv"] = es.enter_context(nc.semaphore("ds_recv"))
            slot_gen["recv"] = 0
        slot_gen["recv"] += 1
        tok_recv = ("S", "recv", slot_gen["recv"] * 16)

        def recv_body(g, c=c, w_recv=w_recv):
            g.wait_ge(ccsem[c], 1)
            do_wait(g, w_recv)
            g.dma_start(out=t_sini[:, :], in_=d_ccout[c]).then_inc(slot_sems["recv"], 16)
        emit("gpsimd", recv_body)

        saved[c] = {"recv": tok_recv, "C_evacs": C_evacs, "pf": tok_pf}

    ubf_dead = ("tensor", cnt["tensor"])
    ubf_ysl = [u_bf[:, 12 + p, :].rearrange("p (c j) -> p c j", j=128) for p in range(2)]

    def ysl_view(t_):
        return ysl01[0] if t_ == 0 else ubf_ysl[t_ % 2]
    tok_wd = dma("sync", wd_v, s_wd, waits=[ubf_dead, wd_wr], slot="wdl")

    emit_deferred(NCH - 1)

    barrierC = barrier_now()
    esC.close()

    esD = contextlib.ExitStack()
    xtm = [esD.enter_context(nc.sbuf_tensor(f"xtm{i}", [128, 1024], F32)) for i in range(2)]
    xb = [esD.enter_context(nc.sbuf_tensor(f"xb{i}", [128, 1024], F32)) for i in range(2)]
    t_hb = [esD.enter_context(nc.sbuf_tensor(f"thb{i}", [128, 1024], F32)) for i in range(2)]
    t_h = [esD.enter_context(nc.sbuf_tensor(f"th{i}", [128, 1024], F32)) for i in range(2)]
    d_q1 = esD.enter_context(nc.sbuf_tensor("dq1", [128, 1024], F32))
    d_q2 = esD.enter_context(nc.sbuf_tensor("dq2", [128, 1024], F32))
    d_q3 = esD.enter_context(nc.sbuf_tensor("dq3", [128, 1024], F32))
    d_q4 = esD.enter_context(nc.sbuf_tensor("dq4", [128, 1024], F32))
    d_qi = esD.enter_context(nc.sbuf_tensor("dqi", [128, 1024], I32))
    d_sc = esD.enter_context(nc.sbuf_tensor("dsc", [128, 32], F32))
    d_rs = esD.enter_context(nc.sbuf_tensor("drs", [128, 32], F32))
    t_mv = esD.enter_context(nc.sbuf_tensor("mv", [128, 8], F32))
    t_xdbf = esD.enter_context(nc.sbuf_tensor("xdbf", [128, 1024], BF16))
    t_xdT = esD.enter_context(nc.sbuf_tensor("xdT", [128, 1024], BF16))
    t_outb = esD.enter_context(nc.sbuf_tensor("outbt", [128, 1024], F32))
    t_ffnb = esD.enter_context(nc.sbuf_tensor("ffnbt", [128, 1024], F32))
    obuf = [esD.enter_context(nc.sbuf_tensor(f"obuf{i}", [128, 1024], F32)) for i in range(2)]

    e_q1 = esD.enter_context(nc.sbuf_tensor("eq1", [128, 1024], F32))
    e_q2 = esD.enter_context(nc.sbuf_tensor("eq2", [128, 1024], F32))
    e_q3 = esD.enter_context(nc.sbuf_tensor("eq3", [128, 1024], F32))
    e_q4 = esD.enter_context(nc.sbuf_tensor("eq4", [128, 1024], F32))
    e_qi = esD.enter_context(nc.sbuf_tensor("eqi", [128, 1024], I32))
    e_sc = esD.enter_context(nc.sbuf_tensor("esc", [128, 32], F32))
    e_rs = esD.enter_context(nc.sbuf_tensor("ers", [128, 32], F32))
    quant_d = make_quant(d_q1, d_q2, d_q3, d_q4, d_qi, d_sc, d_rs)

    dma("sync", t_outb[:, :], d_outb, waits=barrierC, slot="dbias")
    tok_bias = dma("sync", t_ffnb[:, :], d_ffnb, slot="dbias")

    ysl_free = [[], []]
    xtm_free = [[], []]
    xb_free = [[], []]
    hb_free = []
    obuf_free = [[], []]
    h_free, q_free = [], []
    xd_free, xdT_free = [], []
    out_toks = []
    ysl_tok = [None] * 16
    xtm_tok = [None] * 16
    ysl_tok[0] = ("S", "y0f", 16 * NCH)
    ysl_tok[1] = dma("scalar", ysl_view(1),
                     s_ybf[:, :, 128:256].rearrange("c p j -> p c j"),
                     waits=list(ybf_wr) + [ubf_dead], slot="ysl1")
    for tt in range(2):
        xtm_tok[tt] = dma("sync", xtm[tt % 2][:, :],
                          d_x[HALO + tt * 128: HALO + (tt + 1) * 128, :], waits=barrierC,
                          slot=f"xtm{tt % 2}")

    op_banks = [None] * 16
    op_w = [None] * 16

    def emit_op_mm(tau):
        sb = tau % 2
        bks = []
        for hh in range(2):
            bk = 2 * (tau % 2) + hh
            bks.append(bk)
            mm_group([ysl_view(tau)[:, cc, :] for cc in range(NCH)],
                     [u_bf[:, cc // 2, (cc % 2) * 1024 + hh * 512:(cc % 2) * 1024 + (hh + 1) * 512]
                      for cc in range(NCH)],
                     bk, waits=[ysl_tok[tau], tok_ow])
        op_banks[tau] = tuple(bks)
        op_w[tau] = ("tensor", cnt["tensor"])
        ysl_free[sb].append(op_w[tau])

    emit_op_mm(0)
    emit_op_mm(1)

    h_free = [[], []]
    hb_free = [[], []]
    eq_free = []
    mv1_free = []
    mv2_free = []
    back_state = {}

    def emit_back_evacs(tau):
        if tau < 0:
            return
        st = back_state[tau]
        sbb = st["sb"]
        fv = []
        for hh in range(2):
            bk = st["f_banks"][hh]
            etok = op("vector", lambda v, bk=bk, hh=hh, sbb=sbb: v.tensor_add(
                obuf[sbb][:, hh * 512:(hh + 1) * 512], pbank[bk][:, :],
                t_hb[sbb][:, hh * 512:(hh + 1) * 512]),
               waits=[st["f_w"], st["tok_hb"]] + (obuf_free[sbb] if hh == 0 else []))
            if hh == 0:
                obuf_free[sbb] = []
            bank_r[bk].append(etok)
            fv.append(etok)
        hb_free[sbb] = [fv[-1]]
        st["fv"] = fv

    def emit_back(tau, qsink=None):
        nonlocal eq_free, mv2_free

        def qpop(n=2):
            for _ in range(n):
                if qsink:
                    qsink.pop(0)()
        if tau < 0:
            return
        st = back_state.pop(tau)
        sbb = st["sb"]
        fv = st["fv"]
        op("vector", lambda v, sbb=sbb: v.reduce_sum(t_mv[:, 4:5], obuf[sbb][:, :], axis=AX.X),
           waits=fv + mv2_free, selfwait=True)
        mv2_free = []
        qpop(1)
        tok_nm2 = op("vector", lambda v: v.tensor_scalar(t_mv[:, 4:5], t_mv[:, 4:5],
                                                         -1.0 / 1024.0, None, ALU.mult),
                     selfwait=True)
        qpop(2)
        tok_sq2 = op("scalar", lambda s, sbb=sbb: s.activation(e_q1[:, :], obuf[sbb][:, :],
                                                               AF.Square, bias=t_mv[:, 4:5]),
                     waits=[tok_nm2] + eq_free, selfwait=True)
        eq_free = []
        qpop(3)
        op("vector", lambda v: v.reduce_sum(t_mv[:, 5:6], e_q1[:, :], axis=AX.X),
           waits=[tok_sq2], selfwait=True)
        tok_var2 = op("vector", lambda v: v.tensor_scalar(t_mv[:, 5:6], t_mv[:, 5:6],
                                                          1.0 / 1024.0, None, ALU.mult),
                      selfwait=True)
        eq_free.append(tok_var2)
        qpop(2)
        op("scalar", lambda s: s.activation(t_mv[:, 5:6], t_mv[:, 5:6], AF.Ln, bias=1e-5),
           waits=[tok_var2], selfwait=True)
        op("scalar", lambda s: s.activation(t_mv[:, 5:6], t_mv[:, 5:6], AF.Exp,
                                            scale=-0.5), selfwait=True)
        op("scalar", lambda s: s.activation(t_mv[:, 6:7], t_mv[:, 4:5], AF.Identity,
                                            scale=t_mv[:, 5:6]), selfwait=True)
        tok_o = op("scalar", lambda s, sbb=sbb: s.activation(
            obuf[sbb][:, :], obuf[sbb][:, :], AF.Identity,
            scale=t_mv[:, 5:6], bias=t_mv[:, 6:7]), selfwait=True)
        mv2_free = [tok_o]
        otok = dma("sync", d_out[tau * 128:(tau + 1) * 128, :], obuf[sbb][:, :],
                   waits=[tok_o], slot=f"out{sbb}")
        out_toks.append(otok)
        obuf_free[sbb].append(otok)

    for tau in range(16):
        sb = tau % 2
        if tau + 2 < 16:
            tt = tau + 2
            w_extra = [ubf_dead] if tt < 4 else []
            ysl_tok[tt] = dma("scalar", ysl_view(tt),
                              s_ybf[:, :, tt * 128:(tt + 1) * 128].rearrange("c p j -> p c j"),
                              waits=ysl_free[tt % 2] + w_extra, slot=f"ysl{tt % 2}")
            ysl_free[tt % 2] = []

        tok_xb = op("gpsimd", lambda g, sb=sb: g.tensor_add(xb[sb][:, :], xtm[sb][:, :],
                                                            t_outb[:, :]),
                    waits=[xtm_tok[tau], tok_bias] + xb_free[sb], selfwait=True)
        xtm_free[sb].append(tok_xb)
        ev = []
        for hh in range(2):
            bk = op_banks[tau][hh]
            etok = op("vector", lambda v, bk=bk, hh=hh, sb=sb: v.tensor_add(
                t_h[sb][:, hh * 512:(hh + 1) * 512], pbank[bk][:, :],
                xb[sb][:, hh * 512:(hh + 1) * 512]),
               waits=[op_w[tau], tok_xb] + (h_free[sb] if hh == 0 else []))
            if hh == 0:
                h_free[sb] = []
            bank_r[bk].append(etok)
            ev.append(etok)
        xb_free[sb] = [ev[-1]]
        if tau + 2 < 16:
            tt = tau + 2
            xtm_tok[tt] = dma("sync", xtm[tt % 2][:, :],
                              d_x[HALO + tt * 128: HALO + (tt + 1) * 128, :],
                              waits=xtm_free[tt % 2], slot=f"xtm{tt % 2}")
            xtm_free[tt % 2] = []

        op("vector", lambda v, sb=sb: v.reduce_sum(t_mv[:, 0:1], t_h[sb][:, :], axis=AX.X),
           waits=ev + mv1_free, selfwait=True)
        mv1_free = []
        tok_nm = op("vector", lambda v: v.tensor_scalar(t_mv[:, 0:1], t_mv[:, 0:1],
                                                        -1.0 / 1024.0, None, ALU.mult),
                    selfwait=True)
        emit_back_evacs(tau - 1)
        tok_sq = op("scalar", lambda s, sb=sb: s.activation(d_q1[:, :], t_h[sb][:, :], AF.Square,
                                                            bias=t_mv[:, 0:1]),
                    waits=[tok_nm] + q_free, selfwait=True)
        q_free = []
        op("vector", lambda v: v.reduce_sum(t_mv[:, 1:2], d_q1[:, :], axis=AX.X),
           waits=[tok_sq], selfwait=True)
        tok_var = op("vector", lambda v: v.tensor_scalar(t_mv[:, 1:2], t_mv[:, 1:2],
                                                         1.0 / 1024.0, None, ALU.mult),
                     selfwait=True)
        op("scalar", lambda s: s.activation(t_mv[:, 1:2], t_mv[:, 1:2], AF.Ln, bias=1e-5),
           waits=[tok_var, tok_consts], selfwait=True)
        op("scalar", lambda s: s.activation(t_mv[:, 1:2], t_mv[:, 1:2], AF.Exp,
                                            scale=-0.5), selfwait=True)
        op("scalar", lambda s: s.activation(t_mv[:, 2:3], t_mv[:, 0:1], AF.Identity,
                                            scale=t_mv[:, 1:2]), selfwait=True)
        tok_ln1 = op("scalar", lambda s, sb=sb: s.activation(
            t_h[sb][:, :], t_h[sb][:, :], AF.Identity,
            scale=t_mv[:, 1:2], bias=t_mv[:, 2:3]), selfwait=True)
        mv1_free = [tok_ln1]

        tok_hb = op("gpsimd", lambda g, sb=sb: g.tensor_add(t_hb[sb][:, :], t_h[sb][:, :],
                                                            t_ffnb[:, :]),
                    waits=[tok_ln1, tok_bias] + hb_free[sb], selfwait=True)

        if debug and tau == 0:
            tok_ln1 = dma("sync", d_dbg[1], t_h[sb][:, :], waits=[tok_ln1], slot="dbg")
        qsink = []
        q_holder = quant_d(t_h[sb][:, :], t_xdbf[:, :], [tok_ln1] + xd_free, sink=qsink)
        emit_back(tau - 1, qsink)
        while qsink:
            qsink.pop(0)()
        tok_qd = q_holder["tok"]
        xd_free = []
        q_free.append(tok_qd)
        h_free[sb] = [tok_hb, tok_qd]

        w = mkwait([tok_qd, tok_smalls] + bf_rd)
        bf_rd = []
        tok_tp = alloc("tensor")

        def tp_body(t, w=w):
            do_wait(t, w)
            for k in range(NDC):
                ins = t.transpose(p_bf[:, k * 128:(k + 1) * 128],
                                  t_xdbf[:, k * 128:(k + 1) * 128], t_eye[:, :])
            ins.then_inc(sem["tensor"], 1)
        emit("tensor", tp_body)
        xd_free.append(tok_tp)
        tok_xdT = op("scalar", lambda s: s.copy(t_xdT[:, :], p_bf[:, :]),
                     waits=[tok_tp] + xdT_free)
        xdT_free = []
        bf_rd.append(tok_xdT)

        f_banks = []
        for hh in range(2):
            bk = 4 + hh
            f_banks.append(bk)
            mm_group([t_xdT[:, k * 128:(k + 1) * 128] for k in range(NDC)],
                     [u_bf[:, 8 + k // 2, (k % 2) * 1024 + hh * 512:(k % 2) * 1024 + (hh + 1) * 512]
                      for k in range(NDC)],
                     bk, waits=[tok_xdT, tok_wd])
        f_w = ("tensor", cnt["tensor"])
        xdT_free.append(f_w)
        back_state[tau] = {"sb": sb, "f_banks": tuple(f_banks), "f_w": f_w, "tok_hb": tok_hb}

        if tau + 2 < 16:
            emit_op_mm(tau + 2)

    emit_back_evacs(15)
    emit_back(15)

    wfin = mkwait(out_toks)

    def fin(q, wfin=wfin):
        do_wait(q, wfin)
    emit("sync", fin)

    esD.close()

    with es:
        with nc.Block() as block:
            block.sync(lambda q: [f(q) for f in streams["sync"]])
            block.gpsimd(lambda q: [f(q) for f in streams["gpsimd"]])
            block.scalar(lambda q: [f(q) for f in streams["scalar"]])
            block.vector(lambda q: [f(q) for f in streams["vector"]])
            block.tensor(lambda q: [f(q) for f in streams["tensor"]])
    return nc


_NC_CACHE = {}


def _get_nc():
    if "nc" not in _NC_CACHE:
        _NC_CACHE["nc"] = build_nc()
    return _NC_CACHE["nc"]


def kernel(L0_attn_x, L0_attn_in_proj_w, L0_attn_in_proj_b, L0_attn_conv_w, L0_attn_conv_b,
           L0_attn_dt_w, L0_attn_dt_b, L0_attn_b_w, L0_attn_c_w, L0_attn_A_log,
           L0_attn_D_param, L0_attn_out_w, L0_attn_out_b, L0_ffn_weight, L0_ffn_bias):
    x = np.ascontiguousarray(np.asarray(L0_attn_x, dtype=np.float32))
    in_w = np.asarray(L0_attn_in_proj_w, dtype=np.float32)
    in_b = np.asarray(L0_attn_in_proj_b, dtype=np.float32)
    conv_w = np.asarray(L0_attn_conv_w, dtype=np.float32)
    conv_b = np.asarray(L0_attn_conv_b, dtype=np.float32)
    dt_w = np.asarray(L0_attn_dt_w, dtype=np.float32)
    dt_b = np.asarray(L0_attn_dt_b, dtype=np.float32)
    b_w = np.asarray(L0_attn_b_w, dtype=np.float32)
    c_w = np.asarray(L0_attn_c_w, dtype=np.float32)
    a_log = np.asarray(L0_attn_A_log, dtype=np.float32)
    d_par = np.asarray(L0_attn_D_param, dtype=np.float32)
    out_w = np.asarray(L0_attn_out_w, dtype=np.float32)
    out_b = np.asarray(L0_attn_out_b, dtype=np.float32)
    ffn_w = np.asarray(L0_ffn_weight, dtype=np.float32)
    ffn_b = np.asarray(L0_ffn_bias, dtype=np.float32)

    win = np.ascontiguousarray(in_w.reshape(32, 128, 8, 128).transpose(0, 3, 2, 1)
                               ).reshape(32, 128, 1024)

    def big_t(w):
        return np.ascontiguousarray(w.reshape(16, 128, 16, 128).transpose(0, 3, 2, 1)).reshape(16, 128, 2048)
    dtw = big_t(dt_w)
    bw = big_t(b_w)
    cw = big_t(c_w)
    ow = np.ascontiguousarray(out_w.T.reshape(16, 128, 1024).transpose(1, 0, 2)).reshape(128, 16384)
    inb = np.ascontiguousarray(in_b.reshape(32, 128).T)
    cwv = np.ascontiguousarray(conv_w[:, 0, :].reshape(16, 128, 4).transpose(1, 0, 2)).reshape(128, 64)
    dtb = np.ascontiguousarray(dt_b.reshape(16, 128).T)
    cvb = np.ascontiguousarray(conv_b.reshape(16, 128).T)
    alog = np.ascontiguousarray(a_log.reshape(16, 128).T)
    dpar = np.ascontiguousarray(d_par.reshape(16, 128).T)
    outbt = np.ascontiguousarray(np.broadcast_to(out_b[None, :], (128, 1024)))
    ffnbt = np.ascontiguousarray(np.broadcast_to(ffn_b[None, :], (128, 1024)))
    eye = np.eye(128).astype(ml_dtypes.bfloat16)

    in_maps = make_in_maps(x, win, dtw, bw, cw, ow, ffn_w, inb, cwv, cvb, dtb, alog,
                           dpar, outbt, ffnbt, eye)
    return run_all(in_maps)


def make_in_maps(x, win, dtw, bw, cw, ow, ffn_w, inb, cwv, cvb, dtb, alog, dpar,
                 outbt, ffnbt, eye):
    in_maps = []
    for c in range(8):
        b, half = c // 2, c % 2
        xs = np.zeros((TPAD, D), np.float32)
        if half == 1:
            xs[0:HALO] = x[b, T - HALO:T]
        xs[HALO:HALO + T] = x[b, half * T:(half + 1) * T]
        halo = np.zeros((128, 1), np.float32)
        if half == 1:
            halo[:, 0] = 1.0
        in_maps.append({
            "x": xs, "win": win, "dtw": dtw, "bw": bw, "cw": cw, "ow": ow, "fw": ffn_w,
            "inb": inb, "cwv": cwv, "cvb": cvb, "dtb": dtb, "alog": alog, "dpar": dpar,
            "outb": outbt, "ffnb": ffnbt, "eye": eye, "halo": halo,
        })
    return in_maps


_LAST_RES = None


def run_all(in_maps):
    global _LAST_RES
    nc = _get_nc()
    res = run_bass_kernel_spmd(nc, in_maps, core_ids=list(range(8)))
    _LAST_RES = res
    out = np.empty((4, 4096, 1024), np.float32)
    for c in range(8):
        b, half = c // 2, c % 2
        out[b, half * T:(half + 1) * T] = res.results[c]["y_out"]
    return out


# version 29
# speedup vs baseline: 1.0324x; 1.0324x over previous
import contextlib
import numpy as np
import ml_dtypes
import concourse.bass as bass
import concourse.mybir as mybir
from concourse.bass_utils import run_bass_kernel_spmd

F32 = mybir.dt.float32
BF16 = mybir.dt.bfloat16
I32 = mybir.dt.int32
AF = mybir.ActivationFunctionType
ALU = mybir.AluOpType
AX = mybir.AxisListType

T = 2048
HALO = 3
TH = T + HALO
TPAD = 2064
D = 1024
DI = 2048
NCH = 16
NDC = 8

ENGS = ("sync", "gpsimd", "scalar", "vector", "tensor")


def build_nc(debug=False):
    nc = bass.Bass("TRN2", target_bir_lowering=False)

    d_x = nc.dram_tensor("x", [TPAD, D], F32, kind="ExternalInput").ap()
    d_win = nc.dram_tensor("win", [32, 128, 1024], F32, kind="ExternalInput").ap()
    d_dtw = nc.dram_tensor("dtw", [16, 128, 2048], F32, kind="ExternalInput").ap()
    d_bw = nc.dram_tensor("bw", [16, 128, 2048], F32, kind="ExternalInput").ap()
    d_cw = nc.dram_tensor("cw", [16, 128, 2048], F32, kind="ExternalInput").ap()
    d_ow = nc.dram_tensor("ow", [128, 16384], F32, kind="ExternalInput").ap()
    d_fw = nc.dram_tensor("fw", [1024, 1024], F32, kind="ExternalInput").ap()
    d_inb = nc.dram_tensor("inb", [128, 32], F32, kind="ExternalInput").ap()
    d_cwv = nc.dram_tensor("cwv", [128, 64], F32, kind="ExternalInput").ap()
    d_cvb = nc.dram_tensor("cvb", [128, 16], F32, kind="ExternalInput").ap()
    d_dtb = nc.dram_tensor("dtb", [128, 16], F32, kind="ExternalInput").ap()
    d_alog = nc.dram_tensor("alog", [128, 16], F32, kind="ExternalInput").ap()
    d_dpar = nc.dram_tensor("dpar", [128, 16], F32, kind="ExternalInput").ap()
    d_outb = nc.dram_tensor("outb", [128, 1024], F32, kind="ExternalInput").ap()
    d_ffnb = nc.dram_tensor("ffnb", [128, 1024], F32, kind="ExternalInput").ap()
    d_eye = nc.dram_tensor("eye", [128, 128], BF16, kind="ExternalInput").ap()
    d_halo = nc.dram_tensor("halo", [128, 1], F32, kind="ExternalInput").ap()
    d_out = nc.dram_tensor("y_out", [T, D], F32, kind="ExternalOutput").ap()

    dk = "Internal"
    if debug:
        d_dbg = nc.dram_tensor("dbg0", [4, 128, 1024], F32, kind="ExternalOutput").ap()
        d_dbg2 = nc.dram_tensor("dbg1", [128, 2048], F32, kind="ExternalOutput").ap()
        d_dbg3 = nc.dram_tensor("dbg2", [128, 1024], F32, kind="ExternalOutput").ap()
    s_xbf = nc.dram_tensor("s_xbf", [TPAD, D], BF16, kind="Internal").ap()
    s_u32 = nc.dram_tensor("s_u32", [NCH, 128, T], F32, kind=dk).ap()
    s_g = nc.dram_tensor("s_g", [NCH, 128, T], F32, kind=dk).ap()
    s_ybf = nc.dram_tensor("s_ybf", [NCH, 128, T], BF16, kind=dk).ap()
    s_wd = nc.dram_tensor("s_wd", [128, 8192], BF16, kind=dk).ap()
    d_ccin = [nc.dram_tensor(f"ccin{c}", [128, 1], F32, kind="Internal").ap() for c in range(NCH)]
    d_ccout = [nc.dram_tensor(f"ccout{c}", [128, 1], F32, kind="Internal").ap() for c in range(NCH)]

    es = contextlib.ExitStack()

    sem = {e: es.enter_context(nc.semaphore(f"sm_{e}")) for e in ENGS}
    ccsem = [es.enter_context(nc.semaphore(f"cc{c}")) for c in range(NCH)]
    cnt = {e: 0 for e in ENGS}
    streams = {e: [] for e in ENGS}

    def emit(eng, fn):
        streams[eng].append(fn)

    def alloc(eng, amt=1):
        cnt[eng] += amt
        return (eng, cnt[eng])

    slot_sems = {}
    slot_gen = {}

    def slot_tok(name):
        return ("S", name, slot_gen.get(name, 0) * 16)

    def mkwait(toks):
        merged = {}
        for t_ in toks:
            if t_ is None:
                continue
            if t_[0] == "S":
                _, nm, v = t_
                key = ("S", nm)
            else:
                key = t_[0]
                v = t_[1]
            merged[key] = max(merged.get(key, 0), v)
        return tuple(merged.items())

    def do_wait(q, merged):
        for key, v in merged:
            if isinstance(key, tuple):
                q.wait_ge(slot_sems[key[1]], v)
            else:
                q.wait_ge(sem[key], v)

    def barrier_now():
        return [(e, cnt[e]) for e in ENGS] + [slot_tok(s) for s in slot_gen]

    t_consts = es.enter_context(nc.sbuf_tensor("cst", [128, 4], F32))
    t_inb = es.enter_context(nc.sbuf_tensor("sb_inb", [128, 32], F32))
    t_cwv = es.enter_context(nc.sbuf_tensor("sb_cwv", [128, 64], F32))
    t_cvb = es.enter_context(nc.sbuf_tensor("sb_cvb", [128, 16], F32))
    t_dtb = es.enter_context(nc.sbuf_tensor("sb_dtb", [128, 16], F32))
    t_negA = es.enter_context(nc.sbuf_tensor("negA", [128, 16], F32))
    t_negAe = es.enter_context(nc.sbuf_tensor("negAe", [128, 16], F32))
    t_dpar = es.enter_context(nc.sbuf_tensor("sb_dpar", [128, 16], F32))
    t_eye = es.enter_context(nc.sbuf_tensor("sb_eye", [128, 128], BF16))
    t_halo = es.enter_context(nc.sbuf_tensor("sb_halo", [128, 1], F32))
    t_evm = es.enter_context(nc.sbuf_tensor("evm", [128, 1], F32))

    nc.const_aps.register(F32, -1.0, t_consts[:, 0:1])
    nc.const_aps.register(F32, 1e-5, t_consts[:, 1:2])
    nc.const_aps.register(F32, -0.5, t_consts[:, 2:3])

    pbank = [es.enter_context(nc.psum_tensor(f"pb{i}", [128, 512], F32)) for i in range(6)]
    p_bf = es.enter_context(nc.psum_tensor("pbf", [128, 1024], BF16))
    bank_w = [None] * 6
    bank_r = [[] for _ in range(6)]

    def dma(eng, out_ap, in_ap, waits=(), slot=None, transpose=False):
        assert slot is not None
        w = mkwait(waits)
        if slot not in slot_sems:
            slot_sems[slot] = es.enter_context(nc.semaphore(f"ds_{slot}"))
            slot_gen[slot] = 0
        slot_gen[slot] += 1
        ssem = slot_sems[slot]
        tok = ("S", slot, slot_gen[slot] * 16)

        def body(q, w=w, out_ap=out_ap, in_ap=in_ap, ssem=ssem, transpose=transpose):
            do_wait(q, w)
            if transpose:
                q.dma_start_transpose(out_ap, in_ap).then_inc(ssem, 16)
            else:
                q.dma_start(out=out_ap, in_=in_ap).then_inc(ssem, 16)
        emit(eng, body)
        return tok

    def mm_group(lhsTs, rhss, bk, waits=(), psum_ap=None):
        w = mkwait(list(waits) + bank_r[bk])
        bank_r[bk] = []
        tok = alloc("tensor")
        out_ap = psum_ap if psum_ap is not None else pbank[bk][:, :]

        def body(t, w=w, lhsTs=tuple(lhsTs), rhss=tuple(rhss), out_ap=out_ap):
            do_wait(t, w)
            n = len(lhsTs)
            for i in range(n):
                ins = t.matmul(out_ap, lhsTs[i], rhss[i], start=(i == 0), stop=(i == n - 1))
            ins.then_inc(sem["tensor"], 1)
        emit("tensor", body)
        bank_w[bk] = tok
        return tok

    def op(eng, builder, waits=(), selfwait=False, amt=1):
        w = mkwait(waits)
        tok = alloc(eng, amt)
        val = cnt[eng]

        def body(q, w=w, builder=builder, val=val, amt=amt, selfwait=selfwait):
            do_wait(q, w)
            builder(q).then_inc(sem[eng], amt)
            if selfwait:
                q.wait_ge(sem[eng], val)
        emit(eng, body)
        return tok

    def pre_g(g):
        g.memset(t_consts[:, 0:1], -1.0)
        g.memset(t_consts[:, 1:2], 1e-5)
        g.memset(t_consts[:, 2:3], -0.5).then_inc(sem["gpsimd"], 1)
    emit("gpsimd", pre_g)
    tok_consts = alloc("gpsimd")

    tok_smalls = None
    for dst, src in ((t_inb, d_inb), (t_cwv, d_cwv), (t_cvb, d_cvb), (t_dtb, d_dtb),
                     (t_negA, d_alog), (t_dpar, d_dpar), (t_eye, d_eye), (t_halo, d_halo)):
        tok_smalls = dma("sync", dst[:, :], src, slot="smalls")

    tok_nA_s = op("scalar", lambda s: s.activation(t_negA[:, :], t_negA[:, :], AF.Exp),
                  waits=[tok_smalls], selfwait=True)
    op("vector", lambda v: v.tensor_scalar(t_negA[:, :], t_negA[:, :], -1.0, None, ALU.mult),
       waits=[tok_nA_s], selfwait=True)
    op("vector", lambda v: v.tensor_scalar(t_negAe[:, :], t_negA[:, :], 1e-4, None, ALU.mult),
       selfwait=True)
    tok_negA = op("vector", lambda v: v.tensor_scalar(t_evm[:, :], t_halo[:, :], -1.0, 1.0,
                                                      ALU.mult, ALU.add), selfwait=True)

    tok_xbf = dma("gpsimd", s_xbf, d_x, slot="xcast")
    abank = 0

    esA = contextlib.ExitStack()
    u_bf = es.enter_context(nc.sbuf_tensor("u_bf", [128, NCH, T], BF16))
    ysl01 = [es.enter_context(nc.sbuf_tensor("ysl01_0", [128, NCH, 128], BF16))]
    x_fm = esA.enter_context(nc.sbuf_tensor("x_fm", [128, NDC, TPAD], BF16))
    a_slab = [esA.enter_context(nc.sbuf_tensor(f"aslab{i}", [128, NDC, 128], BF16)) for i in range(2)]
    t_xbr = esA.enter_context(nc.sbuf_tensor("xbr", [128, TH], F32))
    t_acc = esA.enter_context(nc.sbuf_tensor("cacc", [128, T], F32))
    u32b = [esA.enter_context(nc.sbuf_tensor(f"u32b{i}", [128, T], F32)) for i in range(2)]
    gb = [esA.enter_context(nc.sbuf_tensor(f"gb{i}", [128, T], F32)) for i in range(2)]
    wq_bf = esA.enter_context(nc.sbuf_tensor("wq_bf", [128, NDC, 1024], BF16))
    wqf = [esA.enter_context(nc.sbuf_tensor(f"wqf{i}", [128, 1024], F32)) for i in range(2)]
    a_q1 = esA.enter_context(nc.sbuf_tensor("wq1", [128, 1024], F32))
    a_q2 = esA.enter_context(nc.sbuf_tensor("wq2", [128, 1024], F32))
    a_q3 = esA.enter_context(nc.sbuf_tensor("wq3", [128, 1024], F32))
    a_q4 = esA.enter_context(nc.sbuf_tensor("wq4", [128, 1024], F32))
    a_qi = esA.enter_context(nc.sbuf_tensor("wqi", [128, 1024], I32))
    a_qsc = esA.enter_context(nc.sbuf_tensor("wqsc", [128, 32], F32))
    a_qrs = esA.enter_context(nc.sbuf_tensor("wqrs", [128, 32], F32))
    wdst = esA.enter_context(nc.sbuf_tensor("wdst", [128, 1024], BF16))

    def make_quant(q1, q2, q3, q4, qi, qsc, qrs, eng="vector", qsp=None):
        if eng == "split":
            def emit_split(src_ap, out_bf_ap, waits):
                src3 = src_ap.rearrange("p (n m) -> p n m", m=32)
                rs_b = qrs[:, :].unsqueeze(2).broadcast_to([128, 32, 32])
                sc_b = qsc[:, :].unsqueeze(2).broadcast_to([128, 32, 32])
                op("vector", lambda v: v.reduce_max(qsc[:, :], src3, axis=AX.X,
                                                    apply_absolute_value=True),
                   waits=waits, selfwait=True)
                op("vector", lambda v: v.tensor_scalar(qsc[:, :], qsc[:, :], 1.0 / 7.5, 1e-30,
                                                       ALU.mult, ALU.max), selfwait=True)
                tv1 = op("vector", lambda v: v.reciprocal(qrs[:, :], qsc[:, :]), selfwait=True)
                op("gpsimd", lambda g: g.tensor_tensor(
                    q1[:, :].rearrange("p (n m) -> p n m", m=32), src3, rs_b, ALU.mult),
                   waits=[tv1], selfwait=True)
                tg1 = op("gpsimd", lambda g: g.tensor_mul(q2[:, :], q1[:, :], q1[:, :]),
                         selfwait=True)
                op("vector", lambda v: v.tensor_scalar(q3[:, :], q2[:, :], 4.0, None, ALU.is_ge),
                   waits=[tg1], selfwait=True)
                tv2 = op("vector", lambda v: v.tensor_scalar(q2[:, :], q2[:, :], 16.0, None,
                                                             ALU.is_ge), selfwait=True)
                op("gpsimd", lambda g: g.tensor_scalar(q4[:, :], q3[:, :], 0.125, 0.125,
                                                       ALU.mult, ALU.add),
                   waits=[tv2], selfwait=True)
                op("gpsimd", lambda g: g.tensor_scalar(qsp[:, :], q2[:, :], 0.25, None, ALU.mult),
                   selfwait=True)
                op("gpsimd", lambda g: g.tensor_add(q4[:, :], q4[:, :], qsp[:, :]), selfwait=True)
                op("gpsimd", lambda g: g.tensor_scalar(q3[:, :], q3[:, :], -4.0, 8.0, ALU.mult,
                                                       ALU.add), selfwait=True)
                op("gpsimd", lambda g: g.tensor_scalar(qsp[:, :], q2[:, :], -2.0, None, ALU.mult),
                   selfwait=True)
                op("gpsimd", lambda g: g.tensor_add(q3[:, :], q3[:, :], qsp[:, :]), selfwait=True)
                op("gpsimd", lambda g: g.tensor_mul(q1[:, :], q1[:, :], q3[:, :]), selfwait=True)
                op("gpsimd", lambda g: g.tensor_scalar(q1[:, :], q1[:, :], -1.0, 0.5, ALU.mult,
                                                       ALU.add), selfwait=True)
                op("gpsimd", lambda g: g.tensor_copy(qi[:, :], q1[:, :]), selfwait=True)
                tg2 = op("gpsimd", lambda g: g.tensor_copy(q2[:, :], qi[:, :]), selfwait=True)
                tv3 = op("vector", lambda v: v.tensor_tensor(q3[:, :], q2[:, :], q1[:, :],
                                                             ALU.is_gt),
                         waits=[tg2], selfwait=True)
                op("gpsimd", lambda g: g.tensor_sub(q2[:, :], q2[:, :], q3[:, :]),
                   waits=[tv3], selfwait=True)
                op("gpsimd", lambda g: g.tensor_mul(q2[:, :], q2[:, :], q4[:, :]), selfwait=True)
                op("gpsimd", lambda g: g.tensor_scalar(q2[:, :], q2[:, :], -1.0, None, ALU.mult),
                   selfwait=True)
                op("gpsimd", lambda g: g.tensor_tensor(
                    q2[:, :].rearrange("p (n m) -> p n m", m=32),
                    q2[:, :].rearrange("p (n m) -> p n m", m=32), sc_b, ALU.mult), selfwait=True)
                return op("gpsimd", lambda g: g.tensor_copy(out_bf_ap, q2[:, :]), selfwait=True)
            return emit_split

        def emit_quant(src_ap, out_bf_ap, waits, sink=None):
            holder = {}
            steps = []

            def op_q(eng_, builder, waits_=(), selfwait=True):
                steps.append(lambda: op(eng_, builder, waits=waits_, selfwait=selfwait))

            src3 = src_ap.rearrange("p (n m) -> p n m", m=32)
            rs_b = qrs[:, :].unsqueeze(2).broadcast_to([128, 32, 32])
            sc_b = qsc[:, :].unsqueeze(2).broadcast_to([128, 32, 32])
            op_q("vector", lambda v: v.reduce_max(qsc[:, :], src3, axis=AX.X,
                                                  apply_absolute_value=True),
                 waits_=waits)
            op_q("vector", lambda v: v.tensor_scalar(qsc[:, :], qsc[:, :], 1.0 / 7.5, 1e-30,
                                                     ALU.mult, ALU.max))
            op_q("vector", lambda v: v.reciprocal(qrs[:, :], qsc[:, :]))
            op_q(eng, lambda e: e.tensor_tensor(q1[:, :].rearrange("p (n m) -> p n m", m=32),
                                                src3, rs_b, ALU.mult))
            op_q(eng, lambda e: e.tensor_mul(q2[:, :], q1[:, :], q1[:, :]))
            op_q(eng, lambda e: e.tensor_scalar(q3[:, :], q2[:, :], 4.0, None, ALU.is_ge))
            op_q(eng, lambda e: e.tensor_scalar(q2[:, :], q2[:, :], 16.0, None, ALU.is_ge))
            op_q(eng, lambda e: e.tensor_scalar(q4[:, :], q3[:, :], 0.125, 0.125, ALU.mult,
                                                ALU.add))
            op_q(eng, lambda e: e.scalar_tensor_tensor(q4[:, :], q2[:, :], 0.25, q4[:, :],
                                                       ALU.mult, ALU.add))
            op_q(eng, lambda e: e.tensor_scalar(q3[:, :], q3[:, :], -4.0, 8.0, ALU.mult,
                                                ALU.add))
            op_q(eng, lambda e: e.scalar_tensor_tensor(q3[:, :], q2[:, :], -2.0, q3[:, :],
                                                       ALU.mult, ALU.add))
            op_q(eng, lambda e: e.tensor_mul(q1[:, :], q1[:, :], q3[:, :]))
            op_q(eng, lambda e: e.tensor_scalar(q1[:, :], q1[:, :], -1.0, 0.5, ALU.mult,
                                                ALU.add))
            op_q(eng, lambda e: e.tensor_copy(qi[:, :], q1[:, :]))
            op_q(eng, lambda e: e.tensor_copy(q2[:, :], qi[:, :]))
            op_q(eng, lambda e: e.tensor_tensor(q3[:, :], q2[:, :], q1[:, :], ALU.is_gt))
            op_q(eng, lambda e: e.tensor_sub(q2[:, :], q2[:, :], q3[:, :]))
            op_q(eng, lambda e: e.tensor_mul(q2[:, :], q2[:, :], q4[:, :]))
            op_q(eng, lambda e: e.scalar_tensor_tensor(
                q2[:, :].rearrange("p (n m) -> p n m", m=32),
                q2[:, :].rearrange("p (n m) -> p n m", m=32), -1.0, sc_b,
                ALU.mult, ALU.mult))

            def _final():
                holder["tok"] = op(eng, lambda e: e.tensor_copy(out_bf_ap, q2[:, :]),
                                   selfwait=True)
            steps.append(_final)
            if sink is None:
                for s_ in steps:
                    s_()
                return holder["tok"]
            sink.extend(steps)
            return holder
        return emit_quant

    quant_g = make_quant(a_q1, a_q2, a_q3, a_q4, a_qi, a_qsc, a_qrs)

    tok_xfm = None
    for c in range(NDC):
        tok_xfm = dma("scalar", x_fm[:, c, :], s_xbf[:, c * 128:(c + 1) * 128],
                      waits=[tok_xbf], slot="xfm", transpose=True)

    XB_TILES = [(0, 512), (512, 512), (1024, 512), (1536, 512), (2048, 3)]
    Z_TILES = [(3, 512), (515, 512), (1027, 512), (1539, 512)]

    a_slab_free = [[], []]
    u32_free = [[], []]
    g_free = [[], []]
    xbr_free = []
    acc_free = []
    ubf_last = None
    u32_wr = [None] * NCH
    g_wr = [None] * NCH
    wqf_free = [[], []]
    wq_holders = []
    wq_pending = []
    slab_tok_a = [None] * 32

    def emit_a_slab(c):
        sb = c % 2
        slab_tok_a[c] = dma("gpsimd",
                            a_slab[sb][:, :, :].rearrange("p s j -> p (s j)"),
                            d_win[c], waits=a_slab_free[sb] + [tok_consts],
                            slot=f"aslab{sb}")
        a_slab_free[sb] = []

    emit_a_slab(0)

    for c in range(32):
        sb = c % 2
        if c + 1 < 32:
            emit_a_slab(c + 1)
        tiles = XB_TILES if c < 16 else Z_TILES
        evac_toks = []
        for ti, (off, w) in enumerate(tiles):
            bk = abank % 6
            abank += 1
            mm_group([a_slab[sb][:, s_, :] for s_ in range(NDC)],
                     [x_fm[:, s_, off:off + w] for s_ in range(NDC)],
                     bk, waits=[slab_tok_a[c], tok_xfm],
                     psum_ap=pbank[bk][:, 0:w])
            if c < 16:
                etok = op("scalar", lambda s, bk=bk, off=off, w=w, c=c: s.activation(
                    t_xbr[:, off:off + w], pbank[bk][:, 0:w], AF.Identity,
                    bias=t_inb[:, c:c + 1]),
                    waits=[bank_w[bk]] + xbr_free)
            else:
                etok = op("scalar", lambda s, bk=bk, off=off, w=w, c=c, sb=sb: s.activation(
                    gb[sb][:, off - 3:off - 3 + w], pbank[bk][:, 0:w], AF.Sigmoid,
                    bias=t_inb[:, c:c + 1]),
                    waits=[bank_w[bk]] + g_free[sb])
            bank_r[bk].append(etok)
            evac_toks.append(etok)
        a_slab_free[sb].append(bank_w[(abank - 1) % 6])
        if c < 16:
            op("vector", lambda v: v.tensor_scalar(
                t_xbr[:, 0:HALO], t_xbr[:, 0:HALO], t_halo[:, 0:1], None, ALU.mult),
               waits=evac_toks + [tok_negA] + acc_free + u32_free[sb] + xbr_free, selfwait=True)
            xbr_free = []
            acc_free = []
            u32_free[sb] = []
            op("vector", lambda v, c=c: v.tensor_scalar(
                t_acc[:, :], t_xbr[:, 0:T], t_cwv[:, 4 * c:4 * c + 1], None, ALU.mult))
            for k in range(1, 4):
                op("vector", lambda v, c=c, k=k: v.scalar_tensor_tensor(
                    t_acc[:, :], t_xbr[:, k:k + T], t_cwv[:, 4 * c + k:4 * c + k + 1],
                    t_acc[:, :], ALU.mult, ALU.add),
                   selfwait=(k == 3))
            tok_conv = ("vector", cnt["vector"])
            xbr_free.append(tok_conv)
            tok_silu = op("scalar", lambda s, c=c, sb=sb: s.activation(
                u32b[sb][:, :], t_acc[:, :], AF.Silu, bias=t_cvb[:, c:c + 1]), waits=[tok_conv])
            acc_free.append(tok_silu)
            ubf_last = op("scalar", lambda s, c=c, sb=sb: s.copy(u_bf[:, c, :], u32b[sb][:, :]),
                          waits=[tok_silu])
            u32_free[sb].append(ubf_last)
            u32_wr[c] = dma("sync", s_u32[c], u32b[sb][:, :], waits=[ubf_last], slot=f"u32s{sb}")
            u32_free[sb].append(u32_wr[c])
        else:
            cz = c - 16
            g_wr[cz] = dma("sync", s_g[cz], gb[sb][:, :], waits=evac_toks, slot=f"gs{sb}")
            g_free[sb] = [g_wr[cz]]
            if cz % 2 == 0:
                i = cz // 2
                tok_wl = dma("sync", wqf[i % 2][:, :], d_fw[i * 128:(i + 1) * 128, :],
                             waits=wqf_free[i % 2], slot=f"wqf{i % 2}")
                wqf_free[i % 2] = []
                wq_holders.append({"tok": quant_g(wqf[i % 2][:, :], wq_bf[:, i, :],
                                                  [tok_wl])})
                wqf_free[i % 2] = [wq_holders[-1]["tok"]]

    while wq_pending:
        wq_pending.pop(0)()
    wq_toks = [h["tok"] for h in wq_holders]

    bf_rd = []
    wd_wr = None
    for j in range(NDC):
        w = mkwait(wq_toks + bf_rd + [tok_smalls])
        bf_rd = []
        tok_tp = alloc("tensor")

        def body(t, w=w, j=j):
            do_wait(t, w)
            for i in range(NDC):
                ins = t.transpose(p_bf[:, i * 128:(i + 1) * 128],
                                  wq_bf[:, i, j * 128:(j + 1) * 128], t_eye[:, :])
            ins.then_inc(sem["tensor"], 1)
        emit("tensor", body)
        tok_ev = op("scalar", lambda s: s.copy(wdst[:, :], p_bf[:, :]), waits=[tok_tp])
        bf_rd.append(tok_ev)
        wd_wr = dma("gpsimd", s_wd[:, j * 1024:(j + 1) * 1024], wdst[:, :], waits=[tok_ev], slot="wds")
        bf_rd.append(wd_wr)

    barrierA = barrier_now()
    esA.close()

    esC = contextlib.ExitStack()
    c_slab = {wn: [esC.enter_context(nc.sbuf_tensor(f"cs_{wn}{i}", [128, NCH, 128], BF16))
                   for i in range(2)] for wn in ("dt", "bw", "cw")}
    u32p = [esC.enter_context(nc.sbuf_tensor(f"u32p{i}", [128, T], F32)) for i in range(2)]
    gp = [esC.enter_context(nc.sbuf_tensor(f"gp{i}", [128, T], F32)) for i in range(2)]
    t_px = [esC.enter_context(nc.sbuf_tensor(f"px{i}", [128, T], F32)) for i in range(2)]
    t_dec = [esC.enter_context(nc.sbuf_tensor(f"dec{i}", [128, T], F32)) for i in range(2)]
    t_B = [esC.enter_context(nc.sbuf_tensor(f"tB{i}", [128, T], F32)) for i in range(2)]
    t_C = [esC.enter_context(nc.sbuf_tensor(f"tC{i}", [128, T], F32)) for i in range(2)]
    t_st = esC.enter_context(nc.sbuf_tensor("st", [128, T], F32))
    t_ybf = esC.enter_context(nc.sbuf_tensor("ybf", [128, T], BF16))
    t_scol = esC.enter_context(nc.sbuf_tensor("scol", [128, 1], F32))
    t_sini = esC.enter_context(nc.sbuf_tensor("sini", [128, 1], F32))

    dws = {"dt": d_dtw, "bw": d_bw, "cw": d_cw}
    cs_free = {wn: [[], []] for wn in ("dt", "bw", "cw")}
    u32p_free = [[], []]
    gp_free = [[], []]
    px_free = [[], []]
    decf = [[], []]
    Bf = [[], []]
    Cf = [[], []]
    st_free, ybf_free, sini_free, scol_free = [], [], [], []
    ybf_wr = [None] * NCH
    slab_tok_c = [None] * NCH
    cbank = 0
    saved = {}

    def emit_c_slabs(c, first=False):
        sb = c % 2
        toks = {}
        for wn in ("dt", "bw", "cw"):
            toks[wn] = dma("gpsimd", c_slab[wn][sb][:, :, :].rearrange("p s j -> p (s j)"),
                           dws[wn][c], waits=cs_free[wn][sb] + (barrierA if first else []),
                           slot=f"c{wn}{sb}")
            cs_free[wn][sb] = []
        slab_tok_c[c] = toks

    emit_c_slabs(0, first=True)
    ow_v = u_bf[:, 0:8, :].rearrange("p c j -> p (c j)")
    wd_v = u_bf[:, 8:12, :].rearrange("p c j -> p (c j)")
    tok_ow = None

    def emit_deferred(c):
        nonlocal st_free, ybf_free
        if c < 0:
            return
        sb = c % 2
        sv = saved[c]
        op("vector", lambda v: v.tensor_scalar(t_sini[:, :], t_sini[:, :], t_halo[:, 0:1],
                                               None, ALU.mult),
           waits=[sv["recv"]] + st_free, selfwait=True)
        st_free = []
        op("vector", lambda v, sb=sb: v.tensor_tensor_scan(
            t_st[:, :], t_dec[sb][:, :], t_B[sb][:, :], t_sini[:, 0:1], ALU.mult, ALU.add),
           selfwait=True)
        sini_free.append(("vector", cnt["vector"]))
        op("vector", lambda v, sb=sb: v.tensor_mul(t_st[:, :], t_C[sb][:, :], t_st[:, :]),
           waits=sv["C_evacs"])
        op("vector", lambda v, sb=sb, c=c: v.scalar_tensor_tensor(
            t_st[:, :], u32p[sb][:, :], t_dpar[:, c:c + 1], t_st[:, :], ALU.mult, ALU.add))
        tok_y = op("vector", lambda v, sb=sb: v.tensor_mul(t_st[:, :], t_st[:, :], gp[sb][:, :]),
                   waits=[sv["pf"]], selfwait=True)
        decf[sb].append(tok_y)
        Bf[sb].append(tok_y)
        Cf[sb].append(tok_y)
        u32p_free[sb].append(tok_y)
        gp_free[sb].append(tok_y)
        tok_ybf = op("scalar", lambda s: s.copy(t_ybf[:, :], t_st[:, :]),
                     waits=[tok_y] + ybf_free)
        st_free.append(tok_ybf)
        ybf_wr[c] = dma("sync", s_ybf[c], t_ybf[:, :], waits=[tok_ybf], slot="ybfs")
        ybf_free = [ybf_wr[c]]
        dma("sync", ysl01[0][:, c, :], s_ybf[c, :, 0:128], waits=[ybf_wr[c]], slot="y0f")

    for c in range(NCH):
        sb = c % 2
        if c + 1 < NCH:
            emit_c_slabs(c + 1)
        tok_pf_u = dma("sync", u32p[sb][:, :], s_u32[c],
                       waits=u32p_free[sb] + [u32_wr[c]] + (barrierA if c == 0 else []),
                       slot=f"u32p{sb}")
        u32p_free[sb] = []
        tok_pf = dma("sync", gp[sb][:, :], s_g[c], waits=gp_free[sb] + [g_wr[c]], slot=f"gp{sb}")
        gp_free[sb] = []

        dt_evacs = []
        for ti in range(4):
            bk = cbank % 6
            cbank += 1
            mm_group([c_slab["dt"][sb][:, s_, :] for s_ in range(NCH)],
                     [u_bf[:, s_, ti * 512:(ti + 1) * 512] for s_ in range(NCH)],
                     bk, waits=[slab_tok_c[c]["dt"], ubf_last])
            etok = op("scalar", lambda s, bk=bk, ti=ti, c=c, sb=sb: s.activation(
                t_px[sb][:, ti * 512:(ti + 1) * 512], pbank[bk][:, :], AF.Identity,
                bias=t_dtb[:, c:c + 1]), waits=[bank_w[bk]] + px_free[sb])
            bank_r[bk].append(etok)
            dt_evacs.append(etok)
            if ti == 3:
                px_free[sb] = []

        B_evacs = []
        C_evacs = []
        for ti in range(4):
            bk = cbank % 6
            cbank += 1
            mm_group([c_slab["bw"][sb][:, s_, :] for s_ in range(NCH)],
                     [u_bf[:, s_, ti * 512:(ti + 1) * 512] for s_ in range(NCH)],
                     bk, waits=[slab_tok_c[c]["bw"]])
            etok = op("scalar", lambda s, bk=bk, ti=ti, sb=sb: s.activation(
                t_B[sb][:, ti * 512:(ti + 1) * 512], pbank[bk][:, :], AF.Tanh),
                waits=[bank_w[bk]] + Bf[sb])
            bank_r[bk].append(etok)
            B_evacs.append(etok)
            if ti == 3:
                Bf[sb] = []
        for ti in range(4):
            bk = cbank % 6
            cbank += 1
            mm_group([c_slab["cw"][sb][:, s_, :] for s_ in range(NCH)],
                     [u_bf[:, s_, ti * 512:(ti + 1) * 512] for s_ in range(NCH)],
                     bk, waits=[slab_tok_c[c]["cw"]])
            etok = op("scalar", lambda s, bk=bk, ti=ti, sb=sb: s.activation(
                t_C[sb][:, ti * 512:(ti + 1) * 512], pbank[bk][:, :], AF.Tanh),
                waits=[bank_w[bk]] + Cf[sb])
            bank_r[bk].append(etok)
            C_evacs.append(etok)
            if ti == 3:
                Cf[sb] = []
        last_mm = ("tensor", cnt["tensor"])
        for wn in ("dt", "bw", "cw"):
            cs_free[wn][sb] = [last_mm]
        if c == NCH - 1:
            tok_ow = dma("gpsimd", ow_v, d_ow, waits=[last_mm], slot="ow")

        op("scalar", lambda s, sb=sb: s.activation(t_px[sb][:, :], t_px[sb][:, :], AF.Exp),
           waits=dt_evacs + [tok_consts], selfwait=True)
        tok_sp = op("scalar", lambda s, sb=sb: s.activation(t_px[sb][:, :], t_px[sb][:, :],
                                                            AF.Ln, bias=1.0), selfwait=True)
        tok_dec = op("scalar", lambda s, c=c, sb=sb: s.activation(
            t_dec[sb][:, :], t_px[sb][:, :], AF.Exp, bias=t_negAe[:, c:c + 1],
            scale=t_negA[:, c:c + 1]), waits=[tok_sp, tok_negA] + decf[sb])
        decf[sb] = []

        op("vector", lambda v, sb=sb: v.tensor_mul(t_B[sb][:, :], t_B[sb][:, :], u32p[sb][:, :]),
           waits=[tok_dec, tok_pf_u] + B_evacs)
        op("vector", lambda v, sb=sb: v.tensor_scalar(t_px[sb][:, :], t_dec[sb][:, :], -1.0, 1.0,
                                                      ALU.mult, ALU.add))
        op("vector", lambda v, sb=sb: v.tensor_mul(t_B[sb][:, :], t_px[sb][:, :], t_B[sb][:, :]),
           selfwait=True)
        px_free[sb].append(("vector", cnt["vector"]))
        op("vector", lambda v, sb=sb: v.tensor_tensor_scan(
            t_st[:, :], t_dec[sb][:, :], t_B[sb][:, :], 0.0, ALU.mult, ALU.add),
           waits=st_free, selfwait=True)
        st_free = []
        tok_scol = op("vector", lambda v: v.tensor_scalar(
            t_scol[:, :], t_st[:, T - 1:T], t_evm[:, 0:1], None, ALU.mult),
            waits=scol_free, selfwait=True)
        st_free.append(tok_scol)

        tok_ccin = dma("gpsimd", d_ccin[c], t_scol[:, :], waits=[tok_scol], slot="ccin")
        scol_free = [tok_ccin]

        emit_deferred(c - 1)

        def cc_body(g, c=c, w=mkwait([tok_ccin])):
            do_wait(g, w)
            g.collective_compute(
                "AllReduce", ALU.add,
                replica_groups=[[0, 1], [2, 3], [4, 5], [6, 7]],
                ins=[d_ccin[c]], outs=[d_ccout[c]],
            ).then_inc(ccsem[c])
        emit("gpsimd", cc_body)

        w_recv = mkwait(sini_free)
        sini_free = []
        if "recv" not in slot_sems:
            slot_sems["recv"] = es.enter_context(nc.semaphore("ds_recv"))
            slot_gen["recv"] = 0
        slot_gen["recv"] += 1
        tok_recv = ("S", "recv", slot_gen["recv"] * 16)

        def recv_body(g, c=c, w_recv=w_recv):
            g.wait_ge(ccsem[c], 1)
            do_wait(g, w_recv)
            g.dma_start(out=t_sini[:, :], in_=d_ccout[c]).then_inc(slot_sems["recv"], 16)
        emit("gpsimd", recv_body)

        saved[c] = {"recv": tok_recv, "C_evacs": C_evacs, "pf": tok_pf}

    ubf_dead = ("tensor", cnt["tensor"])
    ubf_ysl = [u_bf[:, 12 + p, :].rearrange("p (c j) -> p c j", j=128) for p in range(2)]

    def ysl_view(t_):
        return ysl01[0] if t_ == 0 else ubf_ysl[t_ % 2]
    tok_wd = dma("sync", wd_v, s_wd, waits=[ubf_dead, wd_wr], slot="wdl")

    emit_deferred(NCH - 1)

    barrierC = barrier_now()
    esC.close()

    esD = contextlib.ExitStack()
    xtm = [esD.enter_context(nc.sbuf_tensor(f"xtm{i}", [128, 1024], F32)) for i in range(2)]
    xb = [esD.enter_context(nc.sbuf_tensor(f"xb{i}", [128, 1024], F32)) for i in range(2)]
    t_hb = [esD.enter_context(nc.sbuf_tensor(f"thb{i}", [128, 1024], F32)) for i in range(2)]
    t_h = [esD.enter_context(nc.sbuf_tensor(f"th{i}", [128, 1024], F32)) for i in range(2)]
    d_q1 = esD.enter_context(nc.sbuf_tensor("dq1", [128, 1024], F32))
    d_q2 = esD.enter_context(nc.sbuf_tensor("dq2", [128, 1024], F32))
    d_q3 = esD.enter_context(nc.sbuf_tensor("dq3", [128, 1024], F32))
    d_q4 = esD.enter_context(nc.sbuf_tensor("dq4", [128, 1024], F32))
    d_qi = esD.enter_context(nc.sbuf_tensor("dqi", [128, 1024], I32))
    d_sc = esD.enter_context(nc.sbuf_tensor("dsc", [128, 32], F32))
    d_rs = esD.enter_context(nc.sbuf_tensor("drs", [128, 32], F32))
    t_mv = esD.enter_context(nc.sbuf_tensor("mv", [128, 8], F32))
    t_xdbf = esD.enter_context(nc.sbuf_tensor("xdbf", [128, 1024], BF16))
    t_xdT = esD.enter_context(nc.sbuf_tensor("xdT", [128, 1024], BF16))
    t_outb = esD.enter_context(nc.sbuf_tensor("outbt", [128, 1024], F32))
    t_ffnb = esD.enter_context(nc.sbuf_tensor("ffnbt", [128, 1024], F32))
    obuf = [esD.enter_context(nc.sbuf_tensor(f"obuf{i}", [128, 1024], F32)) for i in range(2)]

    e_q1 = esD.enter_context(nc.sbuf_tensor("eq1", [128, 1024], F32))
    e_q2 = esD.enter_context(nc.sbuf_tensor("eq2", [128, 1024], F32))
    e_q3 = esD.enter_context(nc.sbuf_tensor("eq3", [128, 1024], F32))
    e_q4 = esD.enter_context(nc.sbuf_tensor("eq4", [128, 1024], F32))
    e_qi = esD.enter_context(nc.sbuf_tensor("eqi", [128, 1024], I32))
    e_sc = esD.enter_context(nc.sbuf_tensor("esc", [128, 32], F32))
    e_rs = esD.enter_context(nc.sbuf_tensor("ers", [128, 32], F32))
    quant_d = make_quant(d_q1, d_q2, d_q3, d_q4, d_qi, d_sc, d_rs)

    dma("sync", t_outb[:, :], d_outb, waits=barrierC, slot="dbias")
    tok_bias = dma("sync", t_ffnb[:, :], d_ffnb, slot="dbias")

    ysl_free = [[], []]
    xtm_free = [[], []]
    xb_free = [[], []]
    hb_free = []
    obuf_free = [[], []]
    h_free, q_free = [], []
    xd_free, xdT_free = [], []
    out_toks = []
    ysl_tok = [None] * 16
    xtm_tok = [None] * 16
    ysl_tok[0] = ("S", "y0f", 16 * NCH)
    ysl_tok[1] = dma("scalar", ysl_view(1),
                     s_ybf[:, :, 128:256].rearrange("c p j -> p c j"),
                     waits=list(ybf_wr) + [ubf_dead], slot="ysl1")
    for tt in range(2):
        xtm_tok[tt] = dma("sync", xtm[tt % 2][:, :],
                          d_x[HALO + tt * 128: HALO + (tt + 1) * 128, :], waits=barrierC,
                          slot=f"xtm{tt % 2}")

    op_banks = [None] * 16
    op_w = [None] * 16

    def emit_op_mm(tau):
        sb = tau % 2
        bks = []
        for hh in range(2):
            bk = 2 * (tau % 2) + hh
            bks.append(bk)
            mm_group([ysl_view(tau)[:, cc, :] for cc in range(NCH)],
                     [u_bf[:, cc // 2, (cc % 2) * 1024 + hh * 512:(cc % 2) * 1024 + (hh + 1) * 512]
                      for cc in range(NCH)],
                     bk, waits=[ysl_tok[tau], tok_ow])
        op_banks[tau] = tuple(bks)
        op_w[tau] = ("tensor", cnt["tensor"])
        ysl_free[sb].append(op_w[tau])

    emit_op_mm(0)
    emit_op_mm(1)

    h_free = [[], []]
    hb_free = [[], []]
    eq_free = []
    mv1_free = []
    mv2_free = []
    back_state = {}

    def emit_back(tau, qsink=None):
        nonlocal eq_free, mv2_free

        def qpop(n=2):
            for _ in range(n):
                if qsink:
                    qsink.pop(0)()
        if tau < 0:
            return
        st = back_state.pop(tau)
        sbb = st["sb"]
        fv = []
        for hh in range(2):
            bk = st["f_banks"][hh]
            etok = op("vector", lambda v, bk=bk, hh=hh, sbb=sbb: v.tensor_add(
                obuf[sbb][:, hh * 512:(hh + 1) * 512], pbank[bk][:, :],
                t_hb[sbb][:, hh * 512:(hh + 1) * 512]),
               waits=[st["f_w"], st["tok_hb"]] + (obuf_free[sbb] if hh == 0 else []))
            if hh == 0:
                obuf_free[sbb] = []
            bank_r[bk].append(etok)
            fv.append(etok)
            qpop(1 if hh == 0 else 2)
        hb_free[sbb] = [fv[-1]]
        tok_sq2 = op("scalar", lambda s, sbb=sbb: s.activation(e_q1[:, :], obuf[sbb][:, :],
                                                               AF.Square),
                     waits=fv + eq_free, selfwait=True)
        eq_free = []
        op("vector", lambda v, sbb=sbb: v.reduce_sum(t_mv[:, 4:5], obuf[sbb][:, :], axis=AX.X),
           waits=fv + mv2_free, selfwait=True)
        mv2_free = []
        qpop(2)
        tok_nm2 = op("vector", lambda v: v.tensor_scalar(t_mv[:, 4:5], t_mv[:, 4:5],
                                                         -1.0 / 1024.0, None, ALU.mult),
                     selfwait=True)
        qpop(2)
        op("vector", lambda v: v.reduce_sum(t_mv[:, 5:6], e_q1[:, :], axis=AX.X),
           waits=[tok_sq2], selfwait=True)
        qpop(1)
        op("vector", lambda v: v.tensor_mul(t_mv[:, 7:8], t_mv[:, 4:5], t_mv[:, 4:5]),
           selfwait=True)
        tok_var2 = op("vector", lambda v: v.scalar_tensor_tensor(
            t_mv[:, 5:6], t_mv[:, 5:6], 1.0 / 1024.0, t_mv[:, 7:8],
            ALU.mult, ALU.subtract), selfwait=True)
        eq_free.append(tok_var2)
        qpop(2)
        op("scalar", lambda s: s.activation(t_mv[:, 5:6], t_mv[:, 5:6], AF.Ln, bias=1e-5),
           waits=[tok_var2], selfwait=True)
        op("scalar", lambda s: s.activation(t_mv[:, 5:6], t_mv[:, 5:6], AF.Exp,
                                            scale=-0.5), selfwait=True)
        op("scalar", lambda s: s.activation(t_mv[:, 6:7], t_mv[:, 4:5], AF.Identity,
                                            scale=t_mv[:, 5:6]), selfwait=True)
        tok_o = op("scalar", lambda s, sbb=sbb: s.activation(
            obuf[sbb][:, :], obuf[sbb][:, :], AF.Identity,
            scale=t_mv[:, 5:6], bias=t_mv[:, 6:7]), selfwait=True)
        mv2_free = [tok_o]
        otok = dma("sync", d_out[tau * 128:(tau + 1) * 128, :], obuf[sbb][:, :],
                   waits=[tok_o], slot=f"out{sbb}")
        out_toks.append(otok)
        obuf_free[sbb].append(otok)

    for tau in range(16):
        sb = tau % 2
        if tau + 2 < 16:
            tt = tau + 2
            w_extra = [ubf_dead] if tt < 4 else []
            ysl_tok[tt] = dma("scalar", ysl_view(tt),
                              s_ybf[:, :, tt * 128:(tt + 1) * 128].rearrange("c p j -> p c j"),
                              waits=ysl_free[tt % 2] + w_extra, slot=f"ysl{tt % 2}")
            ysl_free[tt % 2] = []

        tok_xb = op("gpsimd", lambda g, sb=sb: g.tensor_add(xb[sb][:, :], xtm[sb][:, :],
                                                            t_outb[:, :]),
                    waits=[xtm_tok[tau], tok_bias] + xb_free[sb], selfwait=True)
        xtm_free[sb].append(tok_xb)
        ev = []
        for hh in range(2):
            bk = op_banks[tau][hh]
            etok = op("vector", lambda v, bk=bk, hh=hh, sb=sb: v.tensor_add(
                t_h[sb][:, hh * 512:(hh + 1) * 512], pbank[bk][:, :],
                xb[sb][:, hh * 512:(hh + 1) * 512]),
               waits=[op_w[tau], tok_xb] + (h_free[sb] if hh == 0 else []))
            if hh == 0:
                h_free[sb] = []
            bank_r[bk].append(etok)
            ev.append(etok)
        xb_free[sb] = [ev[-1]]
        if tau + 2 < 16:
            tt = tau + 2
            xtm_tok[tt] = dma("sync", xtm[tt % 2][:, :],
                              d_x[HALO + tt * 128: HALO + (tt + 1) * 128, :],
                              waits=xtm_free[tt % 2], slot=f"xtm{tt % 2}")
            xtm_free[tt % 2] = []

        tok_sq = op("scalar", lambda s, sb=sb: s.activation(d_q1[:, :], t_h[sb][:, :], AF.Square),
                    waits=ev + q_free, selfwait=True)
        q_free = []
        op("vector", lambda v, sb=sb: v.reduce_sum(t_mv[:, 0:1], t_h[sb][:, :], axis=AX.X),
           waits=ev + mv1_free, selfwait=True)
        mv1_free = []
        tok_nm = op("vector", lambda v: v.tensor_scalar(t_mv[:, 0:1], t_mv[:, 0:1],
                                                        -1.0 / 1024.0, None, ALU.mult),
                    selfwait=True)
        op("vector", lambda v: v.reduce_sum(t_mv[:, 1:2], d_q1[:, :], axis=AX.X),
           waits=[tok_sq], selfwait=True)
        op("vector", lambda v: v.tensor_mul(t_mv[:, 3:4], t_mv[:, 0:1], t_mv[:, 0:1]),
           selfwait=True)
        tok_var = op("vector", lambda v: v.scalar_tensor_tensor(
            t_mv[:, 1:2], t_mv[:, 1:2], 1.0 / 1024.0, t_mv[:, 3:4],
            ALU.mult, ALU.subtract), selfwait=True)
        op("scalar", lambda s: s.activation(t_mv[:, 1:2], t_mv[:, 1:2], AF.Ln, bias=1e-5),
           waits=[tok_var, tok_consts], selfwait=True)
        op("scalar", lambda s: s.activation(t_mv[:, 1:2], t_mv[:, 1:2], AF.Exp,
                                            scale=-0.5), selfwait=True)
        op("scalar", lambda s: s.activation(t_mv[:, 2:3], t_mv[:, 0:1], AF.Identity,
                                            scale=t_mv[:, 1:2]), selfwait=True)
        tok_ln1 = op("scalar", lambda s, sb=sb: s.activation(
            t_h[sb][:, :], t_h[sb][:, :], AF.Identity,
            scale=t_mv[:, 1:2], bias=t_mv[:, 2:3]), selfwait=True)
        mv1_free = [tok_ln1]

        tok_hb = op("gpsimd", lambda g, sb=sb: g.tensor_add(t_hb[sb][:, :], t_h[sb][:, :],
                                                            t_ffnb[:, :]),
                    waits=[tok_ln1, tok_bias] + hb_free[sb], selfwait=True)

        if debug and tau == 0:
            tok_ln1 = dma("sync", d_dbg[1], t_h[sb][:, :], waits=[tok_ln1], slot="dbg")
        qsink = []
        q_holder = quant_d(t_h[sb][:, :], t_xdbf[:, :], [tok_ln1] + xd_free, sink=qsink)
        emit_back(tau - 1, qsink)
        while qsink:
            qsink.pop(0)()
        tok_qd = q_holder["tok"]
        xd_free = []
        q_free.append(tok_qd)
        h_free[sb] = [tok_hb, tok_qd]

        w = mkwait([tok_qd, tok_smalls] + bf_rd)
        bf_rd = []
        tok_tp = alloc("tensor")

        def tp_body(t, w=w):
            do_wait(t, w)
            for k in range(NDC):
                ins = t.transpose(p_bf[:, k * 128:(k + 1) * 128],
                                  t_xdbf[:, k * 128:(k + 1) * 128], t_eye[:, :])
            ins.then_inc(sem["tensor"], 1)
        emit("tensor", tp_body)
        xd_free.append(tok_tp)
        tok_xdT = op("scalar", lambda s: s.copy(t_xdT[:, :], p_bf[:, :]),
                     waits=[tok_tp] + xdT_free)
        xdT_free = []
        bf_rd.append(tok_xdT)

        f_banks = []
        for hh in range(2):
            bk = 4 + hh
            f_banks.append(bk)
            mm_group([t_xdT[:, k * 128:(k + 1) * 128] for k in range(NDC)],
                     [u_bf[:, 8 + k // 2, (k % 2) * 1024 + hh * 512:(k % 2) * 1024 + (hh + 1) * 512]
                      for k in range(NDC)],
                     bk, waits=[tok_xdT, tok_wd])
        f_w = ("tensor", cnt["tensor"])
        xdT_free.append(f_w)
        back_state[tau] = {"sb": sb, "f_banks": tuple(f_banks), "f_w": f_w, "tok_hb": tok_hb}

        if tau + 2 < 16:
            emit_op_mm(tau + 2)

    emit_back(15)

    wfin = mkwait(out_toks)

    def fin(q, wfin=wfin):
        do_wait(q, wfin)
    emit("sync", fin)

    esD.close()

    with es:
        with nc.Block() as block:
            block.sync(lambda q: [f(q) for f in streams["sync"]])
            block.gpsimd(lambda q: [f(q) for f in streams["gpsimd"]])
            block.scalar(lambda q: [f(q) for f in streams["scalar"]])
            block.vector(lambda q: [f(q) for f in streams["vector"]])
            block.tensor(lambda q: [f(q) for f in streams["tensor"]])
    return nc


_NC_CACHE = {}


def _get_nc():
    if "nc" not in _NC_CACHE:
        _NC_CACHE["nc"] = build_nc()
    return _NC_CACHE["nc"]


def kernel(L0_attn_x, L0_attn_in_proj_w, L0_attn_in_proj_b, L0_attn_conv_w, L0_attn_conv_b,
           L0_attn_dt_w, L0_attn_dt_b, L0_attn_b_w, L0_attn_c_w, L0_attn_A_log,
           L0_attn_D_param, L0_attn_out_w, L0_attn_out_b, L0_ffn_weight, L0_ffn_bias):
    x = np.ascontiguousarray(np.asarray(L0_attn_x, dtype=np.float32))
    in_w = np.asarray(L0_attn_in_proj_w, dtype=np.float32)
    in_b = np.asarray(L0_attn_in_proj_b, dtype=np.float32)
    conv_w = np.asarray(L0_attn_conv_w, dtype=np.float32)
    conv_b = np.asarray(L0_attn_conv_b, dtype=np.float32)
    dt_w = np.asarray(L0_attn_dt_w, dtype=np.float32)
    dt_b = np.asarray(L0_attn_dt_b, dtype=np.float32)
    b_w = np.asarray(L0_attn_b_w, dtype=np.float32)
    c_w = np.asarray(L0_attn_c_w, dtype=np.float32)
    a_log = np.asarray(L0_attn_A_log, dtype=np.float32)
    d_par = np.asarray(L0_attn_D_param, dtype=np.float32)
    out_w = np.asarray(L0_attn_out_w, dtype=np.float32)
    out_b = np.asarray(L0_attn_out_b, dtype=np.float32)
    ffn_w = np.asarray(L0_ffn_weight, dtype=np.float32)
    ffn_b = np.asarray(L0_ffn_bias, dtype=np.float32)

    win = np.ascontiguousarray(in_w.reshape(32, 128, 8, 128).transpose(0, 3, 2, 1)
                               ).reshape(32, 128, 1024)

    def big_t(w):
        return np.ascontiguousarray(w.reshape(16, 128, 16, 128).transpose(0, 3, 2, 1)).reshape(16, 128, 2048)
    dtw = big_t(dt_w)
    bw = big_t(b_w)
    cw = big_t(c_w)
    ow = np.ascontiguousarray(out_w.T.reshape(16, 128, 1024).transpose(1, 0, 2)).reshape(128, 16384)
    inb = np.ascontiguousarray(in_b.reshape(32, 128).T)
    cwv = np.ascontiguousarray(conv_w[:, 0, :].reshape(16, 128, 4).transpose(1, 0, 2)).reshape(128, 64)
    dtb = np.ascontiguousarray(dt_b.reshape(16, 128).T)
    cvb = np.ascontiguousarray(conv_b.reshape(16, 128).T)
    alog = np.ascontiguousarray(a_log.reshape(16, 128).T)
    dpar = np.ascontiguousarray(d_par.reshape(16, 128).T)
    outbt = np.ascontiguousarray(np.broadcast_to(out_b[None, :], (128, 1024)))
    ffnbt = np.ascontiguousarray(np.broadcast_to(ffn_b[None, :], (128, 1024)))
    eye = np.eye(128).astype(ml_dtypes.bfloat16)

    in_maps = make_in_maps(x, win, dtw, bw, cw, ow, ffn_w, inb, cwv, cvb, dtb, alog,
                           dpar, outbt, ffnbt, eye)
    return run_all(in_maps)


def make_in_maps(x, win, dtw, bw, cw, ow, ffn_w, inb, cwv, cvb, dtb, alog, dpar,
                 outbt, ffnbt, eye):
    in_maps = []
    for c in range(8):
        b, half = c // 2, c % 2
        xs = np.zeros((TPAD, D), np.float32)
        if half == 1:
            xs[0:HALO] = x[b, T - HALO:T]
        xs[HALO:HALO + T] = x[b, half * T:(half + 1) * T]
        halo = np.zeros((128, 1), np.float32)
        if half == 1:
            halo[:, 0] = 1.0
        in_maps.append({
            "x": xs, "win": win, "dtw": dtw, "bw": bw, "cw": cw, "ow": ow, "fw": ffn_w,
            "inb": inb, "cwv": cwv, "cvb": cvb, "dtb": dtb, "alog": alog, "dpar": dpar,
            "outb": outbt, "ffnb": ffnbt, "eye": eye, "halo": halo,
        })
    return in_maps


_LAST_RES = None


def run_all(in_maps):
    global _LAST_RES
    nc = _get_nc()
    res = run_bass_kernel_spmd(nc, in_maps, core_ids=list(range(8)))
    _LAST_RES = res
    out = np.empty((4, 4096, 1024), np.float32)
    for c in range(8):
        b, half = c // 2, c % 2
        out[b, half * T:(half + 1) * T] = res.results[c]["y_out"]
    return out


# version 30
# speedup vs baseline: 1.0557x; 1.0226x over previous
import contextlib
import numpy as np
import ml_dtypes
import concourse.bass as bass
import concourse.mybir as mybir
from concourse.bass_utils import run_bass_kernel_spmd

F32 = mybir.dt.float32
BF16 = mybir.dt.bfloat16
I32 = mybir.dt.int32
AF = mybir.ActivationFunctionType
ALU = mybir.AluOpType
AX = mybir.AxisListType

T = 2048
HALO = 3
TH = T + HALO
TPAD = 2064
D = 1024
DI = 2048
NCH = 16
NDC = 8

ENGS = ("sync", "gpsimd", "scalar", "vector", "tensor")


def build_nc(debug=False):
    nc = bass.Bass("TRN2", target_bir_lowering=False)

    d_x = nc.dram_tensor("x", [TPAD, D], F32, kind="ExternalInput").ap()
    d_win = nc.dram_tensor("win", [32, 128, 1024], F32, kind="ExternalInput").ap()
    d_dtw = nc.dram_tensor("dtw", [16, 128, 2048], F32, kind="ExternalInput").ap()
    d_bw = nc.dram_tensor("bw", [16, 128, 2048], F32, kind="ExternalInput").ap()
    d_cw = nc.dram_tensor("cw", [16, 128, 2048], F32, kind="ExternalInput").ap()
    d_ow = nc.dram_tensor("ow", [128, 16384], F32, kind="ExternalInput").ap()
    d_fw = nc.dram_tensor("fw", [1024, 1024], F32, kind="ExternalInput").ap()
    d_inb = nc.dram_tensor("inb", [128, 32], F32, kind="ExternalInput").ap()
    d_cwv = nc.dram_tensor("cwv", [128, 64], F32, kind="ExternalInput").ap()
    d_cvb = nc.dram_tensor("cvb", [128, 16], F32, kind="ExternalInput").ap()
    d_dtb = nc.dram_tensor("dtb", [128, 16], F32, kind="ExternalInput").ap()
    d_alog = nc.dram_tensor("alog", [128, 16], F32, kind="ExternalInput").ap()
    d_dpar = nc.dram_tensor("dpar", [128, 16], F32, kind="ExternalInput").ap()
    d_outb = nc.dram_tensor("outb", [128, 1024], F32, kind="ExternalInput").ap()
    d_ffnb = nc.dram_tensor("ffnb", [128, 1024], F32, kind="ExternalInput").ap()
    d_eye = nc.dram_tensor("eye", [128, 128], BF16, kind="ExternalInput").ap()
    d_halo = nc.dram_tensor("halo", [128, 1], F32, kind="ExternalInput").ap()
    d_out = nc.dram_tensor("y_out", [T, D], F32, kind="ExternalOutput").ap()

    dk = "Internal"
    if debug:
        d_dbg = nc.dram_tensor("dbg0", [4, 128, 1024], F32, kind="ExternalOutput").ap()
        d_dbg2 = nc.dram_tensor("dbg1", [128, 2048], F32, kind="ExternalOutput").ap()
        d_dbg3 = nc.dram_tensor("dbg2", [128, 1024], F32, kind="ExternalOutput").ap()
    s_xbf = nc.dram_tensor("s_xbf", [TPAD, D], BF16, kind="Internal").ap()
    s_u32 = nc.dram_tensor("s_u32", [NCH, 128, T], F32, kind=dk).ap()
    s_g = nc.dram_tensor("s_g", [NCH, 128, T], F32, kind=dk).ap()
    s_ybf = nc.dram_tensor("s_ybf", [NCH, 128, T], BF16, kind=dk).ap()
    s_wd = nc.dram_tensor("s_wd", [128, 8192], BF16, kind=dk).ap()
    d_ccin = [nc.dram_tensor(f"ccin{c}", [128, 1], F32, kind="Internal").ap() for c in range(NCH)]
    d_ccout = [nc.dram_tensor(f"ccout{c}", [128, 1], F32, kind="Internal").ap() for c in range(NCH)]

    es = contextlib.ExitStack()

    sem = {e: es.enter_context(nc.semaphore(f"sm_{e}")) for e in ENGS}
    ccsem = [es.enter_context(nc.semaphore(f"cc{c}")) for c in range(NCH)]
    cnt = {e: 0 for e in ENGS}
    streams = {e: [] for e in ENGS}

    def emit(eng, fn):
        streams[eng].append(fn)

    def alloc(eng, amt=1):
        cnt[eng] += amt
        return (eng, cnt[eng])

    slot_sems = {}
    slot_gen = {}

    def slot_tok(name):
        return ("S", name, slot_gen.get(name, 0) * 16)

    def mkwait(toks):
        merged = {}
        for t_ in toks:
            if t_ is None:
                continue
            if t_[0] == "S":
                _, nm, v = t_
                key = ("S", nm)
            else:
                key = t_[0]
                v = t_[1]
            merged[key] = max(merged.get(key, 0), v)
        return tuple(merged.items())

    def do_wait(q, merged):
        for key, v in merged:
            if isinstance(key, tuple):
                q.wait_ge(slot_sems[key[1]], v)
            else:
                q.wait_ge(sem[key], v)

    def barrier_now():
        return [(e, cnt[e]) for e in ENGS] + [slot_tok(s) for s in slot_gen]

    t_consts = es.enter_context(nc.sbuf_tensor("cst", [128, 4], F32))
    t_inb = es.enter_context(nc.sbuf_tensor("sb_inb", [128, 32], F32))
    t_cwv = es.enter_context(nc.sbuf_tensor("sb_cwv", [128, 64], F32))
    t_cvb = es.enter_context(nc.sbuf_tensor("sb_cvb", [128, 16], F32))
    t_dtb = es.enter_context(nc.sbuf_tensor("sb_dtb", [128, 16], F32))
    t_negA = es.enter_context(nc.sbuf_tensor("negA", [128, 16], F32))
    t_negAe = es.enter_context(nc.sbuf_tensor("negAe", [128, 16], F32))
    t_dpar = es.enter_context(nc.sbuf_tensor("sb_dpar", [128, 16], F32))
    t_eye = es.enter_context(nc.sbuf_tensor("sb_eye", [128, 128], BF16))
    t_halo = es.enter_context(nc.sbuf_tensor("sb_halo", [128, 1], F32))
    t_evm = es.enter_context(nc.sbuf_tensor("evm", [128, 1], F32))

    nc.const_aps.register(F32, -1.0, t_consts[:, 0:1])
    nc.const_aps.register(F32, 1e-5, t_consts[:, 1:2])
    nc.const_aps.register(F32, -0.5, t_consts[:, 2:3])

    pbank = [es.enter_context(nc.psum_tensor(f"pb{i}", [128, 512], F32)) for i in range(6)]
    p_bf = es.enter_context(nc.psum_tensor("pbf", [128, 1024], BF16))
    bank_w = [None] * 6
    bank_r = [[] for _ in range(6)]

    def dma(eng, out_ap, in_ap, waits=(), slot=None, transpose=False):
        assert slot is not None
        w = mkwait(waits)
        if slot not in slot_sems:
            slot_sems[slot] = es.enter_context(nc.semaphore(f"ds_{slot}"))
            slot_gen[slot] = 0
        slot_gen[slot] += 1
        ssem = slot_sems[slot]
        tok = ("S", slot, slot_gen[slot] * 16)

        def body(q, w=w, out_ap=out_ap, in_ap=in_ap, ssem=ssem, transpose=transpose):
            do_wait(q, w)
            if transpose:
                q.dma_start_transpose(out_ap, in_ap).then_inc(ssem, 16)
            else:
                q.dma_start(out=out_ap, in_=in_ap).then_inc(ssem, 16)
        emit(eng, body)
        return tok

    def mm_group(lhsTs, rhss, bk, waits=(), psum_ap=None):
        w = mkwait(list(waits) + bank_r[bk])
        bank_r[bk] = []
        tok = alloc("tensor")
        out_ap = psum_ap if psum_ap is not None else pbank[bk][:, :]

        def body(t, w=w, lhsTs=tuple(lhsTs), rhss=tuple(rhss), out_ap=out_ap):
            do_wait(t, w)
            n = len(lhsTs)
            for i in range(n):
                ins = t.matmul(out_ap, lhsTs[i], rhss[i], start=(i == 0), stop=(i == n - 1))
            ins.then_inc(sem["tensor"], 1)
        emit("tensor", body)
        bank_w[bk] = tok
        return tok

    def op(eng, builder, waits=(), selfwait=False, amt=1):
        w = mkwait(waits)
        tok = alloc(eng, amt)
        val = cnt[eng]

        def body(q, w=w, builder=builder, val=val, amt=amt, selfwait=selfwait):
            do_wait(q, w)
            builder(q).then_inc(sem[eng], amt)
            if selfwait:
                q.wait_ge(sem[eng], val)
        emit(eng, body)
        return tok

    def pre_g(g):
        g.memset(t_consts[:, 0:1], -1.0)
        g.memset(t_consts[:, 1:2], 1e-5)
        g.memset(t_consts[:, 2:3], -0.5).then_inc(sem["gpsimd"], 1)
    emit("gpsimd", pre_g)
    tok_consts = alloc("gpsimd")

    tok_smalls = None
    for dst, src in ((t_inb, d_inb), (t_cwv, d_cwv), (t_cvb, d_cvb), (t_dtb, d_dtb),
                     (t_negA, d_alog), (t_dpar, d_dpar), (t_eye, d_eye), (t_halo, d_halo)):
        tok_smalls = dma("sync", dst[:, :], src, slot="smalls")

    tok_nA_s = op("scalar", lambda s: s.activation(t_negA[:, :], t_negA[:, :], AF.Exp),
                  waits=[tok_smalls], selfwait=True)
    op("vector", lambda v: v.tensor_scalar(t_negA[:, :], t_negA[:, :], -1.0, None, ALU.mult),
       waits=[tok_nA_s], selfwait=True)
    op("vector", lambda v: v.tensor_scalar(t_negAe[:, :], t_negA[:, :], 1e-4, None, ALU.mult),
       selfwait=True)
    tok_negA = op("vector", lambda v: v.tensor_scalar(t_evm[:, :], t_halo[:, :], -1.0, 1.0,
                                                      ALU.mult, ALU.add), selfwait=True)

    tok_xbf = dma("gpsimd", s_xbf, d_x, slot="xcast")
    abank = 0

    esA = contextlib.ExitStack()
    u_bf = es.enter_context(nc.sbuf_tensor("u_bf", [128, NCH, T], BF16))
    ysl01 = [es.enter_context(nc.sbuf_tensor("ysl01_0", [128, NCH, 128], BF16))]
    x_fm = esA.enter_context(nc.sbuf_tensor("x_fm", [128, NDC, TPAD], BF16))
    a_slab = [esA.enter_context(nc.sbuf_tensor(f"aslab{i}", [128, NDC, 128], BF16)) for i in range(2)]
    t_xbr = esA.enter_context(nc.sbuf_tensor("xbr", [128, TH], F32))
    t_acc = esA.enter_context(nc.sbuf_tensor("cacc", [128, T], F32))
    u32b = [esA.enter_context(nc.sbuf_tensor(f"u32b{i}", [128, T], F32)) for i in range(2)]
    gb = [esA.enter_context(nc.sbuf_tensor(f"gb{i}", [128, T], F32)) for i in range(2)]
    wq_bf = esA.enter_context(nc.sbuf_tensor("wq_bf", [128, NDC, 1024], BF16))
    wqf = [esA.enter_context(nc.sbuf_tensor(f"wqf{i}", [128, 1024], F32)) for i in range(2)]
    a_q1 = esA.enter_context(nc.sbuf_tensor("wq1", [128, 1024], F32))
    a_q2 = esA.enter_context(nc.sbuf_tensor("wq2", [128, 1024], F32))
    a_q3 = esA.enter_context(nc.sbuf_tensor("wq3", [128, 1024], F32))
    a_q4 = esA.enter_context(nc.sbuf_tensor("wq4", [128, 1024], F32))
    a_qi = esA.enter_context(nc.sbuf_tensor("wqi", [128, 1024], I32))
    a_qsc = esA.enter_context(nc.sbuf_tensor("wqsc", [128, 32], F32))
    a_qrs = esA.enter_context(nc.sbuf_tensor("wqrs", [128, 32], F32))
    wdst = esA.enter_context(nc.sbuf_tensor("wdst", [128, 1024], BF16))

    def make_quant(q1, q2, q3, q4, qi, qsc, qrs, eng="vector", qsp=None, nblk=32):
        if eng == "split":
            def emit_split(src_ap, out_bf_ap, waits):
                src3 = src_ap.rearrange("p (n m) -> p n m", m=32)
                rs_b = qrs[:, :].unsqueeze(2).broadcast_to([128, 32, 32])
                sc_b = qsc[:, :].unsqueeze(2).broadcast_to([128, 32, 32])
                op("vector", lambda v: v.reduce_max(qsc[:, :], src3, axis=AX.X,
                                                    apply_absolute_value=True),
                   waits=waits, selfwait=True)
                op("vector", lambda v: v.tensor_scalar(qsc[:, :], qsc[:, :], 1.0 / 7.5, 1e-30,
                                                       ALU.mult, ALU.max), selfwait=True)
                tv1 = op("vector", lambda v: v.reciprocal(qrs[:, :], qsc[:, :]), selfwait=True)
                op("gpsimd", lambda g: g.tensor_tensor(
                    q1[:, :].rearrange("p (n m) -> p n m", m=32), src3, rs_b, ALU.mult),
                   waits=[tv1], selfwait=True)
                tg1 = op("gpsimd", lambda g: g.tensor_mul(q2[:, :], q1[:, :], q1[:, :]),
                         selfwait=True)
                op("vector", lambda v: v.tensor_scalar(q3[:, :], q2[:, :], 4.0, None, ALU.is_ge),
                   waits=[tg1], selfwait=True)
                tv2 = op("vector", lambda v: v.tensor_scalar(q2[:, :], q2[:, :], 16.0, None,
                                                             ALU.is_ge), selfwait=True)
                op("gpsimd", lambda g: g.tensor_scalar(q4[:, :], q3[:, :], 0.125, 0.125,
                                                       ALU.mult, ALU.add),
                   waits=[tv2], selfwait=True)
                op("gpsimd", lambda g: g.tensor_scalar(qsp[:, :], q2[:, :], 0.25, None, ALU.mult),
                   selfwait=True)
                op("gpsimd", lambda g: g.tensor_add(q4[:, :], q4[:, :], qsp[:, :]), selfwait=True)
                op("gpsimd", lambda g: g.tensor_scalar(q3[:, :], q3[:, :], -4.0, 8.0, ALU.mult,
                                                       ALU.add), selfwait=True)
                op("gpsimd", lambda g: g.tensor_scalar(qsp[:, :], q2[:, :], -2.0, None, ALU.mult),
                   selfwait=True)
                op("gpsimd", lambda g: g.tensor_add(q3[:, :], q3[:, :], qsp[:, :]), selfwait=True)
                op("gpsimd", lambda g: g.tensor_mul(q1[:, :], q1[:, :], q3[:, :]), selfwait=True)
                op("gpsimd", lambda g: g.tensor_scalar(q1[:, :], q1[:, :], -1.0, 0.5, ALU.mult,
                                                       ALU.add), selfwait=True)
                op("gpsimd", lambda g: g.tensor_copy(qi[:, :], q1[:, :]), selfwait=True)
                tg2 = op("gpsimd", lambda g: g.tensor_copy(q2[:, :], qi[:, :]), selfwait=True)
                tv3 = op("vector", lambda v: v.tensor_tensor(q3[:, :], q2[:, :], q1[:, :],
                                                             ALU.is_gt),
                         waits=[tg2], selfwait=True)
                op("gpsimd", lambda g: g.tensor_sub(q2[:, :], q2[:, :], q3[:, :]),
                   waits=[tv3], selfwait=True)
                op("gpsimd", lambda g: g.tensor_mul(q2[:, :], q2[:, :], q4[:, :]), selfwait=True)
                op("gpsimd", lambda g: g.tensor_scalar(q2[:, :], q2[:, :], -1.0, None, ALU.mult),
                   selfwait=True)
                op("gpsimd", lambda g: g.tensor_tensor(
                    q2[:, :].rearrange("p (n m) -> p n m", m=32),
                    q2[:, :].rearrange("p (n m) -> p n m", m=32), sc_b, ALU.mult), selfwait=True)
                return op("gpsimd", lambda g: g.tensor_copy(out_bf_ap, q2[:, :]), selfwait=True)
            return emit_split

        def emit_quant(src_ap, out_bf_ap, waits, sink=None):
            holder = {}
            steps = []

            def op_q(eng_, builder, waits_=()):
                def run(eng_=eng_, builder=builder, waits_=waits_):
                    prev = holder.get("prev")
                    ws = list(waits_) + ([prev] if prev is not None else [])
                    holder["prev"] = op(eng_, builder, waits=ws)
                steps.append(run)

            src3 = src_ap.rearrange("p (n m) -> p n m", m=32)
            rs_b = qrs[:, :].unsqueeze(2).broadcast_to([128, nblk, 32])
            sc_b = qsc[:, :].unsqueeze(2).broadcast_to([128, nblk, 32])
            op_q("vector", lambda v: v.reduce_max(qsc[:, :], src3, axis=AX.X,
                                                  apply_absolute_value=True),
                 waits_=waits)
            op_q("vector", lambda v: v.tensor_scalar(qsc[:, :], qsc[:, :], 1.0 / 7.5, 1e-30,
                                                     ALU.mult, ALU.max))
            op_q("vector", lambda v: v.reciprocal(qrs[:, :], qsc[:, :]))
            op_q(eng, lambda e: e.tensor_tensor(q1[:, :].rearrange("p (n m) -> p n m", m=32),
                                                src3, rs_b, ALU.mult))
            op_q(eng, lambda e: e.tensor_mul(q2[:, :], q1[:, :], q1[:, :]))
            op_q(eng, lambda e: e.tensor_scalar(q3[:, :], q2[:, :], 4.0, None, ALU.is_ge))
            op_q(eng, lambda e: e.tensor_scalar(q2[:, :], q2[:, :], 16.0, None, ALU.is_ge))
            op_q(eng, lambda e: e.tensor_scalar(q4[:, :], q3[:, :], 0.125, 0.125, ALU.mult,
                                                ALU.add))
            op_q(eng, lambda e: e.scalar_tensor_tensor(q4[:, :], q2[:, :], 0.25, q4[:, :],
                                                       ALU.mult, ALU.add))
            op_q(eng, lambda e: e.tensor_scalar(q3[:, :], q3[:, :], -4.0, 8.0, ALU.mult,
                                                ALU.add))
            op_q(eng, lambda e: e.scalar_tensor_tensor(q3[:, :], q2[:, :], -2.0, q3[:, :],
                                                       ALU.mult, ALU.add))
            op_q(eng, lambda e: e.tensor_mul(q1[:, :], q1[:, :], q3[:, :]))
            op_q(eng, lambda e: e.tensor_scalar(q1[:, :], q1[:, :], -1.0, 0.5, ALU.mult,
                                                ALU.add))
            op_q(eng, lambda e: e.tensor_copy(qi[:, :], q1[:, :]))
            op_q(eng, lambda e: e.tensor_copy(q2[:, :], qi[:, :]))
            op_q(eng, lambda e: e.tensor_tensor(q3[:, :], q2[:, :], q1[:, :], ALU.is_gt))
            op_q(eng, lambda e: e.tensor_sub(q2[:, :], q2[:, :], q3[:, :]))
            op_q(eng, lambda e: e.tensor_mul(q2[:, :], q2[:, :], q4[:, :]))
            op_q(eng, lambda e: e.scalar_tensor_tensor(
                q2[:, :].rearrange("p (n m) -> p n m", m=32),
                q2[:, :].rearrange("p (n m) -> p n m", m=32), -1.0, sc_b,
                ALU.mult, ALU.mult))

            def _final():
                holder["tok"] = op(eng, lambda e: e.tensor_copy(out_bf_ap, q2[:, :]),
                                   waits=[holder["prev"]], selfwait=True)
            steps.append(_final)
            if sink is None:
                for s_ in steps:
                    s_()
                return holder["tok"]
            sink.extend(steps)
            return holder
        return emit_quant

    quant_gh = [make_quant(a_q1[:, h * 512:(h + 1) * 512], a_q2[:, h * 512:(h + 1) * 512],
                           a_q3[:, h * 512:(h + 1) * 512], a_q4[:, h * 512:(h + 1) * 512],
                           a_qi[:, h * 512:(h + 1) * 512], a_qsc[:, h * 16:(h + 1) * 16],
                           a_qrs[:, h * 16:(h + 1) * 16], nblk=16) for h in range(2)]

    tok_xfm = None
    for c in range(NDC):
        tok_xfm = dma("scalar", x_fm[:, c, :], s_xbf[:, c * 128:(c + 1) * 128],
                      waits=[tok_xbf], slot="xfm", transpose=True)

    XB_TILES = [(0, 512), (512, 512), (1024, 512), (1536, 512), (2048, 3)]
    Z_TILES = [(3, 512), (515, 512), (1027, 512), (1539, 512)]

    a_slab_free = [[], []]
    u32_free = [[], []]
    g_free = [[], []]
    xbr_free = []
    acc_free = []
    ubf_last = None
    u32_wr = [None] * NCH
    g_wr = [None] * NCH
    wqf_free = [[], []]
    wq_holders = []
    wq_pending = []
    slab_tok_a = [None] * 32

    def emit_a_slab(c):
        sb = c % 2
        slab_tok_a[c] = dma("gpsimd",
                            a_slab[sb][:, :, :].rearrange("p s j -> p (s j)"),
                            d_win[c], waits=a_slab_free[sb] + [tok_consts],
                            slot=f"aslab{sb}")
        a_slab_free[sb] = []

    emit_a_slab(0)

    for c in range(32):
        sb = c % 2
        if c + 1 < 32:
            emit_a_slab(c + 1)
        tiles = XB_TILES if c < 16 else Z_TILES
        evac_toks = []
        for ti, (off, w) in enumerate(tiles):
            bk = abank % 6
            abank += 1
            mm_group([a_slab[sb][:, s_, :] for s_ in range(NDC)],
                     [x_fm[:, s_, off:off + w] for s_ in range(NDC)],
                     bk, waits=[slab_tok_a[c], tok_xfm],
                     psum_ap=pbank[bk][:, 0:w])
            if c < 16:
                etok = op("scalar", lambda s, bk=bk, off=off, w=w, c=c: s.activation(
                    t_xbr[:, off:off + w], pbank[bk][:, 0:w], AF.Identity,
                    bias=t_inb[:, c:c + 1]),
                    waits=[bank_w[bk]] + xbr_free)
            else:
                etok = op("scalar", lambda s, bk=bk, off=off, w=w, c=c, sb=sb: s.activation(
                    gb[sb][:, off - 3:off - 3 + w], pbank[bk][:, 0:w], AF.Sigmoid,
                    bias=t_inb[:, c:c + 1]),
                    waits=[bank_w[bk]] + g_free[sb])
            bank_r[bk].append(etok)
            evac_toks.append(etok)
        a_slab_free[sb].append(bank_w[(abank - 1) % 6])
        if c < 16:
            op("vector", lambda v: v.tensor_scalar(
                t_xbr[:, 0:HALO], t_xbr[:, 0:HALO], t_halo[:, 0:1], None, ALU.mult),
               waits=evac_toks + [tok_negA] + acc_free + u32_free[sb] + xbr_free, selfwait=True)
            xbr_free = []
            acc_free = []
            u32_free[sb] = []
            op("vector", lambda v, c=c: v.tensor_scalar(
                t_acc[:, :], t_xbr[:, 0:T], t_cwv[:, 4 * c:4 * c + 1], None, ALU.mult))
            for k in range(1, 4):
                op("vector", lambda v, c=c, k=k: v.scalar_tensor_tensor(
                    t_acc[:, :], t_xbr[:, k:k + T], t_cwv[:, 4 * c + k:4 * c + k + 1],
                    t_acc[:, :], ALU.mult, ALU.add),
                   selfwait=(k == 3))
            tok_conv = ("vector", cnt["vector"])
            xbr_free.append(tok_conv)
            tok_silu = op("scalar", lambda s, c=c, sb=sb: s.activation(
                u32b[sb][:, :], t_acc[:, :], AF.Silu, bias=t_cvb[:, c:c + 1]), waits=[tok_conv])
            acc_free.append(tok_silu)
            ubf_last = op("scalar", lambda s, c=c, sb=sb: s.copy(u_bf[:, c, :], u32b[sb][:, :]),
                          waits=[tok_silu])
            u32_free[sb].append(ubf_last)
            u32_wr[c] = dma("sync", s_u32[c], u32b[sb][:, :], waits=[ubf_last], slot=f"u32s{sb}")
            u32_free[sb].append(u32_wr[c])
        else:
            cz = c - 16
            g_wr[cz] = dma("sync", s_g[cz], gb[sb][:, :], waits=evac_toks, slot=f"gs{sb}")
            g_free[sb] = [g_wr[cz]]
            if cz % 2 == 0:
                i = cz // 2
                tok_wl = dma("sync", wqf[i % 2][:, :], d_fw[i * 128:(i + 1) * 128, :],
                             waits=wqf_free[i % 2], slot=f"wqf{i % 2}")
                wqf_free[i % 2] = []
                s_h = [[], []]
                hh_ = [quant_gh[h](wqf[i % 2][:, h * 512:(h + 1) * 512],
                                   wq_bf[:, i, h * 512:(h + 1) * 512],
                                   [tok_wl], sink=s_h[h]) for h in range(2)]
                while s_h[0] or s_h[1]:
                    for h in range(2):
                        if s_h[h]:
                            s_h[h].pop(0)()
                wq_holders.extend(hh_)
                wqf_free[i % 2] = [hh_[0]["tok"], hh_[1]["tok"]]

    while wq_pending:
        wq_pending.pop(0)()
    wq_toks = [h["tok"] for h in wq_holders]

    bf_rd = []
    wd_wr = None
    for j in range(NDC):
        w = mkwait(wq_toks + bf_rd + [tok_smalls])
        bf_rd = []
        tok_tp = alloc("tensor")

        def body(t, w=w, j=j):
            do_wait(t, w)
            for i in range(NDC):
                ins = t.transpose(p_bf[:, i * 128:(i + 1) * 128],
                                  wq_bf[:, i, j * 128:(j + 1) * 128], t_eye[:, :])
            ins.then_inc(sem["tensor"], 1)
        emit("tensor", body)
        tok_ev = op("scalar", lambda s: s.copy(wdst[:, :], p_bf[:, :]), waits=[tok_tp])
        bf_rd.append(tok_ev)
        wd_wr = dma("gpsimd", s_wd[:, j * 1024:(j + 1) * 1024], wdst[:, :], waits=[tok_ev], slot="wds")
        bf_rd.append(wd_wr)

    barrierA = barrier_now()
    esA.close()

    esC = contextlib.ExitStack()
    c_slab = {wn: [esC.enter_context(nc.sbuf_tensor(f"cs_{wn}{i}", [128, NCH, 128], BF16))
                   for i in range(2)] for wn in ("dt", "bw", "cw")}
    u32p = [esC.enter_context(nc.sbuf_tensor(f"u32p{i}", [128, T], F32)) for i in range(2)]
    gp = [esC.enter_context(nc.sbuf_tensor(f"gp{i}", [128, T], F32)) for i in range(2)]
    t_px = [esC.enter_context(nc.sbuf_tensor(f"px{i}", [128, T], F32)) for i in range(2)]
    t_dec = [esC.enter_context(nc.sbuf_tensor(f"dec{i}", [128, T], F32)) for i in range(2)]
    t_B = [esC.enter_context(nc.sbuf_tensor(f"tB{i}", [128, T], F32)) for i in range(2)]
    t_C = [esC.enter_context(nc.sbuf_tensor(f"tC{i}", [128, T], F32)) for i in range(2)]
    t_st = esC.enter_context(nc.sbuf_tensor("st", [128, T], F32))
    t_ybf = esC.enter_context(nc.sbuf_tensor("ybf", [128, T], BF16))
    t_scol = esC.enter_context(nc.sbuf_tensor("scol", [128, 1], F32))
    t_sini = esC.enter_context(nc.sbuf_tensor("sini", [128, 1], F32))

    dws = {"dt": d_dtw, "bw": d_bw, "cw": d_cw}
    cs_free = {wn: [[], []] for wn in ("dt", "bw", "cw")}
    u32p_free = [[], []]
    gp_free = [[], []]
    px_free = [[], []]
    decf = [[], []]
    Bf = [[], []]
    Cf = [[], []]
    st_free, ybf_free, sini_free, scol_free = [], [], [], []
    ybf_wr = [None] * NCH
    slab_tok_c = [None] * NCH
    cbank = 0
    saved = {}

    def emit_c_slabs(c, first=False):
        sb = c % 2
        toks = {}
        for wn in ("dt", "bw", "cw"):
            toks[wn] = dma("gpsimd", c_slab[wn][sb][:, :, :].rearrange("p s j -> p (s j)"),
                           dws[wn][c], waits=cs_free[wn][sb] + (barrierA if first else []),
                           slot=f"c{wn}{sb}")
            cs_free[wn][sb] = []
        slab_tok_c[c] = toks

    emit_c_slabs(0, first=True)
    ow_v = u_bf[:, 0:8, :].rearrange("p c j -> p (c j)")
    wd_v = u_bf[:, 8:12, :].rearrange("p c j -> p (c j)")
    tok_ow = None

    def emit_deferred(c):
        nonlocal st_free, ybf_free
        if c < 0:
            return
        sb = c % 2
        sv = saved[c]
        op("vector", lambda v: v.tensor_scalar(t_sini[:, :], t_sini[:, :], t_halo[:, 0:1],
                                               None, ALU.mult),
           waits=[sv["recv"]] + st_free, selfwait=True)
        st_free = []
        op("vector", lambda v, sb=sb: v.tensor_tensor_scan(
            t_st[:, :], t_dec[sb][:, :], t_B[sb][:, :], t_sini[:, 0:1], ALU.mult, ALU.add),
           selfwait=True)
        sini_free.append(("vector", cnt["vector"]))
        op("vector", lambda v, sb=sb: v.tensor_mul(t_st[:, :], t_C[sb][:, :], t_st[:, :]),
           waits=sv["C_evacs"])
        op("vector", lambda v, sb=sb, c=c: v.scalar_tensor_tensor(
            t_st[:, :], u32p[sb][:, :], t_dpar[:, c:c + 1], t_st[:, :], ALU.mult, ALU.add))
        tok_y = op("vector", lambda v, sb=sb: v.tensor_mul(t_st[:, :], t_st[:, :], gp[sb][:, :]),
                   waits=[sv["pf"]], selfwait=True)
        decf[sb].append(tok_y)
        Bf[sb].append(tok_y)
        Cf[sb].append(tok_y)
        u32p_free[sb].append(tok_y)
        gp_free[sb].append(tok_y)
        tok_ybf = op("scalar", lambda s: s.copy(t_ybf[:, :], t_st[:, :]),
                     waits=[tok_y] + ybf_free)
        st_free.append(tok_ybf)
        ybf_wr[c] = dma("sync", s_ybf[c], t_ybf[:, :], waits=[tok_ybf], slot="ybfs")
        ybf_free = [ybf_wr[c]]
        dma("sync", ysl01[0][:, c, :], s_ybf[c, :, 0:128], waits=[ybf_wr[c]], slot="y0f")

    for c in range(NCH):
        sb = c % 2
        if c + 1 < NCH:
            emit_c_slabs(c + 1)
        tok_pf_u = dma("sync", u32p[sb][:, :], s_u32[c],
                       waits=u32p_free[sb] + [u32_wr[c]] + (barrierA if c == 0 else []),
                       slot=f"u32p{sb}")
        u32p_free[sb] = []
        tok_pf = dma("sync", gp[sb][:, :], s_g[c], waits=gp_free[sb] + [g_wr[c]], slot=f"gp{sb}")
        gp_free[sb] = []

        dt_evacs = []
        for ti in range(4):
            bk = cbank % 6
            cbank += 1
            mm_group([c_slab["dt"][sb][:, s_, :] for s_ in range(NCH)],
                     [u_bf[:, s_, ti * 512:(ti + 1) * 512] for s_ in range(NCH)],
                     bk, waits=[slab_tok_c[c]["dt"], ubf_last])
            etok = op("scalar", lambda s, bk=bk, ti=ti, c=c, sb=sb: s.activation(
                t_px[sb][:, ti * 512:(ti + 1) * 512], pbank[bk][:, :], AF.Identity,
                bias=t_dtb[:, c:c + 1]), waits=[bank_w[bk]] + px_free[sb])
            bank_r[bk].append(etok)
            dt_evacs.append(etok)
            if ti == 3:
                px_free[sb] = []

        B_evacs = []
        C_evacs = []
        for ti in range(4):
            bk = cbank % 6
            cbank += 1
            mm_group([c_slab["bw"][sb][:, s_, :] for s_ in range(NCH)],
                     [u_bf[:, s_, ti * 512:(ti + 1) * 512] for s_ in range(NCH)],
                     bk, waits=[slab_tok_c[c]["bw"]])
            etok = op("scalar", lambda s, bk=bk, ti=ti, sb=sb: s.activation(
                t_B[sb][:, ti * 512:(ti + 1) * 512], pbank[bk][:, :], AF.Tanh),
                waits=[bank_w[bk]] + Bf[sb])
            bank_r[bk].append(etok)
            B_evacs.append(etok)
            if ti == 3:
                Bf[sb] = []
        for ti in range(4):
            bk = cbank % 6
            cbank += 1
            mm_group([c_slab["cw"][sb][:, s_, :] for s_ in range(NCH)],
                     [u_bf[:, s_, ti * 512:(ti + 1) * 512] for s_ in range(NCH)],
                     bk, waits=[slab_tok_c[c]["cw"]])
            etok = op("scalar", lambda s, bk=bk, ti=ti, sb=sb: s.activation(
                t_C[sb][:, ti * 512:(ti + 1) * 512], pbank[bk][:, :], AF.Tanh),
                waits=[bank_w[bk]] + Cf[sb])
            bank_r[bk].append(etok)
            C_evacs.append(etok)
            if ti == 3:
                Cf[sb] = []
        last_mm = ("tensor", cnt["tensor"])
        for wn in ("dt", "bw", "cw"):
            cs_free[wn][sb] = [last_mm]
        if c == NCH - 1:
            tok_ow = dma("gpsimd", ow_v, d_ow, waits=[last_mm], slot="ow")

        op("scalar", lambda s, sb=sb: s.activation(t_px[sb][:, :], t_px[sb][:, :], AF.Exp),
           waits=dt_evacs + [tok_consts], selfwait=True)
        tok_sp = op("scalar", lambda s, sb=sb: s.activation(t_px[sb][:, :], t_px[sb][:, :],
                                                            AF.Ln, bias=1.0), selfwait=True)
        tok_dec = op("scalar", lambda s, c=c, sb=sb: s.activation(
            t_dec[sb][:, :], t_px[sb][:, :], AF.Exp, bias=t_negAe[:, c:c + 1],
            scale=t_negA[:, c:c + 1]), waits=[tok_sp, tok_negA] + decf[sb])
        decf[sb] = []

        op("vector", lambda v, sb=sb: v.tensor_mul(t_B[sb][:, :], t_B[sb][:, :], u32p[sb][:, :]),
           waits=[tok_dec, tok_pf_u] + B_evacs)
        op("vector", lambda v, sb=sb: v.tensor_scalar(t_px[sb][:, :], t_dec[sb][:, :], -1.0, 1.0,
                                                      ALU.mult, ALU.add))
        op("vector", lambda v, sb=sb: v.tensor_mul(t_B[sb][:, :], t_px[sb][:, :], t_B[sb][:, :]),
           selfwait=True)
        px_free[sb].append(("vector", cnt["vector"]))
        op("vector", lambda v, sb=sb: v.tensor_tensor_scan(
            t_st[:, :], t_dec[sb][:, :], t_B[sb][:, :], 0.0, ALU.mult, ALU.add),
           waits=st_free, selfwait=True)
        st_free = []
        tok_scol = op("vector", lambda v: v.tensor_scalar(
            t_scol[:, :], t_st[:, T - 1:T], t_evm[:, 0:1], None, ALU.mult),
            waits=scol_free, selfwait=True)
        st_free.append(tok_scol)

        tok_ccin = dma("gpsimd", d_ccin[c], t_scol[:, :], waits=[tok_scol], slot="ccin")
        scol_free = [tok_ccin]

        emit_deferred(c - 1)

        def cc_body(g, c=c, w=mkwait([tok_ccin])):
            do_wait(g, w)
            g.collective_compute(
                "AllReduce", ALU.add,
                replica_groups=[[0, 1], [2, 3], [4, 5], [6, 7]],
                ins=[d_ccin[c]], outs=[d_ccout[c]],
            ).then_inc(ccsem[c])
        emit("gpsimd", cc_body)

        w_recv = mkwait(sini_free)
        sini_free = []
        if "recv" not in slot_sems:
            slot_sems["recv"] = es.enter_context(nc.semaphore("ds_recv"))
            slot_gen["recv"] = 0
        slot_gen["recv"] += 1
        tok_recv = ("S", "recv", slot_gen["recv"] * 16)

        def recv_body(g, c=c, w_recv=w_recv):
            g.wait_ge(ccsem[c], 1)
            do_wait(g, w_recv)
            g.dma_start(out=t_sini[:, :], in_=d_ccout[c]).then_inc(slot_sems["recv"], 16)
        emit("gpsimd", recv_body)

        saved[c] = {"recv": tok_recv, "C_evacs": C_evacs, "pf": tok_pf}

    ubf_dead = ("tensor", cnt["tensor"])
    ubf_ysl = [u_bf[:, 12 + p, :].rearrange("p (c j) -> p c j", j=128) for p in range(2)]

    def ysl_view(t_):
        return ysl01[0] if t_ == 0 else ubf_ysl[t_ % 2]
    tok_wd = dma("sync", wd_v, s_wd, waits=[ubf_dead, wd_wr], slot="wdl")

    emit_deferred(NCH - 1)

    barrierC = barrier_now()
    esC.close()

    esD = contextlib.ExitStack()
    xtm = [esD.enter_context(nc.sbuf_tensor(f"xtm{i}", [128, 1024], F32)) for i in range(2)]
    xb = [esD.enter_context(nc.sbuf_tensor(f"xb{i}", [128, 1024], F32)) for i in range(2)]
    t_hb = [esD.enter_context(nc.sbuf_tensor(f"thb{i}", [128, 1024], F32)) for i in range(2)]
    t_h = [esD.enter_context(nc.sbuf_tensor(f"th{i}", [128, 1024], F32)) for i in range(2)]
    d_q1 = esD.enter_context(nc.sbuf_tensor("dq1", [128, 1024], F32))
    d_q2 = esD.enter_context(nc.sbuf_tensor("dq2", [128, 1024], F32))
    d_q3 = esD.enter_context(nc.sbuf_tensor("dq3", [128, 1024], F32))
    d_q4 = esD.enter_context(nc.sbuf_tensor("dq4", [128, 1024], F32))
    d_qi = esD.enter_context(nc.sbuf_tensor("dqi", [128, 1024], I32))
    d_sc = esD.enter_context(nc.sbuf_tensor("dsc", [128, 32], F32))
    d_rs = esD.enter_context(nc.sbuf_tensor("drs", [128, 32], F32))
    t_mv = esD.enter_context(nc.sbuf_tensor("mv", [128, 8], F32))
    t_xdbf = esD.enter_context(nc.sbuf_tensor("xdbf", [128, 1024], BF16))
    t_xdT = esD.enter_context(nc.sbuf_tensor("xdT", [128, 1024], BF16))
    t_outb = esD.enter_context(nc.sbuf_tensor("outbt", [128, 1024], F32))
    t_ffnb = esD.enter_context(nc.sbuf_tensor("ffnbt", [128, 1024], F32))
    obuf = [esD.enter_context(nc.sbuf_tensor(f"obuf{i}", [128, 1024], F32)) for i in range(2)]

    e_q1 = esD.enter_context(nc.sbuf_tensor("eq1", [128, 1024], F32))
    e_q2 = esD.enter_context(nc.sbuf_tensor("eq2", [128, 1024], F32))
    e_q3 = esD.enter_context(nc.sbuf_tensor("eq3", [128, 1024], F32))
    e_q4 = esD.enter_context(nc.sbuf_tensor("eq4", [128, 1024], F32))
    e_qi = esD.enter_context(nc.sbuf_tensor("eqi", [128, 1024], I32))
    e_sc = esD.enter_context(nc.sbuf_tensor("esc", [128, 32], F32))
    e_rs = esD.enter_context(nc.sbuf_tensor("ers", [128, 32], F32))
    quant_dh = [make_quant(d_q1[:, h * 512:(h + 1) * 512], d_q2[:, h * 512:(h + 1) * 512],
                           d_q3[:, h * 512:(h + 1) * 512], d_q4[:, h * 512:(h + 1) * 512],
                           d_qi[:, h * 512:(h + 1) * 512], d_sc[:, h * 16:(h + 1) * 16],
                           d_rs[:, h * 16:(h + 1) * 16], nblk=16) for h in range(2)]

    dma("sync", t_outb[:, :], d_outb, waits=barrierC, slot="dbias")
    tok_bias = dma("sync", t_ffnb[:, :], d_ffnb, slot="dbias")

    ysl_free = [[], []]
    xtm_free = [[], []]
    xb_free = [[], []]
    hb_free = []
    obuf_free = [[], []]
    h_free, q_free = [], []
    xd_free, xdT_free = [], []
    out_toks = []
    ysl_tok = [None] * 16
    xtm_tok = [None] * 16
    ysl_tok[0] = ("S", "y0f", 16 * NCH)
    ysl_tok[1] = dma("scalar", ysl_view(1),
                     s_ybf[:, :, 128:256].rearrange("c p j -> p c j"),
                     waits=list(ybf_wr) + [ubf_dead], slot="ysl1")
    for tt in range(2):
        xtm_tok[tt] = dma("sync", xtm[tt % 2][:, :],
                          d_x[HALO + tt * 128: HALO + (tt + 1) * 128, :], waits=barrierC,
                          slot=f"xtm{tt % 2}")

    op_banks = [None] * 16
    op_w = [None] * 16

    def emit_op_mm(tau):
        sb = tau % 2
        bks = []
        for hh in range(2):
            bk = 2 * (tau % 2) + hh
            bks.append(bk)
            mm_group([ysl_view(tau)[:, cc, :] for cc in range(NCH)],
                     [u_bf[:, cc // 2, (cc % 2) * 1024 + hh * 512:(cc % 2) * 1024 + (hh + 1) * 512]
                      for cc in range(NCH)],
                     bk, waits=[ysl_tok[tau], tok_ow])
        op_banks[tau] = tuple(bks)
        op_w[tau] = ("tensor", cnt["tensor"])
        ysl_free[sb].append(op_w[tau])

    emit_op_mm(0)
    emit_op_mm(1)

    h_free = [[], []]
    hb_free = [[], []]
    eq_free = []
    mv1_free = []
    mv2_free = []
    back_state = {}

    def emit_back(tau, qsink=None):
        nonlocal eq_free, mv2_free

        def qpop(n=2):
            for _ in range(n):
                if qsink:
                    qsink.pop(0)()
        if tau < 0:
            return
        st = back_state.pop(tau)
        sbb = st["sb"]
        fv = []
        for hh in range(2):
            bk = st["f_banks"][hh]
            etok = op("vector", lambda v, bk=bk, hh=hh, sbb=sbb: v.tensor_add(
                obuf[sbb][:, hh * 512:(hh + 1) * 512], pbank[bk][:, :],
                t_hb[sbb][:, hh * 512:(hh + 1) * 512]),
               waits=[st["f_w"], st["tok_hb"]] + (obuf_free[sbb] if hh == 0 else []))
            if hh == 0:
                obuf_free[sbb] = []
            bank_r[bk].append(etok)
            fv.append(etok)
            qpop(1 if hh == 0 else 2)
        hb_free[sbb] = [fv[-1]]
        tok_sq2 = op("scalar", lambda s, sbb=sbb: s.activation(e_q1[:, :], obuf[sbb][:, :],
                                                               AF.Square),
                     waits=fv + eq_free, selfwait=True)
        eq_free = []
        op("vector", lambda v, sbb=sbb: v.reduce_sum(t_mv[:, 4:5], obuf[sbb][:, :], axis=AX.X),
           waits=fv + mv2_free, selfwait=True)
        mv2_free = []
        qpop(2)
        tok_nm2 = op("vector", lambda v: v.tensor_scalar(t_mv[:, 4:5], t_mv[:, 4:5],
                                                         -1.0 / 1024.0, None, ALU.mult),
                     selfwait=True)
        qpop(2)
        op("vector", lambda v: v.reduce_sum(t_mv[:, 5:6], e_q1[:, :], axis=AX.X),
           waits=[tok_sq2], selfwait=True)
        qpop(1)
        op("vector", lambda v: v.tensor_mul(t_mv[:, 7:8], t_mv[:, 4:5], t_mv[:, 4:5]),
           selfwait=True)
        tok_var2 = op("vector", lambda v: v.scalar_tensor_tensor(
            t_mv[:, 5:6], t_mv[:, 5:6], 1.0 / 1024.0, t_mv[:, 7:8],
            ALU.mult, ALU.subtract), selfwait=True)
        eq_free.append(tok_var2)
        qpop(2)
        op("scalar", lambda s: s.activation(t_mv[:, 5:6], t_mv[:, 5:6], AF.Ln, bias=1e-5),
           waits=[tok_var2], selfwait=True)
        op("scalar", lambda s: s.activation(t_mv[:, 5:6], t_mv[:, 5:6], AF.Exp,
                                            scale=-0.5), selfwait=True)
        op("scalar", lambda s: s.activation(t_mv[:, 6:7], t_mv[:, 4:5], AF.Identity,
                                            scale=t_mv[:, 5:6]), selfwait=True)
        tok_o = op("scalar", lambda s, sbb=sbb: s.activation(
            obuf[sbb][:, :], obuf[sbb][:, :], AF.Identity,
            scale=t_mv[:, 5:6], bias=t_mv[:, 6:7]), selfwait=True)
        mv2_free = [tok_o]
        otok = dma("sync", d_out[tau * 128:(tau + 1) * 128, :], obuf[sbb][:, :],
                   waits=[tok_o], slot=f"out{sbb}")
        out_toks.append(otok)
        obuf_free[sbb].append(otok)

    for tau in range(16):
        sb = tau % 2
        if tau + 2 < 16:
            tt = tau + 2
            w_extra = [ubf_dead] if tt < 4 else []
            ysl_tok[tt] = dma("scalar", ysl_view(tt),
                              s_ybf[:, :, tt * 128:(tt + 1) * 128].rearrange("c p j -> p c j"),
                              waits=ysl_free[tt % 2] + w_extra, slot=f"ysl{tt % 2}")
            ysl_free[tt % 2] = []

        tok_xb = op("gpsimd", lambda g, sb=sb: g.tensor_add(xb[sb][:, :], xtm[sb][:, :],
                                                            t_outb[:, :]),
                    waits=[xtm_tok[tau], tok_bias] + xb_free[sb], selfwait=True)
        xtm_free[sb].append(tok_xb)
        ev = []
        for hh in range(2):
            bk = op_banks[tau][hh]
            etok = op("vector", lambda v, bk=bk, hh=hh, sb=sb: v.tensor_add(
                t_h[sb][:, hh * 512:(hh + 1) * 512], pbank[bk][:, :],
                xb[sb][:, hh * 512:(hh + 1) * 512]),
               waits=[op_w[tau], tok_xb] + (h_free[sb] if hh == 0 else []))
            if hh == 0:
                h_free[sb] = []
            bank_r[bk].append(etok)
            ev.append(etok)
        xb_free[sb] = [ev[-1]]
        if tau + 2 < 16:
            tt = tau + 2
            xtm_tok[tt] = dma("sync", xtm[tt % 2][:, :],
                              d_x[HALO + tt * 128: HALO + (tt + 1) * 128, :],
                              waits=xtm_free[tt % 2], slot=f"xtm{tt % 2}")
            xtm_free[tt % 2] = []

        tok_sq = op("scalar", lambda s, sb=sb: s.activation(d_q1[:, :], t_h[sb][:, :], AF.Square),
                    waits=ev + q_free, selfwait=True)
        q_free = []
        op("vector", lambda v, sb=sb: v.reduce_sum(t_mv[:, 0:1], t_h[sb][:, :], axis=AX.X),
           waits=ev + mv1_free, selfwait=True)
        mv1_free = []
        tok_nm = op("vector", lambda v: v.tensor_scalar(t_mv[:, 0:1], t_mv[:, 0:1],
                                                        -1.0 / 1024.0, None, ALU.mult),
                    selfwait=True)
        op("vector", lambda v: v.reduce_sum(t_mv[:, 1:2], d_q1[:, :], axis=AX.X),
           waits=[tok_sq], selfwait=True)
        op("vector", lambda v: v.tensor_mul(t_mv[:, 3:4], t_mv[:, 0:1], t_mv[:, 0:1]),
           selfwait=True)
        tok_var = op("vector", lambda v: v.scalar_tensor_tensor(
            t_mv[:, 1:2], t_mv[:, 1:2], 1.0 / 1024.0, t_mv[:, 3:4],
            ALU.mult, ALU.subtract), selfwait=True)
        op("scalar", lambda s: s.activation(t_mv[:, 1:2], t_mv[:, 1:2], AF.Ln, bias=1e-5),
           waits=[tok_var, tok_consts], selfwait=True)
        op("scalar", lambda s: s.activation(t_mv[:, 1:2], t_mv[:, 1:2], AF.Exp,
                                            scale=-0.5), selfwait=True)
        op("scalar", lambda s: s.activation(t_mv[:, 2:3], t_mv[:, 0:1], AF.Identity,
                                            scale=t_mv[:, 1:2]), selfwait=True)
        tok_ln1 = op("scalar", lambda s, sb=sb: s.activation(
            t_h[sb][:, :], t_h[sb][:, :], AF.Identity,
            scale=t_mv[:, 1:2], bias=t_mv[:, 2:3]), selfwait=True)
        mv1_free = [tok_ln1]

        tok_hb = op("gpsimd", lambda g, sb=sb: g.tensor_add(t_hb[sb][:, :], t_h[sb][:, :],
                                                            t_ffnb[:, :]),
                    waits=[tok_ln1, tok_bias] + hb_free[sb], selfwait=True)

        if debug and tau == 0:
            tok_ln1 = dma("sync", d_dbg[1], t_h[sb][:, :], waits=[tok_ln1], slot="dbg")
        s_h = [[], []]
        q_holders = [quant_dh[h](t_h[sb][:, h * 512:(h + 1) * 512],
                                 t_xdbf[:, h * 512:(h + 1) * 512],
                                 [tok_ln1] + xd_free, sink=s_h[h]) for h in range(2)]
        qsink = []
        while s_h[0] or s_h[1]:
            for h in range(2):
                if s_h[h]:
                    qsink.append(s_h[h].pop(0))
        emit_back(tau - 1, qsink)
        while qsink:
            qsink.pop(0)()
        tok_qd = [q_holders[0]["tok"], q_holders[1]["tok"]]
        xd_free = []
        q_free += tok_qd
        h_free[sb] = [tok_hb] + tok_qd

        w = mkwait(tok_qd + [tok_smalls] + bf_rd)
        bf_rd = []
        tok_tp = alloc("tensor")

        def tp_body(t, w=w):
            do_wait(t, w)
            for k in range(NDC):
                ins = t.transpose(p_bf[:, k * 128:(k + 1) * 128],
                                  t_xdbf[:, k * 128:(k + 1) * 128], t_eye[:, :])
            ins.then_inc(sem["tensor"], 1)
        emit("tensor", tp_body)
        xd_free.append(tok_tp)
        tok_xdT = op("scalar", lambda s: s.copy(t_xdT[:, :], p_bf[:, :]),
                     waits=[tok_tp] + xdT_free)
        xdT_free = []
        bf_rd.append(tok_xdT)

        f_banks = []
        for hh in range(2):
            bk = 4 + hh
            f_banks.append(bk)
            mm_group([t_xdT[:, k * 128:(k + 1) * 128] for k in range(NDC)],
                     [u_bf[:, 8 + k // 2, (k % 2) * 1024 + hh * 512:(k % 2) * 1024 + (hh + 1) * 512]
                      for k in range(NDC)],
                     bk, waits=[tok_xdT, tok_wd])
        f_w = ("tensor", cnt["tensor"])
        xdT_free.append(f_w)
        back_state[tau] = {"sb": sb, "f_banks": tuple(f_banks), "f_w": f_w, "tok_hb": tok_hb}

        if tau + 2 < 16:
            emit_op_mm(tau + 2)

    emit_back(15)

    wfin = mkwait(out_toks)

    def fin(q, wfin=wfin):
        do_wait(q, wfin)
    emit("sync", fin)

    esD.close()

    with es:
        with nc.Block() as block:
            block.sync(lambda q: [f(q) for f in streams["sync"]])
            block.gpsimd(lambda q: [f(q) for f in streams["gpsimd"]])
            block.scalar(lambda q: [f(q) for f in streams["scalar"]])
            block.vector(lambda q: [f(q) for f in streams["vector"]])
            block.tensor(lambda q: [f(q) for f in streams["tensor"]])
    return nc


_NC_CACHE = {}


def _get_nc():
    if "nc" not in _NC_CACHE:
        _NC_CACHE["nc"] = build_nc()
    return _NC_CACHE["nc"]


def kernel(L0_attn_x, L0_attn_in_proj_w, L0_attn_in_proj_b, L0_attn_conv_w, L0_attn_conv_b,
           L0_attn_dt_w, L0_attn_dt_b, L0_attn_b_w, L0_attn_c_w, L0_attn_A_log,
           L0_attn_D_param, L0_attn_out_w, L0_attn_out_b, L0_ffn_weight, L0_ffn_bias):
    x = np.ascontiguousarray(np.asarray(L0_attn_x, dtype=np.float32))
    in_w = np.asarray(L0_attn_in_proj_w, dtype=np.float32)
    in_b = np.asarray(L0_attn_in_proj_b, dtype=np.float32)
    conv_w = np.asarray(L0_attn_conv_w, dtype=np.float32)
    conv_b = np.asarray(L0_attn_conv_b, dtype=np.float32)
    dt_w = np.asarray(L0_attn_dt_w, dtype=np.float32)
    dt_b = np.asarray(L0_attn_dt_b, dtype=np.float32)
    b_w = np.asarray(L0_attn_b_w, dtype=np.float32)
    c_w = np.asarray(L0_attn_c_w, dtype=np.float32)
    a_log = np.asarray(L0_attn_A_log, dtype=np.float32)
    d_par = np.asarray(L0_attn_D_param, dtype=np.float32)
    out_w = np.asarray(L0_attn_out_w, dtype=np.float32)
    out_b = np.asarray(L0_attn_out_b, dtype=np.float32)
    ffn_w = np.asarray(L0_ffn_weight, dtype=np.float32)
    ffn_b = np.asarray(L0_ffn_bias, dtype=np.float32)

    win = np.ascontiguousarray(in_w.reshape(32, 128, 8, 128).transpose(0, 3, 2, 1)
                               ).reshape(32, 128, 1024)

    def big_t(w):
        return np.ascontiguousarray(w.reshape(16, 128, 16, 128).transpose(0, 3, 2, 1)).reshape(16, 128, 2048)
    dtw = big_t(dt_w)
    bw = big_t(b_w)
    cw = big_t(c_w)
    ow = np.ascontiguousarray(out_w.T.reshape(16, 128, 1024).transpose(1, 0, 2)).reshape(128, 16384)
    inb = np.ascontiguousarray(in_b.reshape(32, 128).T)
    cwv = np.ascontiguousarray(conv_w[:, 0, :].reshape(16, 128, 4).transpose(1, 0, 2)).reshape(128, 64)
    dtb = np.ascontiguousarray(dt_b.reshape(16, 128).T)
    cvb = np.ascontiguousarray(conv_b.reshape(16, 128).T)
    alog = np.ascontiguousarray(a_log.reshape(16, 128).T)
    dpar = np.ascontiguousarray(d_par.reshape(16, 128).T)
    outbt = np.ascontiguousarray(np.broadcast_to(out_b[None, :], (128, 1024)))
    ffnbt = np.ascontiguousarray(np.broadcast_to(ffn_b[None, :], (128, 1024)))
    eye = np.eye(128).astype(ml_dtypes.bfloat16)

    in_maps = make_in_maps(x, win, dtw, bw, cw, ow, ffn_w, inb, cwv, cvb, dtb, alog,
                           dpar, outbt, ffnbt, eye)
    return run_all(in_maps)


def make_in_maps(x, win, dtw, bw, cw, ow, ffn_w, inb, cwv, cvb, dtb, alog, dpar,
                 outbt, ffnbt, eye):
    in_maps = []
    for c in range(8):
        b, half = c // 2, c % 2
        xs = np.zeros((TPAD, D), np.float32)
        if half == 1:
            xs[0:HALO] = x[b, T - HALO:T]
        xs[HALO:HALO + T] = x[b, half * T:(half + 1) * T]
        halo = np.zeros((128, 1), np.float32)
        if half == 1:
            halo[:, 0] = 1.0
        in_maps.append({
            "x": xs, "win": win, "dtw": dtw, "bw": bw, "cw": cw, "ow": ow, "fw": ffn_w,
            "inb": inb, "cwv": cwv, "cvb": cvb, "dtb": dtb, "alog": alog, "dpar": dpar,
            "outb": outbt, "ffnb": ffnbt, "eye": eye, "halo": halo,
        })
    return in_maps


_LAST_RES = None


def run_all(in_maps):
    global _LAST_RES
    nc = _get_nc()
    res = run_bass_kernel_spmd(nc, in_maps, core_ids=list(range(8)))
    _LAST_RES = res
    out = np.empty((4, 4096, 1024), np.float32)
    for c in range(8):
        b, half = c // 2, c % 2
        out[b, half * T:(half + 1) * T] = res.results[c]["y_out"]
    return out


# version 32 (final)
# speedup vs baseline: 1.0658x; 1.0095x over previous
import contextlib
import numpy as np
import ml_dtypes
import concourse.bass as bass
import concourse.mybir as mybir
from concourse.bass_utils import run_bass_kernel_spmd

F32 = mybir.dt.float32
BF16 = mybir.dt.bfloat16
I32 = mybir.dt.int32
AF = mybir.ActivationFunctionType
ALU = mybir.AluOpType
AX = mybir.AxisListType

T = 2048
HALO = 3
TH = T + HALO
TPAD = 2064
D = 1024
DI = 2048
NCH = 16
NDC = 8

ENGS = ("sync", "gpsimd", "scalar", "vector", "tensor")


def build_nc(debug=False):
    nc = bass.Bass("TRN2", target_bir_lowering=False)

    d_x = nc.dram_tensor("x", [TPAD, D], F32, kind="ExternalInput").ap()
    d_win = nc.dram_tensor("win", [32, 128, 1024], F32, kind="ExternalInput").ap()
    d_dtw = nc.dram_tensor("dtw", [16, 128, 2048], F32, kind="ExternalInput").ap()
    d_bw = nc.dram_tensor("bw", [16, 128, 2048], F32, kind="ExternalInput").ap()
    d_cw = nc.dram_tensor("cw", [16, 128, 2048], F32, kind="ExternalInput").ap()
    d_ow = nc.dram_tensor("ow", [128, 16384], F32, kind="ExternalInput").ap()
    d_fw = nc.dram_tensor("fw", [1024, 1024], F32, kind="ExternalInput").ap()
    d_inb = nc.dram_tensor("inb", [128, 32], F32, kind="ExternalInput").ap()
    d_cwv = nc.dram_tensor("cwv", [128, 64], F32, kind="ExternalInput").ap()
    d_cvb = nc.dram_tensor("cvb", [128, 16], F32, kind="ExternalInput").ap()
    d_dtb = nc.dram_tensor("dtb", [128, 16], F32, kind="ExternalInput").ap()
    d_alog = nc.dram_tensor("alog", [128, 16], F32, kind="ExternalInput").ap()
    d_dpar = nc.dram_tensor("dpar", [128, 16], F32, kind="ExternalInput").ap()
    d_outb = nc.dram_tensor("outb", [128, 1024], F32, kind="ExternalInput").ap()
    d_ffnb = nc.dram_tensor("ffnb", [128, 1024], F32, kind="ExternalInput").ap()
    d_eye = nc.dram_tensor("eye", [128, 128], BF16, kind="ExternalInput").ap()
    d_halo = nc.dram_tensor("halo", [128, 1], F32, kind="ExternalInput").ap()
    d_out = nc.dram_tensor("y_out", [T, D], F32, kind="ExternalOutput").ap()

    dk = "Internal"
    if debug:
        d_dbg = nc.dram_tensor("dbg0", [4, 128, 1024], F32, kind="ExternalOutput").ap()
        d_dbg2 = nc.dram_tensor("dbg1", [128, 2048], F32, kind="ExternalOutput").ap()
        d_dbg3 = nc.dram_tensor("dbg2", [128, 1024], F32, kind="ExternalOutput").ap()
    s_xbf = nc.dram_tensor("s_xbf", [TPAD, D], BF16, kind="Internal").ap()
    s_u32 = nc.dram_tensor("s_u32", [NCH, 128, T], F32, kind=dk).ap()
    s_g = nc.dram_tensor("s_g", [NCH, 128, T], F32, kind=dk).ap()
    s_ybf = nc.dram_tensor("s_ybf", [NCH, 128, T], BF16, kind=dk).ap()
    s_wd = nc.dram_tensor("s_wd", [128, 8192], BF16, kind=dk).ap()
    d_ccin = [nc.dram_tensor(f"ccin{c}", [128, 1], F32, kind="Internal").ap() for c in range(NCH)]
    d_ccout = [nc.dram_tensor(f"ccout{c}", [128, 1], F32, kind="Internal").ap() for c in range(NCH)]

    es = contextlib.ExitStack()

    sem = {e: es.enter_context(nc.semaphore(f"sm_{e}")) for e in ENGS}
    ccsem = [es.enter_context(nc.semaphore(f"cc{c}")) for c in range(NCH)]
    cnt = {e: 0 for e in ENGS}
    streams = {e: [] for e in ENGS}

    def emit(eng, fn):
        streams[eng].append(fn)

    def alloc(eng, amt=1):
        cnt[eng] += amt
        return (eng, cnt[eng])

    slot_sems = {}
    slot_gen = {}

    def slot_tok(name):
        return ("S", name, slot_gen.get(name, 0) * 16)

    def mkwait(toks):
        merged = {}
        for t_ in toks:
            if t_ is None:
                continue
            if t_[0] == "S":
                _, nm, v = t_
                key = ("S", nm)
            else:
                key = t_[0]
                v = t_[1]
            merged[key] = max(merged.get(key, 0), v)
        return tuple(merged.items())

    def do_wait(q, merged):
        for key, v in merged:
            if isinstance(key, tuple):
                q.wait_ge(slot_sems[key[1]], v)
            else:
                q.wait_ge(sem[key], v)

    def barrier_now():
        return [(e, cnt[e]) for e in ENGS] + [slot_tok(s) for s in slot_gen]

    t_consts = es.enter_context(nc.sbuf_tensor("cst", [128, 4], F32))
    t_inb = es.enter_context(nc.sbuf_tensor("sb_inb", [128, 32], F32))
    t_cwv = es.enter_context(nc.sbuf_tensor("sb_cwv", [128, 64], F32))
    t_cvb = es.enter_context(nc.sbuf_tensor("sb_cvb", [128, 16], F32))
    t_dtb = es.enter_context(nc.sbuf_tensor("sb_dtb", [128, 16], F32))
    t_negA = es.enter_context(nc.sbuf_tensor("negA", [128, 16], F32))
    t_negAe = es.enter_context(nc.sbuf_tensor("negAe", [128, 16], F32))
    t_dpar = es.enter_context(nc.sbuf_tensor("sb_dpar", [128, 16], F32))
    t_eye = es.enter_context(nc.sbuf_tensor("sb_eye", [128, 128], BF16))
    t_halo = es.enter_context(nc.sbuf_tensor("sb_halo", [128, 1], F32))
    t_evm = es.enter_context(nc.sbuf_tensor("evm", [128, 1], F32))

    nc.const_aps.register(F32, -1.0, t_consts[:, 0:1])
    nc.const_aps.register(F32, 1e-5, t_consts[:, 1:2])
    nc.const_aps.register(F32, -0.5, t_consts[:, 2:3])

    pbank = [es.enter_context(nc.psum_tensor(f"pb{i}", [128, 512], F32)) for i in range(6)]
    p_bf = es.enter_context(nc.psum_tensor("pbf", [128, 1024], BF16))
    bank_w = [None] * 6
    bank_r = [[] for _ in range(6)]

    def dma(eng, out_ap, in_ap, waits=(), slot=None, transpose=False):
        assert slot is not None
        w = mkwait(waits)
        if slot not in slot_sems:
            slot_sems[slot] = es.enter_context(nc.semaphore(f"ds_{slot}"))
            slot_gen[slot] = 0
        slot_gen[slot] += 1
        ssem = slot_sems[slot]
        tok = ("S", slot, slot_gen[slot] * 16)

        def body(q, w=w, out_ap=out_ap, in_ap=in_ap, ssem=ssem, transpose=transpose):
            do_wait(q, w)
            if transpose:
                q.dma_start_transpose(out_ap, in_ap).then_inc(ssem, 16)
            else:
                q.dma_start(out=out_ap, in_=in_ap).then_inc(ssem, 16)
        emit(eng, body)
        return tok

    def mm_group(lhsTs, rhss, bk, waits=(), psum_ap=None):
        w = mkwait(list(waits) + bank_r[bk])
        bank_r[bk] = []
        tok = alloc("tensor")
        out_ap = psum_ap if psum_ap is not None else pbank[bk][:, :]

        def body(t, w=w, lhsTs=tuple(lhsTs), rhss=tuple(rhss), out_ap=out_ap):
            do_wait(t, w)
            n = len(lhsTs)
            for i in range(n):
                ins = t.matmul(out_ap, lhsTs[i], rhss[i], start=(i == 0), stop=(i == n - 1))
            ins.then_inc(sem["tensor"], 1)
        emit("tensor", body)
        bank_w[bk] = tok
        return tok

    def op(eng, builder, waits=(), selfwait=False, amt=1):
        w = mkwait(waits)
        tok = alloc(eng, amt)
        val = cnt[eng]

        def body(q, w=w, builder=builder, val=val, amt=amt, selfwait=selfwait):
            do_wait(q, w)
            builder(q).then_inc(sem[eng], amt)
            if selfwait:
                q.wait_ge(sem[eng], val)
        emit(eng, body)
        return tok

    def pre_g(g):
        g.memset(t_consts[:, 0:1], -1.0)
        g.memset(t_consts[:, 1:2], 1e-5)
        g.memset(t_consts[:, 2:3], -0.5).then_inc(sem["gpsimd"], 1)
    emit("gpsimd", pre_g)
    tok_consts = alloc("gpsimd")

    tok_smalls = None
    for dst, src in ((t_inb, d_inb), (t_cwv, d_cwv), (t_cvb, d_cvb), (t_dtb, d_dtb),
                     (t_negA, d_alog), (t_dpar, d_dpar), (t_eye, d_eye), (t_halo, d_halo)):
        tok_smalls = dma("sync", dst[:, :], src, slot="smalls")

    tok_nA_s = op("scalar", lambda s: s.activation(t_negA[:, :], t_negA[:, :], AF.Exp),
                  waits=[tok_smalls], selfwait=True)
    op("vector", lambda v: v.tensor_scalar(t_negA[:, :], t_negA[:, :], -1.0, None, ALU.mult),
       waits=[tok_nA_s], selfwait=True)
    op("vector", lambda v: v.tensor_scalar(t_negAe[:, :], t_negA[:, :], 1e-4, None, ALU.mult),
       selfwait=True)
    tok_negA = op("vector", lambda v: v.tensor_scalar(t_evm[:, :], t_halo[:, :], -1.0, 1.0,
                                                      ALU.mult, ALU.add), selfwait=True)

    tok_xbf = dma("gpsimd", s_xbf, d_x, slot="xcast")
    abank = 0

    esA = contextlib.ExitStack()
    u_bf = es.enter_context(nc.sbuf_tensor("u_bf", [128, NCH, T], BF16))
    ysl01 = [es.enter_context(nc.sbuf_tensor("ysl01_0", [128, NCH, 128], BF16))]
    x_fm = esA.enter_context(nc.sbuf_tensor("x_fm", [128, NDC, TPAD], BF16))
    a_slab = [esA.enter_context(nc.sbuf_tensor(f"aslab{i}", [128, NDC, 128], BF16)) for i in range(2)]
    t_xbr = esA.enter_context(nc.sbuf_tensor("xbr", [128, TH], F32))
    t_acc = esA.enter_context(nc.sbuf_tensor("cacc", [128, T], F32))
    u32b = [esA.enter_context(nc.sbuf_tensor(f"u32b{i}", [128, T], F32)) for i in range(2)]
    gb = [esA.enter_context(nc.sbuf_tensor(f"gb{i}", [128, T], F32)) for i in range(2)]
    wq_bf = esA.enter_context(nc.sbuf_tensor("wq_bf", [128, NDC, 1024], BF16))
    wqf = [esA.enter_context(nc.sbuf_tensor(f"wqf{i}", [128, 1024], F32)) for i in range(2)]
    a_q1 = esA.enter_context(nc.sbuf_tensor("wq1", [128, 1024], F32))
    a_q2 = esA.enter_context(nc.sbuf_tensor("wq2", [128, 1024], F32))
    a_q3 = esA.enter_context(nc.sbuf_tensor("wq3", [128, 1024], F32))
    a_q4 = esA.enter_context(nc.sbuf_tensor("wq4", [128, 1024], F32))
    a_qi = esA.enter_context(nc.sbuf_tensor("wqi", [128, 1024], I32))
    a_qsc = esA.enter_context(nc.sbuf_tensor("wqsc", [128, 32], F32))
    a_qrs = esA.enter_context(nc.sbuf_tensor("wqrs", [128, 32], F32))
    wdst = esA.enter_context(nc.sbuf_tensor("wdst", [128, 1024], BF16))

    def make_quant(q1, q2, q3, q4, qi, qsc, qrs, eng="vector", qsp=None, nblk=32):
        if eng == "split":
            def emit_split(src_ap, out_bf_ap, waits):
                src3 = src_ap.rearrange("p (n m) -> p n m", m=32)
                rs_b = qrs[:, :].unsqueeze(2).broadcast_to([128, 32, 32])
                sc_b = qsc[:, :].unsqueeze(2).broadcast_to([128, 32, 32])
                op("vector", lambda v: v.reduce_max(qsc[:, :], src3, axis=AX.X,
                                                    apply_absolute_value=True),
                   waits=waits, selfwait=True)
                op("vector", lambda v: v.tensor_scalar(qsc[:, :], qsc[:, :], 1.0 / 7.5, 1e-30,
                                                       ALU.mult, ALU.max), selfwait=True)
                tv1 = op("vector", lambda v: v.reciprocal(qrs[:, :], qsc[:, :]), selfwait=True)
                op("gpsimd", lambda g: g.tensor_tensor(
                    q1[:, :].rearrange("p (n m) -> p n m", m=32), src3, rs_b, ALU.mult),
                   waits=[tv1], selfwait=True)
                tg1 = op("gpsimd", lambda g: g.tensor_mul(q2[:, :], q1[:, :], q1[:, :]),
                         selfwait=True)
                op("vector", lambda v: v.tensor_scalar(q3[:, :], q2[:, :], 4.0, None, ALU.is_ge),
                   waits=[tg1], selfwait=True)
                tv2 = op("vector", lambda v: v.tensor_scalar(q2[:, :], q2[:, :], 16.0, None,
                                                             ALU.is_ge), selfwait=True)
                op("gpsimd", lambda g: g.tensor_scalar(q4[:, :], q3[:, :], 0.125, 0.125,
                                                       ALU.mult, ALU.add),
                   waits=[tv2], selfwait=True)
                op("gpsimd", lambda g: g.tensor_scalar(qsp[:, :], q2[:, :], 0.25, None, ALU.mult),
                   selfwait=True)
                op("gpsimd", lambda g: g.tensor_add(q4[:, :], q4[:, :], qsp[:, :]), selfwait=True)
                op("gpsimd", lambda g: g.tensor_scalar(q3[:, :], q3[:, :], -4.0, 8.0, ALU.mult,
                                                       ALU.add), selfwait=True)
                op("gpsimd", lambda g: g.tensor_scalar(qsp[:, :], q2[:, :], -2.0, None, ALU.mult),
                   selfwait=True)
                op("gpsimd", lambda g: g.tensor_add(q3[:, :], q3[:, :], qsp[:, :]), selfwait=True)
                op("gpsimd", lambda g: g.tensor_mul(q1[:, :], q1[:, :], q3[:, :]), selfwait=True)
                op("gpsimd", lambda g: g.tensor_scalar(q1[:, :], q1[:, :], -1.0, 0.5, ALU.mult,
                                                       ALU.add), selfwait=True)
                op("gpsimd", lambda g: g.tensor_copy(qi[:, :], q1[:, :]), selfwait=True)
                tg2 = op("gpsimd", lambda g: g.tensor_copy(q2[:, :], qi[:, :]), selfwait=True)
                tv3 = op("vector", lambda v: v.tensor_tensor(q3[:, :], q2[:, :], q1[:, :],
                                                             ALU.is_gt),
                         waits=[tg2], selfwait=True)
                op("gpsimd", lambda g: g.tensor_sub(q2[:, :], q2[:, :], q3[:, :]),
                   waits=[tv3], selfwait=True)
                op("gpsimd", lambda g: g.tensor_mul(q2[:, :], q2[:, :], q4[:, :]), selfwait=True)
                op("gpsimd", lambda g: g.tensor_scalar(q2[:, :], q2[:, :], -1.0, None, ALU.mult),
                   selfwait=True)
                op("gpsimd", lambda g: g.tensor_tensor(
                    q2[:, :].rearrange("p (n m) -> p n m", m=32),
                    q2[:, :].rearrange("p (n m) -> p n m", m=32), sc_b, ALU.mult), selfwait=True)
                return op("gpsimd", lambda g: g.tensor_copy(out_bf_ap, q2[:, :]), selfwait=True)
            return emit_split

        def emit_quant(src_ap, out_bf_ap, waits, sink=None):
            holder = {}
            steps = []

            def op_q(eng_, builder, waits_=()):
                def run(eng_=eng_, builder=builder, waits_=waits_):
                    prev = holder.get("prev")
                    ws = list(waits_) + ([prev] if prev is not None else [])
                    holder["prev"] = op(eng_, builder, waits=ws)
                steps.append(run)

            src3 = src_ap.rearrange("p (n m) -> p n m", m=32)
            rs_b = qrs[:, :].unsqueeze(2).broadcast_to([128, nblk, 32])
            sc_b = qsc[:, :].unsqueeze(2).broadcast_to([128, nblk, 32])
            op_q("vector", lambda v: v.reduce_max(qsc[:, :], src3, axis=AX.X,
                                                  apply_absolute_value=True),
                 waits_=waits)
            op_q("vector", lambda v: v.tensor_scalar(qsc[:, :], qsc[:, :], 1.0 / 7.5, 1e-30,
                                                     ALU.mult, ALU.max))
            op_q("vector", lambda v: v.reciprocal(qrs[:, :], qsc[:, :]))
            op_q(eng, lambda e: e.tensor_tensor(q1[:, :].rearrange("p (n m) -> p n m", m=32),
                                                src3, rs_b, ALU.mult))
            op_q(eng, lambda e: e.tensor_mul(q2[:, :], q1[:, :], q1[:, :]))
            op_q(eng, lambda e: e.tensor_scalar(q3[:, :], q2[:, :], 4.0, None, ALU.is_ge))
            op_q(eng, lambda e: e.tensor_scalar(q2[:, :], q2[:, :], 16.0, None, ALU.is_ge))
            op_q(eng, lambda e: e.tensor_scalar(q4[:, :], q3[:, :], 0.125, 0.125, ALU.mult,
                                                ALU.add))
            op_q(eng, lambda e: e.scalar_tensor_tensor(q4[:, :], q2[:, :], 0.25, q4[:, :],
                                                       ALU.mult, ALU.add))
            op_q(eng, lambda e: e.tensor_scalar(q3[:, :], q3[:, :], -4.0, 8.0, ALU.mult,
                                                ALU.add))
            op_q(eng, lambda e: e.scalar_tensor_tensor(q3[:, :], q2[:, :], -2.0, q3[:, :],
                                                       ALU.mult, ALU.add))
            op_q(eng, lambda e: e.tensor_mul(q1[:, :], q1[:, :], q3[:, :]))
            op_q(eng, lambda e: e.tensor_scalar(q1[:, :], q1[:, :], -1.0, 0.5, ALU.mult,
                                                ALU.add))
            op_q(eng, lambda e: e.tensor_copy(qi[:, :], q1[:, :]))
            op_q(eng, lambda e: e.tensor_copy(q2[:, :], qi[:, :]))
            op_q(eng, lambda e: e.tensor_tensor(q3[:, :], q2[:, :], q1[:, :], ALU.is_gt))
            op_q(eng, lambda e: e.tensor_sub(q2[:, :], q2[:, :], q3[:, :]))
            op_q(eng, lambda e: e.tensor_mul(q2[:, :], q2[:, :], q4[:, :]))
            op_q(eng, lambda e: e.scalar_tensor_tensor(
                q2[:, :].rearrange("p (n m) -> p n m", m=32),
                q2[:, :].rearrange("p (n m) -> p n m", m=32), -1.0, sc_b,
                ALU.mult, ALU.mult))

            def _final():
                holder["tok"] = op(eng, lambda e: e.tensor_copy(out_bf_ap, q2[:, :]),
                                   waits=[holder["prev"]], selfwait=True)
            steps.append(_final)
            if sink is None:
                for s_ in steps:
                    s_()
                return holder["tok"]
            sink.extend(steps)
            return holder
        return emit_quant

    quant_gh = [make_quant(a_q1[:, h * 512:(h + 1) * 512], a_q2[:, h * 512:(h + 1) * 512],
                           a_q3[:, h * 512:(h + 1) * 512], a_q4[:, h * 512:(h + 1) * 512],
                           a_qi[:, h * 512:(h + 1) * 512], a_qsc[:, h * 16:(h + 1) * 16],
                           a_qrs[:, h * 16:(h + 1) * 16], nblk=16) for h in range(2)]

    tok_xfm = None
    for c in range(NDC):
        tok_xfm = dma("scalar", x_fm[:, c, :], s_xbf[:, c * 128:(c + 1) * 128],
                      waits=[tok_xbf], slot="xfm", transpose=True)

    XB_TILES = [(0, 512), (512, 512), (1024, 512), (1536, 512), (2048, 3)]
    Z_TILES = [(3, 512), (515, 512), (1027, 512), (1539, 512)]

    a_slab_free = [[], []]
    u32_free = [[], []]
    g_free = [[], []]
    xbr_free = []
    acc_free = []
    ubf_last = None
    u32_wr = [None] * NCH
    g_wr = [None] * NCH
    wqf_free = [[], []]
    wq_holders = []
    wq_pending = []
    slab_tok_a = [None] * 32

    def emit_a_slab(c):
        sb = c % 2
        slab_tok_a[c] = dma("gpsimd",
                            a_slab[sb][:, :, :].rearrange("p s j -> p (s j)"),
                            d_win[c], waits=a_slab_free[sb] + [tok_consts],
                            slot=f"aslab{sb}")
        a_slab_free[sb] = []

    emit_a_slab(0)

    for c in range(32):
        sb = c % 2
        if c + 1 < 32:
            emit_a_slab(c + 1)
        tiles = XB_TILES if c < 16 else Z_TILES
        evac_toks = []
        for ti, (off, w) in enumerate(tiles):
            bk = abank % 6
            abank += 1
            mm_group([a_slab[sb][:, s_, :] for s_ in range(NDC)],
                     [x_fm[:, s_, off:off + w] for s_ in range(NDC)],
                     bk, waits=[slab_tok_a[c], tok_xfm],
                     psum_ap=pbank[bk][:, 0:w])
            if c < 16:
                etok = op("scalar", lambda s, bk=bk, off=off, w=w, c=c: s.activation(
                    t_xbr[:, off:off + w], pbank[bk][:, 0:w], AF.Identity,
                    bias=t_inb[:, c:c + 1]),
                    waits=[bank_w[bk]] + xbr_free)
            else:
                etok = op("scalar", lambda s, bk=bk, off=off, w=w, c=c, sb=sb: s.activation(
                    gb[sb][:, off - 3:off - 3 + w], pbank[bk][:, 0:w], AF.Sigmoid,
                    bias=t_inb[:, c:c + 1]),
                    waits=[bank_w[bk]] + g_free[sb])
            bank_r[bk].append(etok)
            evac_toks.append(etok)
        a_slab_free[sb].append(bank_w[(abank - 1) % 6])
        if c < 16:
            op("vector", lambda v: v.tensor_scalar(
                t_xbr[:, 0:HALO], t_xbr[:, 0:HALO], t_halo[:, 0:1], None, ALU.mult),
               waits=evac_toks + [tok_negA] + acc_free + u32_free[sb] + xbr_free, selfwait=True)
            xbr_free = []
            acc_free = []
            u32_free[sb] = []
            op("vector", lambda v, c=c: v.tensor_scalar(
                t_acc[:, :], t_xbr[:, 0:T], t_cwv[:, 4 * c:4 * c + 1], None, ALU.mult))
            for k in range(1, 4):
                op("vector", lambda v, c=c, k=k: v.scalar_tensor_tensor(
                    t_acc[:, :], t_xbr[:, k:k + T], t_cwv[:, 4 * c + k:4 * c + k + 1],
                    t_acc[:, :], ALU.mult, ALU.add),
                   selfwait=(k == 3))
            tok_conv = ("vector", cnt["vector"])
            xbr_free.append(tok_conv)
            tok_silu = op("scalar", lambda s, c=c, sb=sb: s.activation(
                u32b[sb][:, :], t_acc[:, :], AF.Silu, bias=t_cvb[:, c:c + 1]), waits=[tok_conv])
            acc_free.append(tok_silu)
            ubf_last = op("scalar", lambda s, c=c, sb=sb: s.copy(u_bf[:, c, :], u32b[sb][:, :]),
                          waits=[tok_silu])
            u32_free[sb].append(ubf_last)
            u32_wr[c] = dma("sync", s_u32[c], u32b[sb][:, :], waits=[ubf_last], slot=f"u32s{sb}")
            u32_free[sb].append(u32_wr[c])
        else:
            cz = c - 16
            g_wr[cz] = dma("sync", s_g[cz], gb[sb][:, :], waits=evac_toks, slot=f"gs{sb}")
            g_free[sb] = [g_wr[cz]]
            if cz % 2 == 0:
                i = cz // 2
                tok_wl = dma("sync", wqf[i % 2][:, :], d_fw[i * 128:(i + 1) * 128, :],
                             waits=wqf_free[i % 2], slot=f"wqf{i % 2}")
                wqf_free[i % 2] = []
                s_h = [[], []]
                hh_ = [quant_gh[h](wqf[i % 2][:, h * 512:(h + 1) * 512],
                                   wq_bf[:, i, h * 512:(h + 1) * 512],
                                   [tok_wl], sink=s_h[h]) for h in range(2)]
                while s_h[0] or s_h[1]:
                    for h in range(2):
                        if s_h[h]:
                            s_h[h].pop(0)()
                wq_holders.extend(hh_)
                wqf_free[i % 2] = [hh_[0]["tok"], hh_[1]["tok"]]

    while wq_pending:
        wq_pending.pop(0)()
    wq_toks = [h["tok"] for h in wq_holders]

    bf_rd = []
    wd_wr = None
    for j in range(NDC):
        w = mkwait(wq_toks + bf_rd + [tok_smalls])
        bf_rd = []
        tok_tp = alloc("tensor")

        def body(t, w=w, j=j):
            do_wait(t, w)
            for i in range(NDC):
                ins = t.transpose(p_bf[:, i * 128:(i + 1) * 128],
                                  wq_bf[:, i, j * 128:(j + 1) * 128], t_eye[:, :])
            ins.then_inc(sem["tensor"], 1)
        emit("tensor", body)
        tok_ev = op("scalar", lambda s: s.copy(wdst[:, :], p_bf[:, :]), waits=[tok_tp])
        bf_rd.append(tok_ev)
        wd_wr = dma("gpsimd", s_wd[:, j * 1024:(j + 1) * 1024], wdst[:, :], waits=[tok_ev], slot="wds")
        bf_rd.append(wd_wr)

    barrierA = barrier_now()
    esA.close()

    esC = contextlib.ExitStack()
    c_slab = {wn: [esC.enter_context(nc.sbuf_tensor(f"cs_{wn}{i}", [128, NCH, 128], BF16))
                   for i in range(2)] for wn in ("dt", "bw", "cw")}
    u32p = [esC.enter_context(nc.sbuf_tensor(f"u32p{i}", [128, T], F32)) for i in range(2)]
    gp = [esC.enter_context(nc.sbuf_tensor(f"gp{i}", [128, T], F32)) for i in range(2)]
    t_px = [esC.enter_context(nc.sbuf_tensor(f"px{i}", [128, T], F32)) for i in range(2)]
    t_dec = [esC.enter_context(nc.sbuf_tensor(f"dec{i}", [128, T], F32)) for i in range(2)]
    t_B = [esC.enter_context(nc.sbuf_tensor(f"tB{i}", [128, T], F32)) for i in range(2)]
    t_C = [esC.enter_context(nc.sbuf_tensor(f"tC{i}", [128, T], F32)) for i in range(2)]
    t_st = esC.enter_context(nc.sbuf_tensor("st", [128, T], F32))
    t_ybf = esC.enter_context(nc.sbuf_tensor("ybf", [128, T], BF16))
    t_scol = esC.enter_context(nc.sbuf_tensor("scol", [128, 1], F32))
    t_sini = esC.enter_context(nc.sbuf_tensor("sini", [128, 1], F32))

    dws = {"dt": d_dtw, "bw": d_bw, "cw": d_cw}
    cs_free = {wn: [[], []] for wn in ("dt", "bw", "cw")}
    u32p_free = [[], []]
    gp_free = [[], []]
    px_free = [[], []]
    decf = [[], []]
    Bf = [[], []]
    Cf = [[], []]
    st_free, ybf_free, sini_free, scol_free = [], [], [], []
    ybf_wr = [None] * NCH
    slab_tok_c = [None] * NCH
    cbank = 0
    saved = {}

    def emit_c_slabs(c, first=False):
        sb = c % 2
        toks = {}
        for wn in ("dt", "bw", "cw"):
            toks[wn] = dma("gpsimd", c_slab[wn][sb][:, :, :].rearrange("p s j -> p (s j)"),
                           dws[wn][c], waits=cs_free[wn][sb] + (barrierA if first else []),
                           slot=f"c{wn}{sb}")
            cs_free[wn][sb] = []
        slab_tok_c[c] = toks

    emit_c_slabs(0, first=True)
    ow_v = u_bf[:, 0:8, :].rearrange("p c j -> p (c j)")
    wd_v = u_bf[:, 8:12, :].rearrange("p c j -> p (c j)")
    tok_ow = None

    def emit_deferred(c):
        nonlocal st_free, ybf_free
        if c < 0:
            return
        sb = c % 2
        sv = saved[c]
        op("vector", lambda v: v.tensor_scalar(t_sini[:, :], t_sini[:, :], t_halo[:, 0:1],
                                               None, ALU.mult),
           waits=[sv["recv"]] + st_free, selfwait=True)
        st_free = []
        op("vector", lambda v, sb=sb: v.tensor_tensor_scan(
            t_st[:, :], t_dec[sb][:, :], t_B[sb][:, :], t_sini[:, 0:1], ALU.mult, ALU.add),
           selfwait=True)
        sini_free.append(("vector", cnt["vector"]))
        op("vector", lambda v, sb=sb: v.tensor_mul(t_st[:, :], t_C[sb][:, :], t_st[:, :]),
           waits=sv["C_evacs"])
        op("vector", lambda v, sb=sb, c=c: v.scalar_tensor_tensor(
            t_st[:, :], u32p[sb][:, :], t_dpar[:, c:c + 1], t_st[:, :], ALU.mult, ALU.add))
        tok_y = op("vector", lambda v, sb=sb: v.tensor_mul(t_st[:, :], t_st[:, :], gp[sb][:, :]),
                   waits=[sv["pf"]], selfwait=True)
        decf[sb].append(tok_y)
        Bf[sb].append(tok_y)
        Cf[sb].append(tok_y)
        u32p_free[sb].append(tok_y)
        gp_free[sb].append(tok_y)
        tok_ybf = op("scalar", lambda s: s.copy(t_ybf[:, :], t_st[:, :]),
                     waits=[tok_y] + ybf_free)
        st_free.append(tok_ybf)
        ybf_wr[c] = dma("sync", s_ybf[c], t_ybf[:, :], waits=[tok_ybf], slot="ybfs")
        ybf_free = [ybf_wr[c]]
        dma("sync", ysl01[0][:, c, :], s_ybf[c, :, 0:128], waits=[ybf_wr[c]], slot="y0f")

    for c in range(NCH):
        sb = c % 2
        if c + 1 < NCH:
            emit_c_slabs(c + 1)
        tok_pf_u = dma("sync", u32p[sb][:, :], s_u32[c],
                       waits=u32p_free[sb] + [u32_wr[c]] + (barrierA if c == 0 else []),
                       slot=f"u32p{sb}")
        u32p_free[sb] = []
        tok_pf = dma("sync", gp[sb][:, :], s_g[c], waits=gp_free[sb] + [g_wr[c]], slot=f"gp{sb}")
        gp_free[sb] = []

        dt_evacs = []
        for ti in range(4):
            bk = cbank % 6
            cbank += 1
            mm_group([c_slab["dt"][sb][:, s_, :] for s_ in range(NCH)],
                     [u_bf[:, s_, ti * 512:(ti + 1) * 512] for s_ in range(NCH)],
                     bk, waits=[slab_tok_c[c]["dt"], ubf_last])
            etok = op("scalar", lambda s, bk=bk, ti=ti, c=c, sb=sb: s.activation(
                t_px[sb][:, ti * 512:(ti + 1) * 512], pbank[bk][:, :], AF.Identity,
                bias=t_dtb[:, c:c + 1]), waits=[bank_w[bk]] + px_free[sb])
            bank_r[bk].append(etok)
            dt_evacs.append(etok)
            if ti == 3:
                px_free[sb] = []

        B_evacs = []
        C_evacs = []
        for ti in range(4):
            bk = cbank % 6
            cbank += 1
            mm_group([c_slab["bw"][sb][:, s_, :] for s_ in range(NCH)],
                     [u_bf[:, s_, ti * 512:(ti + 1) * 512] for s_ in range(NCH)],
                     bk, waits=[slab_tok_c[c]["bw"]])
            etok = op("scalar", lambda s, bk=bk, ti=ti, sb=sb: s.activation(
                t_B[sb][:, ti * 512:(ti + 1) * 512], pbank[bk][:, :], AF.Tanh),
                waits=[bank_w[bk]] + Bf[sb])
            bank_r[bk].append(etok)
            B_evacs.append(etok)
            if ti == 3:
                Bf[sb] = []
        for ti in range(4):
            bk = cbank % 6
            cbank += 1
            mm_group([c_slab["cw"][sb][:, s_, :] for s_ in range(NCH)],
                     [u_bf[:, s_, ti * 512:(ti + 1) * 512] for s_ in range(NCH)],
                     bk, waits=[slab_tok_c[c]["cw"]])
            etok = op("scalar", lambda s, bk=bk, ti=ti, sb=sb: s.activation(
                t_C[sb][:, ti * 512:(ti + 1) * 512], pbank[bk][:, :], AF.Tanh),
                waits=[bank_w[bk]] + Cf[sb])
            bank_r[bk].append(etok)
            C_evacs.append(etok)
            if ti == 3:
                Cf[sb] = []
        last_mm = ("tensor", cnt["tensor"])
        for wn in ("dt", "bw", "cw"):
            cs_free[wn][sb] = [last_mm]
        if c == NCH - 1:
            tok_ow = dma("gpsimd", ow_v, d_ow, waits=[last_mm], slot="ow")

        op("scalar", lambda s, sb=sb: s.activation(t_px[sb][:, :], t_px[sb][:, :], AF.Exp),
           waits=dt_evacs + [tok_consts], selfwait=True)
        tok_sp = op("scalar", lambda s, sb=sb: s.activation(t_px[sb][:, :], t_px[sb][:, :],
                                                            AF.Ln, bias=1.0), selfwait=True)
        tok_dec = op("scalar", lambda s, c=c, sb=sb: s.activation(
            t_dec[sb][:, :], t_px[sb][:, :], AF.Exp, bias=t_negAe[:, c:c + 1],
            scale=t_negA[:, c:c + 1]), waits=[tok_sp, tok_negA] + decf[sb])
        decf[sb] = []

        op("vector", lambda v, sb=sb: v.tensor_mul(t_B[sb][:, :], t_B[sb][:, :], u32p[sb][:, :]),
           waits=[tok_dec, tok_pf_u] + B_evacs)
        op("vector", lambda v, sb=sb: v.tensor_scalar(t_px[sb][:, :], t_dec[sb][:, :], -1.0, 1.0,
                                                      ALU.mult, ALU.add))
        op("vector", lambda v, sb=sb: v.tensor_mul(t_B[sb][:, :], t_px[sb][:, :], t_B[sb][:, :]),
           selfwait=True)
        px_free[sb].append(("vector", cnt["vector"]))
        op("vector", lambda v, sb=sb: v.tensor_tensor_scan(
            t_st[:, :], t_dec[sb][:, :], t_B[sb][:, :], 0.0, ALU.mult, ALU.add),
           waits=st_free, selfwait=True)
        st_free = []
        tok_scol = op("vector", lambda v: v.tensor_scalar(
            t_scol[:, :], t_st[:, T - 1:T], t_evm[:, 0:1], None, ALU.mult),
            waits=scol_free, selfwait=True)
        st_free.append(tok_scol)

        tok_ccin = dma("gpsimd", d_ccin[c], t_scol[:, :], waits=[tok_scol], slot="ccin")
        scol_free = [tok_ccin]

        emit_deferred(c - 1)

        def cc_body(g, c=c, w=mkwait([tok_ccin])):
            do_wait(g, w)
            g.collective_compute(
                "AllReduce", ALU.add,
                replica_groups=[[0, 1], [2, 3], [4, 5], [6, 7]],
                ins=[d_ccin[c]], outs=[d_ccout[c]],
            ).then_inc(ccsem[c])
        emit("gpsimd", cc_body)

        w_recv = mkwait(sini_free)
        sini_free = []
        if "recv" not in slot_sems:
            slot_sems["recv"] = es.enter_context(nc.semaphore("ds_recv"))
            slot_gen["recv"] = 0
        slot_gen["recv"] += 1
        tok_recv = ("S", "recv", slot_gen["recv"] * 16)

        def recv_body(g, c=c, w_recv=w_recv):
            g.wait_ge(ccsem[c], 1)
            do_wait(g, w_recv)
            g.dma_start(out=t_sini[:, :], in_=d_ccout[c]).then_inc(slot_sems["recv"], 16)
        emit("gpsimd", recv_body)

        saved[c] = {"recv": tok_recv, "C_evacs": C_evacs, "pf": tok_pf}

    ubf_dead = ("tensor", cnt["tensor"])
    ubf_ysl = [u_bf[:, 12 + p, :].rearrange("p (c j) -> p c j", j=128) for p in range(2)]

    def ysl_view(t_):
        return ysl01[0] if t_ == 0 else ubf_ysl[t_ % 2]
    tok_wd = dma("sync", wd_v, s_wd, waits=[ubf_dead, wd_wr], slot="wdl")

    emit_deferred(NCH - 1)

    barrierC = barrier_now()
    esC.close()

    esD = contextlib.ExitStack()
    xtm = [esD.enter_context(nc.sbuf_tensor(f"xtm{i}", [128, 1024], F32)) for i in range(2)]
    xb = [esD.enter_context(nc.sbuf_tensor(f"xb{i}", [128, 1024], F32)) for i in range(2)]
    t_hb = [esD.enter_context(nc.sbuf_tensor(f"thb{i}", [128, 1024], F32)) for i in range(2)]
    t_h = [esD.enter_context(nc.sbuf_tensor(f"th{i}", [128, 1024], F32)) for i in range(2)]
    d_q1 = esD.enter_context(nc.sbuf_tensor("dq1", [128, 1024], F32))
    d_q2 = esD.enter_context(nc.sbuf_tensor("dq2", [128, 1024], F32))
    d_q3 = esD.enter_context(nc.sbuf_tensor("dq3", [128, 1024], F32))
    d_q4 = esD.enter_context(nc.sbuf_tensor("dq4", [128, 1024], F32))
    d_qi = esD.enter_context(nc.sbuf_tensor("dqi", [128, 1024], I32))
    d_sc = esD.enter_context(nc.sbuf_tensor("dsc", [128, 32], F32))
    d_rs = esD.enter_context(nc.sbuf_tensor("drs", [128, 32], F32))
    t_mv = esD.enter_context(nc.sbuf_tensor("mv", [128, 8], F32))
    t_xdbf = esD.enter_context(nc.sbuf_tensor("xdbf", [128, 1024], BF16))
    t_xdT = esD.enter_context(nc.sbuf_tensor("xdT", [128, 1024], BF16))
    t_outb = esD.enter_context(nc.sbuf_tensor("outbt", [128, 1024], F32))
    t_ffnb = esD.enter_context(nc.sbuf_tensor("ffnbt", [128, 1024], F32))
    obuf = [esD.enter_context(nc.sbuf_tensor(f"obuf{i}", [128, 1024], F32)) for i in range(2)]

    e_q1 = esD.enter_context(nc.sbuf_tensor("eq1", [128, 1024], F32))
    e_q2 = esD.enter_context(nc.sbuf_tensor("eq2", [128, 1024], F32))
    e_q3 = esD.enter_context(nc.sbuf_tensor("eq3", [128, 1024], F32))
    e_q4 = esD.enter_context(nc.sbuf_tensor("eq4", [128, 1024], F32))
    e_qi = esD.enter_context(nc.sbuf_tensor("eqi", [128, 1024], I32))
    e_sc = esD.enter_context(nc.sbuf_tensor("esc", [128, 32], F32))
    e_rs = esD.enter_context(nc.sbuf_tensor("ers", [128, 32], F32))
    quant_dh = [make_quant(d_q1[:, h * 512:(h + 1) * 512], d_q2[:, h * 512:(h + 1) * 512],
                           d_q3[:, h * 512:(h + 1) * 512], d_q4[:, h * 512:(h + 1) * 512],
                           d_qi[:, h * 512:(h + 1) * 512], d_sc[:, h * 16:(h + 1) * 16],
                           d_rs[:, h * 16:(h + 1) * 16], nblk=16) for h in range(2)]

    dma("sync", t_outb[:, :], d_outb, waits=barrierC, slot="dbias")
    tok_bias = dma("sync", t_ffnb[:, :], d_ffnb, slot="dbias")

    ysl_free = [[], []]
    xtm_free = [[], []]
    xb_free = [[], []]
    hb_free = []
    obuf_free = [[], []]
    h_free, q_free = [], []
    xd_free, xdT_free = [], []
    out_toks = []
    ysl_tok = [None] * 16
    xtm_tok = [None] * 16
    ysl_tok[0] = ("S", "y0f", 16 * NCH)
    ysl_tok[1] = dma("scalar", ysl_view(1),
                     s_ybf[:, :, 128:256].rearrange("c p j -> p c j"),
                     waits=list(ybf_wr) + [ubf_dead], slot="ysl1")
    for tt in range(2):
        xtm_tok[tt] = dma("sync", xtm[tt % 2][:, :],
                          d_x[HALO + tt * 128: HALO + (tt + 1) * 128, :], waits=barrierC,
                          slot=f"xtm{tt % 2}")

    op_banks = [None] * 16
    op_w = [None] * 16

    def emit_op_mm(tau):
        sb = tau % 2
        bks = []
        for hh in range(2):
            bk = 2 * (tau % 2) + hh
            bks.append(bk)
            mm_group([ysl_view(tau)[:, cc, :] for cc in range(NCH)],
                     [u_bf[:, cc // 2, (cc % 2) * 1024 + hh * 512:(cc % 2) * 1024 + (hh + 1) * 512]
                      for cc in range(NCH)],
                     bk, waits=[ysl_tok[tau], tok_ow])
        op_banks[tau] = tuple(bks)
        op_w[tau] = ("tensor", cnt["tensor"])
        ysl_free[sb].append(op_w[tau])

    emit_op_mm(0)
    emit_op_mm(1)

    h_free = [[], []]
    hb_free = [[], []]
    eq_free = []
    mv1_free = []
    mv2_free = []
    back_state = {}

    def emit_back(tau, qsink=None):
        nonlocal eq_free, mv2_free

        def qpop(n=2):
            for _ in range(n):
                if qsink:
                    qsink.pop(0)()
        if tau < 0:
            return
        st = back_state.pop(tau)
        sbb = st["sb"]
        fv = []
        for hh in range(2):
            bk = st["f_banks"][hh]
            etok = op("vector", lambda v, bk=bk, hh=hh, sbb=sbb: v.tensor_add(
                obuf[sbb][:, hh * 512:(hh + 1) * 512], pbank[bk][:, :],
                t_hb[sbb][:, hh * 512:(hh + 1) * 512]),
               waits=[st["f_w"], st["tok_hb"]] + (obuf_free[sbb] if hh == 0 else []))
            if hh == 0:
                obuf_free[sbb] = []
            bank_r[bk].append(etok)
            fv.append(etok)
            qpop(1 if hh == 0 else 2)
        hb_free[sbb] = [fv[-1]]
        tok_sq2 = op("scalar", lambda s, sbb=sbb: s.activation(e_q1[:, :], obuf[sbb][:, :],
                                                               AF.Square),
                     waits=fv + eq_free, selfwait=True)
        eq_free = []
        t1 = op("vector", lambda v, sbb=sbb: v.reduce_sum(t_mv[:, 4:5], obuf[sbb][:, :],
                                                          axis=AX.X),
                waits=fv + mv2_free)
        mv2_free = []
        qpop(2)
        t2 = op("vector", lambda v: v.tensor_scalar(t_mv[:, 4:5], t_mv[:, 4:5],
                                                    -1.0 / 1024.0, None, ALU.mult),
                waits=[t1])
        qpop(2)
        t3 = op("vector", lambda v: v.reduce_sum(t_mv[:, 5:6], e_q1[:, :], axis=AX.X),
                waits=[tok_sq2])
        qpop(1)
        t4 = op("vector", lambda v: v.tensor_mul(t_mv[:, 7:8], t_mv[:, 4:5], t_mv[:, 4:5]),
                waits=[t2])
        tok_var2 = op("vector", lambda v: v.scalar_tensor_tensor(
            t_mv[:, 5:6], t_mv[:, 5:6], 1.0 / 1024.0, t_mv[:, 7:8],
            ALU.mult, ALU.subtract), waits=[t3, t4])
        eq_free.append(tok_var2)
        qpop(2)
        op("scalar", lambda s: s.activation(t_mv[:, 5:6], t_mv[:, 5:6], AF.Ln, bias=1e-5),
           waits=[tok_var2], selfwait=True)
        op("scalar", lambda s: s.activation(t_mv[:, 5:6], t_mv[:, 5:6], AF.Exp,
                                            scale=-0.5), selfwait=True)
        op("scalar", lambda s: s.activation(t_mv[:, 6:7], t_mv[:, 4:5], AF.Identity,
                                            scale=t_mv[:, 5:6]), selfwait=True)
        tok_o = op("scalar", lambda s, sbb=sbb: s.activation(
            obuf[sbb][:, :], obuf[sbb][:, :], AF.Identity,
            scale=t_mv[:, 5:6], bias=t_mv[:, 6:7]), selfwait=True)
        mv2_free = [tok_o]
        otok = dma("sync", d_out[tau * 128:(tau + 1) * 128, :], obuf[sbb][:, :],
                   waits=[tok_o], slot=f"out{sbb}")
        out_toks.append(otok)
        obuf_free[sbb].append(otok)

    for tau in range(16):
        sb = tau % 2
        if tau + 2 < 16:
            tt = tau + 2
            w_extra = [ubf_dead] if tt < 4 else []
            ysl_tok[tt] = dma("scalar", ysl_view(tt),
                              s_ybf[:, :, tt * 128:(tt + 1) * 128].rearrange("c p j -> p c j"),
                              waits=ysl_free[tt % 2] + w_extra, slot=f"ysl{tt % 2}")
            ysl_free[tt % 2] = []

        tok_xb = op("gpsimd", lambda g, sb=sb: g.tensor_add(xb[sb][:, :], xtm[sb][:, :],
                                                            t_outb[:, :]),
                    waits=[xtm_tok[tau], tok_bias] + xb_free[sb], selfwait=True)
        xtm_free[sb].append(tok_xb)
        ev = []
        for hh in range(2):
            bk = op_banks[tau][hh]
            etok = op("vector", lambda v, bk=bk, hh=hh, sb=sb: v.tensor_add(
                t_h[sb][:, hh * 512:(hh + 1) * 512], pbank[bk][:, :],
                xb[sb][:, hh * 512:(hh + 1) * 512]),
               waits=[op_w[tau], tok_xb] + (h_free[sb] if hh == 0 else []))
            if hh == 0:
                h_free[sb] = []
            bank_r[bk].append(etok)
            ev.append(etok)
        xb_free[sb] = [ev[-1]]
        if tau + 2 < 16:
            tt = tau + 2
            xtm_tok[tt] = dma("sync", xtm[tt % 2][:, :],
                              d_x[HALO + tt * 128: HALO + (tt + 1) * 128, :],
                              waits=xtm_free[tt % 2], slot=f"xtm{tt % 2}")
            xtm_free[tt % 2] = []

        tok_sq = op("scalar", lambda s, sb=sb: s.activation(d_q1[:, :], t_h[sb][:, :], AF.Square),
                    waits=ev + q_free, selfwait=True)
        q_free = []
        t1 = op("vector", lambda v, sb=sb: v.reduce_sum(t_mv[:, 0:1], t_h[sb][:, :], axis=AX.X),
                waits=ev + mv1_free)
        mv1_free = []
        t2 = op("vector", lambda v: v.tensor_scalar(t_mv[:, 0:1], t_mv[:, 0:1],
                                                    -1.0 / 1024.0, None, ALU.mult),
                waits=[t1])
        t3 = op("vector", lambda v: v.reduce_sum(t_mv[:, 1:2], d_q1[:, :], axis=AX.X),
                waits=[tok_sq])
        t4 = op("vector", lambda v: v.tensor_mul(t_mv[:, 3:4], t_mv[:, 0:1], t_mv[:, 0:1]),
                waits=[t2])
        tok_var = op("vector", lambda v: v.scalar_tensor_tensor(
            t_mv[:, 1:2], t_mv[:, 1:2], 1.0 / 1024.0, t_mv[:, 3:4],
            ALU.mult, ALU.subtract), waits=[t3, t4])
        op("scalar", lambda s: s.activation(t_mv[:, 1:2], t_mv[:, 1:2], AF.Ln, bias=1e-5),
           waits=[tok_var, tok_consts], selfwait=True)
        op("scalar", lambda s: s.activation(t_mv[:, 1:2], t_mv[:, 1:2], AF.Exp,
                                            scale=-0.5), selfwait=True)
        op("scalar", lambda s: s.activation(t_mv[:, 2:3], t_mv[:, 0:1], AF.Identity,
                                            scale=t_mv[:, 1:2]), selfwait=True)
        tok_ln1 = op("scalar", lambda s, sb=sb: s.activation(
            t_h[sb][:, :], t_h[sb][:, :], AF.Identity,
            scale=t_mv[:, 1:2], bias=t_mv[:, 2:3]), selfwait=True)
        mv1_free = [tok_ln1]

        tok_hb = op("gpsimd", lambda g, sb=sb: g.tensor_add(t_hb[sb][:, :], t_h[sb][:, :],
                                                            t_ffnb[:, :]),
                    waits=[tok_ln1, tok_bias] + hb_free[sb], selfwait=True)

        if debug and tau == 0:
            tok_ln1 = dma("sync", d_dbg[1], t_h[sb][:, :], waits=[tok_ln1], slot="dbg")
        s_h = [[], []]
        q_holders = [quant_dh[h](t_h[sb][:, h * 512:(h + 1) * 512],
                                 t_xdbf[:, h * 512:(h + 1) * 512],
                                 [tok_ln1] + xd_free, sink=s_h[h]) for h in range(2)]
        qsink = []
        while s_h[0] or s_h[1]:
            for h in range(2):
                if s_h[h]:
                    qsink.append(s_h[h].pop(0))
        emit_back(tau - 1, qsink)
        while qsink:
            qsink.pop(0)()
        tok_qd = [q_holders[0]["tok"], q_holders[1]["tok"]]
        xd_free = []
        q_free += tok_qd
        h_free[sb] = [tok_hb] + tok_qd

        w = mkwait(tok_qd + [tok_smalls] + bf_rd)
        bf_rd = []
        tok_tp = alloc("tensor")

        def tp_body(t, w=w):
            do_wait(t, w)
            for k in range(NDC):
                ins = t.transpose(p_bf[:, k * 128:(k + 1) * 128],
                                  t_xdbf[:, k * 128:(k + 1) * 128], t_eye[:, :])
            ins.then_inc(sem["tensor"], 1)
        emit("tensor", tp_body)
        xd_free.append(tok_tp)
        tok_xdT = op("scalar", lambda s: s.copy(t_xdT[:, :], p_bf[:, :]),
                     waits=[tok_tp] + xdT_free)
        xdT_free = []
        bf_rd.append(tok_xdT)

        f_banks = []
        for hh in range(2):
            bk = 4 + hh
            f_banks.append(bk)
            mm_group([t_xdT[:, k * 128:(k + 1) * 128] for k in range(NDC)],
                     [u_bf[:, 8 + k // 2, (k % 2) * 1024 + hh * 512:(k % 2) * 1024 + (hh + 1) * 512]
                      for k in range(NDC)],
                     bk, waits=[tok_xdT, tok_wd])
        f_w = ("tensor", cnt["tensor"])
        xdT_free.append(f_w)
        back_state[tau] = {"sb": sb, "f_banks": tuple(f_banks), "f_w": f_w, "tok_hb": tok_hb}

        if tau + 2 < 16:
            emit_op_mm(tau + 2)

    emit_back(15)

    wfin = mkwait(out_toks)

    def fin(q, wfin=wfin):
        do_wait(q, wfin)
    emit("sync", fin)

    esD.close()

    with es:
        with nc.Block() as block:
            block.sync(lambda q: [f(q) for f in streams["sync"]])
            block.gpsimd(lambda q: [f(q) for f in streams["gpsimd"]])
            block.scalar(lambda q: [f(q) for f in streams["scalar"]])
            block.vector(lambda q: [f(q) for f in streams["vector"]])
            block.tensor(lambda q: [f(q) for f in streams["tensor"]])
    return nc


_NC_CACHE = {}


def _get_nc():
    if "nc" not in _NC_CACHE:
        _NC_CACHE["nc"] = build_nc()
    return _NC_CACHE["nc"]


def kernel(L0_attn_x, L0_attn_in_proj_w, L0_attn_in_proj_b, L0_attn_conv_w, L0_attn_conv_b,
           L0_attn_dt_w, L0_attn_dt_b, L0_attn_b_w, L0_attn_c_w, L0_attn_A_log,
           L0_attn_D_param, L0_attn_out_w, L0_attn_out_b, L0_ffn_weight, L0_ffn_bias):
    x = np.ascontiguousarray(np.asarray(L0_attn_x, dtype=np.float32))
    in_w = np.asarray(L0_attn_in_proj_w, dtype=np.float32)
    in_b = np.asarray(L0_attn_in_proj_b, dtype=np.float32)
    conv_w = np.asarray(L0_attn_conv_w, dtype=np.float32)
    conv_b = np.asarray(L0_attn_conv_b, dtype=np.float32)
    dt_w = np.asarray(L0_attn_dt_w, dtype=np.float32)
    dt_b = np.asarray(L0_attn_dt_b, dtype=np.float32)
    b_w = np.asarray(L0_attn_b_w, dtype=np.float32)
    c_w = np.asarray(L0_attn_c_w, dtype=np.float32)
    a_log = np.asarray(L0_attn_A_log, dtype=np.float32)
    d_par = np.asarray(L0_attn_D_param, dtype=np.float32)
    out_w = np.asarray(L0_attn_out_w, dtype=np.float32)
    out_b = np.asarray(L0_attn_out_b, dtype=np.float32)
    ffn_w = np.asarray(L0_ffn_weight, dtype=np.float32)
    ffn_b = np.asarray(L0_ffn_bias, dtype=np.float32)

    win = np.ascontiguousarray(in_w.reshape(32, 128, 8, 128).transpose(0, 3, 2, 1)
                               ).reshape(32, 128, 1024)

    def big_t(w):
        return np.ascontiguousarray(w.reshape(16, 128, 16, 128).transpose(0, 3, 2, 1)).reshape(16, 128, 2048)
    dtw = big_t(dt_w)
    bw = big_t(b_w)
    cw = big_t(c_w)
    ow = np.ascontiguousarray(out_w.T.reshape(16, 128, 1024).transpose(1, 0, 2)).reshape(128, 16384)
    inb = np.ascontiguousarray(in_b.reshape(32, 128).T)
    cwv = np.ascontiguousarray(conv_w[:, 0, :].reshape(16, 128, 4).transpose(1, 0, 2)).reshape(128, 64)
    dtb = np.ascontiguousarray(dt_b.reshape(16, 128).T)
    cvb = np.ascontiguousarray(conv_b.reshape(16, 128).T)
    alog = np.ascontiguousarray(a_log.reshape(16, 128).T)
    dpar = np.ascontiguousarray(d_par.reshape(16, 128).T)
    outbt = np.ascontiguousarray(np.broadcast_to(out_b[None, :], (128, 1024)))
    ffnbt = np.ascontiguousarray(np.broadcast_to(ffn_b[None, :], (128, 1024)))
    eye = np.eye(128).astype(ml_dtypes.bfloat16)

    in_maps = make_in_maps(x, win, dtw, bw, cw, ow, ffn_w, inb, cwv, cvb, dtb, alog,
                           dpar, outbt, ffnbt, eye)
    return run_all(in_maps)


def make_in_maps(x, win, dtw, bw, cw, ow, ffn_w, inb, cwv, cvb, dtb, alog, dpar,
                 outbt, ffnbt, eye):
    in_maps = []
    for c in range(8):
        b, half = c // 2, c % 2
        xs = np.zeros((TPAD, D), np.float32)
        if half == 1:
            xs[0:HALO] = x[b, T - HALO:T]
        xs[HALO:HALO + T] = x[b, half * T:(half + 1) * T]
        halo = np.zeros((128, 1), np.float32)
        if half == 1:
            halo[:, 0] = 1.0
        in_maps.append({
            "x": xs, "win": win, "dtw": dtw, "bw": bw, "cw": cw, "ow": ow, "fw": ffn_w,
            "inb": inb, "cwv": cwv, "cvb": cvb, "dtb": dtb, "alog": alog, "dpar": dpar,
            "outb": outbt, "ffnb": ffnbt, "eye": eye, "halo": halo,
        })
    return in_maps


_LAST_RES = None


def run_all(in_maps):
    global _LAST_RES
    nc = _get_nc()
    res = run_bass_kernel_spmd(nc, in_maps, core_ids=list(range(8)))
    _LAST_RES = res
    out = np.empty((4, 4096, 1024), np.float32)
    for c in range(8):
        b, half = c // 2, c % 2
        out[b, half * T:(half + 1) * T] = res.results[c]["y_out"]
    return out
